# version 13 (final)
# baseline (speedup 1.0000x reference)
import numpy as np
import ml_dtypes
from contextlib import ExitStack

import concourse.bass as bass
import concourse.mybir as mybir
from concourse.bass_utils import run_bass_kernel_spmd

F32 = mybir.dt.float32
BF16 = mybir.dt.bfloat16
F16 = mybir.dt.float16
I32 = mybir.dt.int32
U16 = mybir.dt.uint16
U8 = mybir.dt.uint8
AF = mybir.ActivationFunctionType
ALU = mybir.AluOpType
BF = ml_dtypes.bfloat16

NC = 8
D = 2048
NI = 5632
NH = 16
HD = 128
SEQ = 4096
NB = 2
LN_EPS = 1e-5
RMS_EPS = 1e-6
SM_SCALE = float(1.0 / np.sqrt(128.0))

S_SPLIT = 2816
T_LOW = S_SPLIT // 2
T_HIGH = (SEQ - S_SPLIT) // 2
T_CONE = S_SPLIT // 2
KT_LOW = S_SPLIT // 128
KT_HIGH = SEQ // 128


def ttiles(T):
    out, t0 = [], 0
    while t0 < T:
        out.append((t0, min(512, T - t0)))
        t0 += 512
    return out


class BlkMgr:
    def __init__(self, nc):
        self.nc = nc
        self.cm = None

    def new(self, name):
        if self.cm is not None:
            self.cm.__exit__(None, None, None)
        self.cm = self.nc.Block(name)
        return self.cm.__enter__()

    def close(self):
        if self.cm is not None:
            self.cm.__exit__(None, None, None)
            self.cm = None


class Sem:
    def __init__(self, handle):
        self.h = handle
        self.v = 0

    def inc(self, amt=1):
        self.v += amt
        return self.v

    def clone(self):
        s = Sem(self.h)
        s.v = self.v
        return s


def chain(eng, sem, fns):
    for k, f in enumerate(fns):
        if k > 0:
            eng.wait_ge(sem.h, sem.v)
        f().then_inc(sem.h, 1)
        sem.inc(1)



def emit_ln(nc, BLK, sec, bufs, S, x_views, t0, tw, tag, xn=None, xoff=0, xbf=None):
    s_pe, s_dv, s_sc = S["pe"], S["dv"], S["sc"]
    xn = xn if xn is not None else bufs["xn"]
    xbf = xbf if xbf is not None else bufs["xbf"]
    rows = bufs["rows"]
    a_b, b_b, tmp = bufs["a_b"], bufs["b_b"], bufs["tmp"]
    psS1, psS2, psC, psD = bufs["psS1"], bufs["psS2"], bufs["psC"], bufs["psD"]
    ones_bf, ones_f32, cst = bufs["ones_bf"], bufs["ones_f32"], bufs["cst"]

    blk = BLK.new(f"ln_{tag}")

    @sec(blk, "vector")
    def _(vector, dv_now=s_dv.v):
        vector.wait_ge(s_dv.h, dv_now)
        for c in range(16):
            ins = vector.tensor_copy(out=xbf[c][:, 0:tw], in_=x_views[c][:, t0:t0 + tw])
            if c == 15:
                ins.then_inc(s_dv.h, 1)
    s_dv.inc(1)
    cast_done = s_dv.v

    @sec(blk, "tensor")
    def _(tensor, cast_done=cast_done):
        tensor.wait_ge(s_dv.h, cast_done)
        for c in range(16):
            ins = tensor.matmul(out=psS1[0:1, 0:tw], lhsT=ones_bf[:, 0:1], rhs=xbf[c][:, 0:tw],
                                start=(c == 0), stop=(c == 15))
            if c == 15:
                ins.then_inc(s_pe.h, 1)
    s_pe.inc(1)
    s1_done = s_pe.v

    @sec(blk, "scalar")
    def _(scalar, s1_done=s1_done, sc_now=s_sc.v):
        scalar.wait_ge(s_pe.h, s1_done)
        scalar.wait_ge(s_sc.h, sc_now)
        for c in range(16):
            ins = scalar.activation(out=xn[c][:, xoff:xoff + tw], in_=xbf[c][:, 0:tw],
                                    func=AF.Square)
            if c == 15:
                ins.then_inc(s_sc.h, 1)
    s_sc.inc(1)
    sq_done = s_sc.v

    @sec(blk, "tensor")
    def _(tensor, sq_done=sq_done):
        tensor.wait_ge(s_sc.h, sq_done)
        for c in range(16):
            ins = tensor.matmul(out=psS2[0:1, 0:tw], lhsT=ones_bf[:, 0:1],
                                rhs=xn[c][:, xoff:xoff + tw],
                                start=(c == 0), stop=(c == 15))
            if c == 15:
                ins.then_inc(s_pe.h, 1)
    s_pe.inc(1)
    stats_done = s_pe.v

    blk = BLK.new(f"ln_{tag}_r")

    @sec(blk, "vector")
    def _(vector, stats_done=stats_done):
        vector.wait_ge(s_pe.h, stats_done)
        chain(vector, s_dv, [
            lambda: vector.tensor_scalar(out=rows[0][:, 0:tw], in0=psS1[0:1, 0:tw],
                                         scalar1=1.0 / D, scalar2=None, op0=ALU.mult),
            lambda: vector.tensor_scalar(out=rows[1][:, 0:tw], in0=psS2[0:1, 0:tw],
                                         scalar1=1.0 / D, scalar2=None, op0=ALU.mult),
            lambda: vector.tensor_tensor(out=rows[2][:, 0:tw], in0=rows[0][:, 0:tw],
                                         in1=rows[0][:, 0:tw], op=ALU.mult),
            lambda: vector.tensor_tensor(out=rows[2][:, 0:tw], in0=rows[1][:, 0:tw],
                                         in1=rows[2][:, 0:tw], op=ALU.subtract),
        ])
    var_done = s_dv.v

    @sec(blk, "scalar")
    def _(scalar, var_done=var_done):
        scalar.wait_ge(s_dv.h, var_done)
        chain(scalar, s_sc, [
            lambda: scalar.activation(out=rows[1][:, 0:tw], in_=rows[2][:, 0:tw], func=AF.Ln,
                                      bias=cst[0:1, 0:1]),
            lambda: scalar.activation(out=rows[2][:, 0:tw], in_=rows[1][:, 0:tw], func=AF.Exp,
                                      scale=-0.5),
        ])
    rstd_done = s_sc.v

    @sec(blk, "vector")
    def _(vector, rstd_done=rstd_done, dv_now=s_dv.v):
        vector.wait_ge(s_sc.h, rstd_done)
        vector.wait_ge(s_dv.h, dv_now)
        vector.tensor_tensor(out=rows[1][:, 0:tw], in0=rows[0][:, 0:tw], in1=rows[2][:, 0:tw],
                             op=ALU.mult).then_inc(s_dv.h, 1)
    s_dv.inc(1)
    mr_done = s_dv.v

    @sec(blk, "tensor")
    def _(tensor, mr_done=mr_done):
        tensor.wait_ge(s_dv.h, mr_done)
        tensor.matmul(out=psC[:, 0:tw], lhsT=ones_f32[0:1, :], rhs=rows[2][:, 0:tw],
                      start=True, stop=True)
        tensor.matmul(out=psD[:, 0:tw], lhsT=ones_f32[0:1, :], rhs=rows[1][:, 0:tw],
                      start=True, stop=True).then_inc(s_pe.h, 1)
    s_pe.inc(1)
    bc_done = s_pe.v

    blk = BLK.new(f"ln_{tag}_a")

    @sec(blk, "vector")
    def _(vector, bc_done=bc_done):
        vector.wait_ge(s_pe.h, bc_done)
        fns = [lambda: vector.tensor_copy(out=a_b[:, 0:tw], in_=psC[:, 0:tw]),
               lambda: vector.tensor_copy(out=b_b[:, 0:tw], in_=psD[:, 0:tw])]
        for c in range(16):
            fns.append(lambda c=c: vector.tensor_tensor(
                out=tmp[:, 0:tw], in0=x_views[c][:, t0:t0 + tw], in1=a_b[:, 0:tw], op=ALU.mult))
            fns.append(lambda c=c: vector.tensor_tensor(
                out=xn[c][:, xoff:xoff + tw], in0=tmp[:, 0:tw], in1=b_b[:, 0:tw], op=ALU.subtract))
        chain(vector, s_dv, fns)
    return s_dv.v



def emit_ffn(nc, BLK, sec, bufs, S, T, wshg4, wshd4, tag,
             x_views=None, x_src=None, x_dst=None, attn_add=None):
    s_pe, s_dv, s_sc = S["pe"], S["dv"], S["sc"]
    s_st = S["st"]
    s_lx = S["lx"]
    s_wg = [S["wg0"], S["wg1"]]
    s_wd = [S["wd0"], S["wd1"]]
    s_as = [S["as0"], S["as1"]]
    s_sp = [S["sp0"], S["sp1"]]
    xn, rows = bufs["xn"], bufs["rows"]
    pbk = bufs["pbk"]
    ones_bf, ones_f32, cst = bufs["ones_bf"], bufs["ones_f32"], bufs["cst"]
    wguA, wdcA = bufs["wguA"], bufs["wdc"]
    h, y_bf, sgb = bufs["h"], bufs["y_bf"], bufs["sgb"]
    rb, tmp = bufs["rb"], bufs["tmp"]
    xt, atp, ash = bufs["xt"], bufs["atp"], bufs["ash"]

    last_gu = {0: 0, 1: 0}
    last_dn = {0: 0, 1: 0}
    gbank_free = {0: 0, 1: 0}
    ubank_free = {0: 0, 1: 0}
    dbank_free = {k: (0, 0) for k in (2, 3)}
    sgb_free = {0: 0, 1: 0}
    sqb_free = {0: 0, 1: 0}
    DB = [2, 3]
    CH = 768
    FST = 4
    DN = 7
    prev_st = s_st.v

    for ck, c0 in enumerate(range(0, T, CH)):
        cw = min(CH, T - c0)
        subs = ttiles(cw)
        nsub = len(subs)
        if x_src is not None:
            xv = [xt[c][:, 0:cw] for c in range(16)]
            blk = BLK.new(f"xl_{tag}{ck}")

            @sec(blk, "scalar")
            def _(scalar, ck=ck, c0=c0, cw=cw, prev_st=prev_st, dv_now=s_dv.v,
                  aa=attn_add):
                if ck == 0:
                    scalar.wait_ge(s_dv.h, dv_now)
                    if aa is not None:
                        scalar.wait_ge(S["bar"].h, aa["b3"])
                scalar.wait_ge(s_st.h, prev_st)
                for c in range(16):
                    scalar.dma_start(out=xt[c][:, 0:cw],
                                     in_=x_src[c * 128:(c + 1) * 128, c0:c0 + cw]
                                     ).then_inc(s_lx.h, 16)
            s_lx.inc(16 * 16)
            x_loaded = s_lx.v

            if attn_add is not None:
                pair = attn_add["pair"]
                atp_free = {0: 0, 1: 0}
                add_vals = []
                ash_vals = []
                blk = BLK.new(f"aa_{tag}{ck}")

                @sec(blk, "scalar")
                def _(scalar, c0=c0, cw=cw, pair=pair, base_dv=s_dv.v):
                    for c in range(16):
                        if c >= 2:
                            scalar.wait_ge(s_dv.h, base_dv + c - 1)

                        def load(base, c=c):
                            scalar.dma_start(out=atp[c & 1][:, 0:cw],
                                             in_=ash[c, :, base + c0:base + c0 + cw]
                                             ).then_inc(s_as[c & 1].h, 16)
                        bufs["bsel"](scalar, "scalar",
                                     lambda c=c: load(pair[0], c), lambda c=c: load(pair[1], c))
                for c in range(16):
                    s_as[c & 1].inc(16)
                    ash_vals.append(s_as[c & 1].v)

                @sec(blk, "vector")
                def _(vector, cw=cw, x_loaded=x_loaded, ash_vals=tuple(ash_vals),
                      dv_now=s_dv.v):
                    vector.wait_ge(s_dv.h, dv_now)
                    vector.wait_ge(s_lx.h, x_loaded)
                    for c in range(16):
                        vector.wait_ge(s_as[c & 1].h, ash_vals[c])
                        vector.tensor_tensor(out=xt[c][:, 0:cw], in0=xt[c][:, 0:cw],
                                             in1=atp[c & 1][:, 0:cw],
                                             op=ALU.add).then_inc(s_dv.h, 1)
                for c in range(16):
                    s_dv.inc(1)
            else:
                blk = BLK.new(f"xw_{tag}{ck}")

                @sec(blk, "vector")
                def _(vector, x_loaded=x_loaded, dv_now=s_dv.v):
                    vector.wait_ge(s_dv.h, dv_now)
                    vector.wait_ge(s_lx.h, x_loaded)
                    vector.engine_nop().then_inc(s_dv.h, 1)
                s_dv.inc(1)
        else:
            xv = [x_views[c][:, c0:c0 + cw] for c in range(16)]

        xn_ready = 0
        for si, (s0, sw) in enumerate(subs):
            xn_ready = emit_ln(nc, BLK, sec, bufs, S, xv, s0, sw,
                               f"{tag}c{ck}s{si}", xoff=s0)

        for i in range(44 if FST >= 2 else 0):
            par = i & 1
            blk = BLK.new(f"gu_{tag}{ck}_{i}")

            @sec(blk, "sync")
            def _(sync, i=i, par=par, lw=last_gu[par],
                  sp_now=(s_sp[0].v, s_sp[1].v)):
                if lw > 0:
                    sync.wait_ge(s_pe.h, lw)
                else:
                    if sp_now[0] > 0:
                        sync.wait_ge(s_sp[0].h, sp_now[0])
                    if sp_now[1] > 0:
                        sync.wait_ge(s_sp[1].h, sp_now[1])
                sync.dma_start(out=wguA[par][:],
                               in_=wshg4[:, i, :, :].rearrange("p a b -> p (a b)")
                               ).then_inc(s_wg[par].h, 16)
            s_wg[par].inc(16)
            gu_loaded = s_wg[par].v

            plan = []
            for si, (s0, sw) in enumerate(subs):
                bi = (i * nsub + si) & 1
                plan.append((si, s0, sw, bi, gbank_free[bi], ubank_free[bi]))
            gate_done, up_done = {}, {}
            for si in range(nsub):
                s_pe.inc(1)
                gate_done[si] = s_pe.v
                s_pe.inc(1)
                up_done[si] = s_pe.v

            @sec(blk, "tensor")
            def _(tensor, i=i, par=par, gu_loaded=gu_loaded, xn_ready=xn_ready,
                  plan=tuple(plan)):
                tensor.wait_ge(s_wg[par].h, gu_loaded)
                if i == 0:
                    tensor.wait_ge(s_dv.h, xn_ready)
                for (si, s0, sw, bi, gfree, ufree) in plan:
                    if gfree > 0:
                        tensor.wait_ge(s_sc.h, gfree)
                    for g in range(16):
                        ins = tensor.matmul(out=pbk[bi][:, 0:sw],
                                            lhsT=wguA[par][:, g * 128:(g + 1) * 128],
                                            rhs=xn[g][:, s0:s0 + sw],
                                            start=(g == 0), stop=(g == 15))
                        if g == 15:
                            ins.then_inc(s_pe.h, 1)
                    if ufree > 0:
                        tensor.wait_ge(s_dv.h, ufree)
                    for g in range(16):
                        ins = tensor.matmul(out=pbk[2 + bi][:, 0:sw],
                                            lhsT=wguA[par][:, 2048 + g * 128:2048 + (g + 1) * 128],
                                            rhs=xn[g][:, s0:s0 + sw],
                                            start=(g == 0), stop=(g == 15))
                        if g == 15:
                            ins.then_inc(s_pe.h, 1)
            last_gu[par] = up_done[nsub - 1]

            splan = [(si, s0, sw, (i * nsub + si) & 1, gate_done[si], sgb_free[si])
                     for si, (s0, sw) in enumerate(subs)]

            @sec(blk, "scalar")
            def _(scalar, splan=tuple(splan)):
                for (si, s0, sw, bi, gd, sfree) in splan:
                    scalar.wait_ge(s_pe.h, gd)
                    if sfree > 0:
                        scalar.wait_ge(s_dv.h, sfree)
                    scalar.activation(out=sgb[si][:, 0:sw], in_=pbk[bi][:, 0:sw],
                                      func=AF.Silu).then_inc(s_sc.h, 1)
            silu_done = {}
            for si in range(nsub):
                s_sc.inc(1)
                silu_done[si] = s_sc.v

            vplan = [(si, s0, sw, (i * nsub + si) & 1, silu_done[si], up_done[si])
                     for si, (s0, sw) in enumerate(subs)]

            @sec(blk, "vector")
            def _(vector, i=i, vplan=tuple(vplan), dv_now=s_dv.v):
                vector.wait_ge(s_dv.h, dv_now)
                for (si, s0, sw, bi, sd, ud) in vplan:
                    vector.wait_ge(s_sc.h, sd)
                    vector.wait_ge(s_pe.h, ud)
                    vector.tensor_tensor(out=h[i][:, s0:s0 + sw], in0=sgb[si][:, 0:sw],
                                         in1=pbk[2 + bi][:, 0:sw],
                                         op=ALU.mult).then_inc(s_dv.h, 1)
            base_dv = s_dv.v
            for si in range(nsub):
                s_dv.inc(1)
            for si in range(nsub):
                bi = (i * nsub + si) & 1
                gbank_free[bi] = silu_done[si]
                ubank_free[bi] = base_dv + si + 1
                sgb_free[si] = base_dv + si + 1

        h_all = s_dv.v

        stats_done = {}
        sq_vals = {}
        dmm_vals = {}
        for m in range(17 if FST >= 3 else 0):
            par = m & 1
            blk = BLK.new(f"dn_{tag}{ck}_{m}")
            if m < 16:
                @sec(blk, "gpsimd")
                def _(gpsimd, m=m, par=par, lw=last_dn[par]):
                    if lw > 0:
                        gpsimd.wait_ge(s_pe.h, lw)
                    gpsimd.dma_start(out=wdcA[par][:],
                                     in_=wshd4[:, m, :, :].rearrange("p a b -> p (a b)")
                                     ).then_inc(s_wd[par].h, 16)
                s_wd[par].inc(16)
                d_loaded = s_wd[par].v

                dplan = []
                for si, (s0, sw) in enumerate(subs):
                    bi = DB[(m * nsub + si) % 2]
                    dplan.append((si, s0, sw, bi) + dbank_free[bi])
                tplan = []
                if m > 0 and (DN & 2):
                    for si, (s0, sw) in enumerate(subs):
                        tplan.append((si, s0, sw, m - 1, sq_vals[(m - 1, si)]))

                @sec(blk, "tensor")
                def _(tensor, m=m, par=par, d_loaded=d_loaded, h_all=h_all,
                      dplan=tuple(dplan), tplan=tuple(tplan)):
                    for (si, s0, sw, mm, sqv) in tplan:
                        tensor.wait_ge(s_sc.h, sqv)
                        tensor.matmul(out=pbk[6 + si][0:1, 0:sw], lhsT=ones_bf[:, 0:1],
                                      rhs=sgb[si][:, 0:sw], start=(mm == 0),
                                      stop=(mm == 15))
                    tensor.wait_ge(s_wd[par].h, d_loaded)
                    if m == 0:
                        tensor.wait_ge(s_dv.h, h_all)
                    for (si, s0, sw, bi, fsc, fdv) in dplan:
                        if fsc > 0:
                            tensor.wait_ge(s_sc.h, fsc)
                        if fdv > 0:
                            tensor.wait_ge(s_dv.h, fdv)
                        for g in range(44):
                            ins = tensor.matmul(out=pbk[bi][:, 0:sw],
                                                lhsT=wdcA[par][:, g * 128:(g + 1) * 128],
                                                rhs=h[g][:, s0:s0 + sw],
                                                start=(g == 0), stop=(g == 43))
                            if g == 43:
                                ins.then_inc(s_pe.h, 1)
                for si in range(nsub):
                    s_pe.inc(1)
                    dmm_vals[(m, si)] = s_pe.v
                last_dn[par] = dmm_vals[(m, nsub - 1)]

                splan = [(si, s0, sw, DB[(m * nsub + si) % 2], dmm_vals[(m, si)])
                         for si, (s0, sw) in enumerate(subs)]

                if DN & 1:
                    @sec(blk, "scalar")
                    def _(scalar, splan=tuple(splan)):
                        for (si, s0, sw, bi, dv_) in splan:
                            scalar.wait_ge(s_pe.h, dv_)
                            scalar.activation(out=sgb[si][:, 0:sw], in_=pbk[bi][:, 0:sw],
                                              func=AF.Square).then_inc(s_sc.h, 1)
                    for si in range(nsub):
                        s_sc.inc(1)
                        sq_vals[(m, si)] = s_sc.v

                yplan = [(si, s0, sw, DB[(m * nsub + si) % 2], dmm_vals[(m, si)])
                         for si, (s0, sw) in enumerate(subs)]

                if DN & 4:
                    @sec(blk, "vector")
                    def _(vector, m=m, yplan=tuple(yplan), dv_now=s_dv.v,
                          sqv=tuple(sq_vals.get((m, si), 0) for si in range(nsub))):
                        vector.wait_ge(s_dv.h, dv_now)
                        for (si, s0, sw, bi, dv_) in yplan:
                            vector.wait_ge(s_pe.h, dv_)
                            if sqv[si] > 0:
                                vector.wait_ge(s_sc.h, sqv[si])
                            vector.tensor_copy(out=y_bf[m][:, s0:s0 + sw],
                                               in_=pbk[bi][:, 0:sw]).then_inc(s_dv.h, 1)
                    base_dv = s_dv.v
                    for si in range(nsub):
                        s_dv.inc(1)
                    for si in range(nsub):
                        bi = DB[(m * nsub + si) % 2]
                        dbank_free[bi] = (0, base_dv + si + 1)
            elif DN & 2:
                tplan = [(si, s0, sw, sq_vals[(15, si)]) for si, (s0, sw) in enumerate(subs)]

                @sec(blk, "tensor")
                def _(tensor, tplan=tuple(tplan)):
                    for (si, s0, sw, sqv) in tplan:
                        tensor.wait_ge(s_sc.h, sqv)
                        ins = tensor.matmul(out=pbk[6 + si][0:1, 0:sw], lhsT=ones_bf[:, 0:1],
                                            rhs=sgb[si][:, 0:sw], start=False, stop=True)
                        ins.then_inc(s_pe.h, 1)
                for si in range(nsub):
                    s_pe.inc(1)
                    stats_done[si] = s_pe.v

        if FST < 4:
            continue
        y_consumed = {(15, si): (sq_vals[(15, si)], dbank_free[DB[(15 * nsub + si) % 2]][1])
                      for si in range(nsub)}
        for si, (s0, sw) in enumerate(subs):
            blk = BLK.new(f"rms_{tag}{ck}_{si}")

            @sec(blk, "scalar")
            def _(scalar, si=si, sw=sw, sd=stats_done[si]):
                scalar.wait_ge(s_pe.h, sd)
                chain(scalar, s_sc, [
                    lambda: scalar.activation(out=rows[0][:, 0:sw], in_=pbk[6 + si][0:1, 0:sw],
                                              func=AF.Ln, scale=1.0 / D, bias=cst[0:1, 1:2]),
                    lambda: scalar.activation(out=rows[1][:, 0:sw], in_=rows[0][:, 0:sw],
                                              func=AF.Exp, scale=-0.5, bias=cst[0:1, 2:3]),
                ])
            rr_done = s_sc.v

            @sec(blk, "tensor")
            def _(tensor, si=si, sw=sw, rr_done=rr_done,
                  yc=y_consumed[(15, si)]):
                tensor.wait_ge(s_sc.h, rr_done)
                tensor.wait_ge(s_sc.h, yc[0])
                tensor.wait_ge(s_dv.h, yc[1])
                tensor.matmul(out=pbk[si][:, 0:sw], lhsT=ones_f32[0:1, :],
                              rhs=rows[1][:, 0:sw], start=True, stop=True).then_inc(s_pe.h, 1)
            s_pe.inc(1)
            rbc_done = s_pe.v

            @sec(blk, "vector")
            def _(vector, si=si, s0=s0, sw=sw, xv=tuple(xv), rbc_done=rbc_done, dv_now=s_dv.v):
                vector.wait_ge(s_pe.h, rbc_done)
                vector.wait_ge(s_dv.h, dv_now)
                fns = [lambda: vector.tensor_copy(out=rb[:, 0:sw], in_=pbk[si][:, 0:sw])]
                for c in range(16):
                    fns.append(lambda c=c: vector.tensor_copy(
                        out=tmp[:, 0:sw], in_=y_bf[c][:, s0:s0 + sw]))
                    fns.append(lambda c=c: vector.tensor_tensor(
                        out=tmp[:, 0:sw], in0=tmp[:, 0:sw], in1=rb[:, 0:sw], op=ALU.mult))
                    fns.append(lambda c=c: vector.tensor_tensor(
                        out=xv[c][:, s0:s0 + sw], in0=xv[c][:, s0:s0 + sw],
                        in1=tmp[:, 0:sw], op=ALU.add))
                chain(vector, s_dv, fns)

        if x_dst is not None:
            apply_done = s_dv.v
            blk = BLK.new(f"xs_{tag}{ck}")

            @sec(blk, "gpsimd")
            def _(gpsimd, c0=c0, cw=cw, apply_done=apply_done):
                gpsimd.wait_ge(s_dv.h, apply_done)
                for c in range(16):
                    gpsimd.dma_start(out=x_dst[c * 128:(c + 1) * 128, c0:c0 + cw],
                                     in_=xt[c][:, 0:cw]).then_inc(s_st.h, 16)
            s_st.inc(16 * 16)
            prev_st = s_st.v
    return s_dv.v



def emit_qkv(nc, BLK, sec, bufs, S, T, x_views, cos_dram, sin_dram, tab_base, tok_pair,
             tok_const, wqk, wv, bv_in, ksh, vsh, qsh, tag, with_q=True, x_src=None):
    s_pe, s_dv, s_sc, s_st = S["pe"], S["dv"], S["sc"], S["st"]
    s_lx, s_tab, s_wq = S["lx"], S["tab"], S["wq"]
    s_sp = [S["sp0"], S["sp1"]]
    xt = bufs["xt"]
    xn_q = bufs["xn_q"]
    tabcos, tabsin = bufs["tabc_q"], bufs["tabs_q"]
    pbk = bufs["pbk"]
    wcolf, wcol_bf = bufs["wcolf"], bufs["wcol_bf"]
    bqk_sb, bvb = bufs["bqk_sb"], bufs["bvb"]
    qb2, qc, t2 = bufs["qb2"], bufs["qc"], bufs["t2"]
    qk_bf, v_bf = bufs["qk_bf"], bufs["v_bf"]
    wvb = bufs["wvb"]
    tiles = ttiles(T)

    blk = BLK.new(f"qkv_{tag}_bv")

    @sec(blk, "gpsimd")
    def _(gpsimd, dv_now=S["dv"].v, pe_now=s_pe.v):
        gpsimd.wait_ge(S["dv"].h, dv_now)
        gpsimd.wait_ge(s_pe.h, pe_now)
        gpsimd.dma_start(out=bvb, in_=bv_in.broadcast_to([128, D])).then_inc(S["ld2"].h, 16)
    S["ld2"].inc(16)
    bv_ready = S["ld2"].v

    xn_ready = 0
    for tt, (t0, tw) in enumerate(tiles):
        if x_src is not None:
            xv_t = [xt[c][:, 0:tw] for c in range(16)]
            blk = BLK.new(f"qxl_{tag}{tt}")

            @sec(blk, "scalar")
            def _(scalar, t0=t0, tw=tw, tt=tt, dv_now=s_dv.v, st_now=s_st.v):
                scalar.wait_ge(s_dv.h, dv_now)
                if tt == 0:
                    scalar.wait_ge(s_st.h, st_now)
                for c in range(16):
                    scalar.dma_start(out=xt[c][:, 0:tw],
                                     in_=x_src[c * 128:(c + 1) * 128, t0:t0 + tw]
                                     ).then_inc(s_lx.h, 16)
            s_lx.inc(16 * 16)
            qx_loaded = s_lx.v
            blk = BLK.new(f"qxw_{tag}{tt}")

            @sec(blk, "vector")
            def _(vector, qx_loaded=qx_loaded, dv_now=s_dv.v):
                vector.wait_ge(s_dv.h, dv_now)
                vector.wait_ge(s_lx.h, qx_loaded)
                vector.engine_nop().then_inc(s_dv.h, 1)
            s_dv.inc(1)
            xn_ready = emit_ln(nc, BLK, sec, bufs, S, xv_t, 0, tw, f"{tag}q{tt}",
                               xn=xn_q, xoff=t0, xbf=bufs["xbf_q"])
        else:
            xn_ready = emit_ln(nc, BLK, sec, bufs, S, x_views, t0, tw, f"{tag}q{tt}",
                               xn=xn_q, xoff=t0, xbf=bufs["xbf_q"])

    blk = BLK.new(f"qkv_{tag}_tab")

    @sec(blk, "sync")
    def _(sync, T=T, dv_now=S["dv"].v, pe_now=s_pe.v):
        sync.wait_ge(S["dv"].h, dv_now)
        sync.wait_ge(s_pe.h, pe_now)
        sync.dma_start(out=tabcos[:, 0:T],
                       in_=cos_dram[:, tab_base:tab_base + T]).then_inc(s_tab.h, 16)
        sync.dma_start(out=tabsin[:, 0:T],
                       in_=sin_dram[:, tab_base:tab_base + T]).then_inc(s_tab.h, 16)
    s_tab.inc(32)
    tab_ready = s_tab.v

    js = [j for j in range(32) if with_q or (j % 2 == 1)]
    bufs["rope_prev"] = {0: 0, 1: 0}
    cast_done = {0: 0, 1: 0}
    load_done = {0: 0, 1: 0}
    mmref = {0: 0, 1: 0}
    loadref = {0: 0, 1: 0}
    cnt = 0
    for ji, j in enumerate(js):
        par = ji & 1
        h_idx = j // 2
        is_q = (j % 2) == 0
        blk = BLK.new(f"qk_{tag}_{j}")

        @sec(blk, "gpsimd")
        def _(gpsimd, j=j, par=par, lr=loadref[par], pe_now=s_pe.v):
            if lr > 0:
                gpsimd.wait_ge(s_dv.h, lr)
            else:
                gpsimd.wait_ge(s_pe.h, pe_now)
            gpsimd.dma_start(out=wcolf[par][:], in_=wqk[:, j, :]).then_inc(s_wq.h, 16)
        s_wq.inc(16)
        load_done[par] = s_wq.v

        @sec(blk, "vector")
        def _(vector, par=par, ld=load_done[par], mr=mmref[par], xn_ready=xn_ready,
              ji=ji):
            vector.wait_ge(s_wq.h, ld)
            if mr > 0:
                vector.wait_ge(s_pe.h, mr)
            if ji == 0:
                vector.wait_ge(s_dv.h, xn_ready)
            vector.tensor_copy(out=wcol_bf[par][:], in_=wcolf[par][:]).then_inc(s_dv.h, 1)
        s_dv.inc(1)
        cast_done[par] = s_dv.v
        loadref[par] = cast_done[par]

        for tt, (t0, tw) in enumerate(tiles):
            op = cnt & 1
            ps = pbk[op]
            blk = BLK.new(f"qk_{tag}_{j}_t{tt}")

            @sec(blk, "tensor")
            def _(tensor, par=par, ps=ps, t0=t0, tw=tw, cd=cast_done[par], sc_now=s_sc.v):
                tensor.wait_ge(s_dv.h, cd)
                tensor.wait_ge(s_sc.h, sc_now)
                for g in range(16):
                    ins = tensor.matmul(out=ps[:, 0:tw],
                                        lhsT=wcol_bf[par][:, g * 128:(g + 1) * 128],
                                        rhs=xn_q[g][:, t0:t0 + tw],
                                        start=(g == 0), stop=(g == 15))
                    if g == 15:
                        ins.then_inc(s_pe.h, 1)
            s_pe.inc(1)
            mm_done = s_pe.v
            mmref[par] = mm_done

            @sec(blk, "scalar")
            def _(scalar, j=j, ps=ps, tw=tw, mm_done=mm_done, op=op,
                  rope_prev=bufs["rope_prev"][cnt & 1], sc_now=s_sc.v):
                scalar.wait_ge(s_pe.h, mm_done)
                scalar.wait_ge(S["ld"].h, bufs["bqk_ld"])
                if rope_prev > 0:
                    scalar.wait_ge(s_dv.h, rope_prev)
                scalar.wait_ge(s_sc.h, sc_now)
                scalar.activation(out=qb2[op][:, 0:tw], in_=ps[:, 0:tw], func=AF.Identity,
                                  bias=bqk_sb[:, j:j + 1]).then_inc(s_sc.h, 1)
            s_sc.inc(1)
            qb_done = s_sc.v

            @sec(blk, "vector")
            def _(vector, t0=t0, tw=tw, qb_done=qb_done, op=op, lst=s_sp[op].v,
                  tab_ready=tab_ready, qb=qb2[cnt & 1]):
                vector.wait_ge(s_sc.h, qb_done)
                if lst > 0:
                    vector.wait_ge(s_sp[op].h, lst)
                vector.wait_ge(s_tab.h, tab_ready)
                chain(vector, s_dv, [
                    lambda: vector.tensor_copy(out=t2[0:64, 0:tw], in_=qb[64:128, 0:tw]),
                    lambda: vector.tensor_copy(out=t2[64:128, 0:tw], in_=qb[0:64, 0:tw]),
                    lambda: vector.tensor_tensor(out=qc[:, 0:tw], in0=qb[:, 0:tw],
                                                 in1=tabcos[:, t0:t0 + tw], op=ALU.mult),
                    lambda: vector.tensor_tensor(out=t2[:, 0:tw], in0=t2[:, 0:tw],
                                                 in1=tabsin[:, t0:t0 + tw], op=ALU.mult),
                    lambda: vector.tensor_tensor(out=qk_bf[op][:, 0:tw], in0=qc[:, 0:tw],
                                                 in1=t2[:, 0:tw], op=ALU.add),
                ])
            rope_done = s_dv.v
            bufs["rope_prev"][cnt & 1] = rope_done

            @sec(blk, "gpsimd")
            def _(gpsimd, h_idx=h_idx, is_q=is_q, t0=t0, tw=tw, rope_done=rope_done, op=op):
                gpsimd.wait_ge(s_dv.h, rope_done)
                dst = qsh if is_q else ksh

                def store(base):
                    c0 = base + tok_const + t0
                    gpsimd.dma_start(out=dst[h_idx, :, c0:c0 + tw],
                                     in_=qk_bf[op][:, 0:tw]).then_inc(s_sp[op].h, 16)
                bufs["bsel"](gpsimd, "gpsimd", lambda: store(tok_pair[0]), lambda: store(tok_pair[1]))
            s_sp[op].inc(16)
            cnt += 1

    wv_mm = 0
    for fc in range(8):
        for hf in range(2):
            par = hf
            blk = BLK.new(f"v_{tag}_{fc}_{hf}")

            @sec(blk, "gpsimd")
            def _(gpsimd, fc=fc, hf=hf, par=par, lr=loadref[par]):
                if lr > 0:
                    gpsimd.wait_ge(s_dv.h, lr)
                gpsimd.dma_start(out=wcolf[par][:],
                                 in_=wv[:, fc, hf * 2048:(hf + 1) * 2048]).then_inc(s_wq.h, 16)
            s_wq.inc(16)
            ld = s_wq.v

            @sec(blk, "vector")
            def _(vector, hf=hf, par=par, ld=ld, wv_mm=wv_mm):
                vector.wait_ge(s_wq.h, ld)
                if wv_mm > 0:
                    vector.wait_ge(s_pe.h, wv_mm)
                vector.tensor_copy(out=wvb[:, hf * 2048:(hf + 1) * 2048],
                                   in_=wcolf[par][:]).then_inc(s_dv.h, 1)
            s_dv.inc(1)
            loadref[par] = s_dv.v
        v_cast = s_dv.v

        for tt, (t0, tw) in enumerate(tiles):
            for t in range((tw + 127) // 128):
                sw = min(128, tw - t * 128)
                op = cnt & 1
                ps = pbk[op]
                blk = BLK.new(f"v_{tag}_{fc}_{tt}_{t}")

                @sec(blk, "tensor")
                def _(tensor, t0=t0, t=t, sw=sw, ps=ps, v_cast=v_cast, dv_now=s_dv.v,
                      sc_now=s_sc.v):
                    tensor.wait_ge(s_dv.h, v_cast)
                    tensor.wait_ge(s_dv.h, dv_now)
                    tensor.wait_ge(s_sc.h, sc_now)
                    for g in range(16):
                        ins = tensor.matmul(out=ps[0:sw, 0:256],
                                            lhsT=xn_q[g][:, t0 + t * 128:t0 + t * 128 + sw],
                                            rhs=wvb[:, g * 256:(g + 1) * 256],
                                            start=(g == 0), stop=(g == 15))
                        if g == 15:
                            ins.then_inc(s_pe.h, 1)
                s_pe.inc(1)
                vmm_done = s_pe.v
                wv_mm = vmm_done

                @sec(blk, "vector")
                def _(vector, fc=fc, sw=sw, ps=ps, vmm_done=vmm_done, op=op,
                      lst=s_sp[op].v, dv_now=s_dv.v, bv_ready=bv_ready):
                    vector.wait_ge(s_pe.h, vmm_done)
                    if lst > 0:
                        vector.wait_ge(s_sp[op].h, lst)
                    vector.wait_ge(s_dv.h, dv_now)
                    vector.wait_ge(S["ld2"].h, bv_ready)
                    vector.tensor_tensor(out=v_bf[op & 1][0:sw, 0:256], in0=ps[0:sw, 0:256],
                                         in1=bvb[0:sw, fc * 256:(fc + 1) * 256],
                                         op=ALU.add).then_inc(s_dv.h, 1)
                s_dv.inc(1)
                vevac = s_dv.v

                @sec(blk, "gpsimd")
                def _(gpsimd, fc=fc, t0=t0, t=t, sw=sw, vevac=vevac, op=op):
                    gpsimd.wait_ge(s_dv.h, vevac)

                    def store(base):
                        jj = (base + tok_const + t0) // 128 + t
                        for s in range(2):
                            h_i = fc * 2 + s
                            gpsimd.dma_start(out=vsh[h_i, 0:sw, jj, :],
                                             in_=v_bf[op & 1][0:sw, s * 128:(s + 1) * 128]
                                             ).then_inc(s_sp[op].h, 16)
                    bufs["bsel"](gpsimd, "gpsimd", lambda: store(tok_pair[0]), lambda: store(tok_pair[1]))
                s_sp[op].inc(32)
                cnt += 1
    return s_sp[0].v



def emit_attn(nc, BLK, sec, bufs, S, qt0, qt1, n_kt, tag):
    s_pe, s_dv, s_sc, s_st = S["pe"], S["dv"], S["sc"], S["st"]
    s_kv, s_one = S["kv"], S["one"]
    s_sp = [S["sp0"], S["sp1"]]
    ksh, vsh, qsh, ash = bufs["ksh"], bufs["vsh"], bufs["qsh"], bufs["ash"]
    k_sb, v_sb, q_sb = bufs["k_sb"], bufs["v_sb"], bufs["q_sb"]
    pt, msk = bufs["pt"], bufs["msk"]
    ones_bf, ones_f32 = bufs["ones_bf"], bufs["ones_f32"]
    r_sb, rb_sb, o_sb = bufs["r_sb"], bufs["rb_sb"], bufs["o_sb"]
    ps_st, ps_pv, ps_l, ps_rb = bufs["ps_st"], bufs["ps_pv"], bufs["ps_l"], bufs["ps_rb"]
    qlen = (qt1 - qt0) * 128

    blocks = []
    t = qt0
    while t < qt1:
        blocks.append((t, min(t + 4, qt1)))
        t = min(t + 4, qt1)

    blk = BLK.new(f"at_{tag}_msk")

    @sec(blk, "sync")
    def _(sync):
        sync.dma_start(out=msk, in_=bufs["mask_in"]).then_inc(s_one.h, 16)
    s_one.inc(16)
    msk_ld = s_one.v

    last_exp = {0: 0, 1: 0}
    last_lpv = {0: 0, 1: 0}
    last_ost = {0: 0, 1: 0}
    prev_norm_dv = 0
    for hl in range(8):
        blk = BLK.new(f"at_{tag}{hl}_ld")

        @sec(blk, "sync")
        def _(sync, hl=hl, pe_now=s_pe.v):
            if hl > 0:
                sync.wait_ge(s_pe.h, pe_now)

            def load(hb):
                hh = hb + hl
                sync.dma_start(out=k_sb[:, 0:n_kt * 128],
                               in_=ksh[hh, :, 0:n_kt * 128]).then_inc(s_kv.h, 16)
                sync.dma_start(out=v_sb[:, 0:n_kt * 128],
                               in_=vsh[hh, :, 0:n_kt, :].rearrange("p j f -> p (j f)")
                               ).then_inc(s_kv.h, 16)
                sync.dma_start(out=q_sb[:, 0:qlen],
                               in_=qsh[hh, :, qt0 * 128:qt1 * 128]).then_inc(s_kv.h, 16)
            bufs["bsel"](sync, "sync", lambda: load(0), lambda: load(8))
        s_kv.inc(48)
        kv_loaded = s_kv.v

        for (bt0, bt1) in blocks:
            qw = (bt1 - bt0) * 128
            boff = (bt0 - qt0) * 128
            nkt_b = bt1
            for kt in range(nkt_b):
                par = kt % 2
                scol0 = max(0, (kt - bt0)) * 128
                diag = kt >= bt0
                blk = BLK.new(f"at_{tag}{hl}_{bt0}_{kt}")

                @sec(blk, "tensor")
                def _(tensor, kt=kt, par=par, scol0=scol0, boff=boff, qw=qw,
                      kv_loaded=kv_loaded, le=last_exp[par], first=(kt == 0 and bt0 == qt0)):
                    if first:
                        tensor.wait_ge(s_kv.h, kv_loaded)
                    if le > 0:
                        tensor.wait_ge(s_sc.h, le)
                    tensor.matmul(out=ps_st[par][:, scol0:qw],
                                  lhsT=k_sb[:, kt * 128:(kt + 1) * 128],
                                  rhs=q_sb[:, boff + scol0:boff + qw],
                                  start=True, stop=True).then_inc(s_pe.h, 1)
                s_pe.inc(1)
                st_done = s_pe.v

                @sec(blk, "scalar")
                def _(scalar, par=par, scol0=scol0, qw=qw, st_done=st_done, lp=last_lpv[par]):
                    scalar.wait_ge(s_pe.h, st_done)
                    if lp > 0:
                        scalar.wait_ge(s_pe.h, lp)
                    scalar.activation(out=pt[par][:, scol0:qw], in_=ps_st[par][:, scol0:qw],
                                      func=AF.Exp, scale=SM_SCALE).then_inc(s_sc.h, 1)
                s_sc.inc(1)
                last_exp[par] = s_sc.v
                exp_done = s_sc.v

                if diag:
                    @sec(blk, "vector")
                    def _(vector, par=par, scol0=scol0, exp_done=exp_done, dv_now=s_dv.v,
                          msk_ld=msk_ld):
                        vector.wait_ge(s_sc.h, exp_done)
                        vector.wait_ge(s_dv.h, dv_now)
                        vector.wait_ge(s_one.h, msk_ld)
                        vector.tensor_tensor(out=pt[par][:, scol0:scol0 + 128],
                                             in0=pt[par][:, scol0:scol0 + 128],
                                             in1=msk[:], op=ALU.mult).then_inc(s_dv.h, 1)
                    s_dv.inc(1)
                    rsem, rval = s_dv, s_dv.v
                else:
                    rsem, rval = s_sc, exp_done

                @sec(blk, "tensor")
                def _(tensor, par=par, scol0=scol0, qw=qw, kt=kt, nkt_b=nkt_b,
                      rsem=rsem, rval=rval, pnorm=prev_norm_dv):
                    tensor.wait_ge(rsem.h, rval)
                    if kt == 0 and pnorm > 0:
                        tensor.wait_ge(s_dv.h, pnorm)
                    tensor.matmul(out=ps_l[0:1, scol0:qw], lhsT=ones_bf[:, 0:1],
                                  rhs=pt[par][:, scol0:qw], start=(kt == 0), stop=(kt == nkt_b - 1))
                    tensor.matmul(out=ps_pv[:, scol0:qw], lhsT=v_sb[:, kt * 128:(kt + 1) * 128],
                                  rhs=pt[par][:, scol0:qw], start=(kt == 0),
                                  stop=(kt == nkt_b - 1)).then_inc(s_pe.h, 1)
                s_pe.inc(1)
                last_lpv[par] = s_pe.v

            pv_done = s_pe.v
            op = (bt0 // 4) % 2
            blk = BLK.new(f"at_{tag}{hl}_{bt0}_n")

            @sec(blk, "vector")
            def _(vector, pv_done=pv_done, qw=qw, dv_now=s_dv.v):
                vector.wait_ge(s_pe.h, pv_done)
                vector.wait_ge(s_dv.h, dv_now)
                vector.reciprocal(out=r_sb[0:1, 0:qw], in_=ps_l[0:1, 0:qw]).then_inc(s_dv.h, 1)
            s_dv.inc(1)
            r_done = s_dv.v

            @sec(blk, "tensor")
            def _(tensor, r_done=r_done, qw=qw):
                tensor.wait_ge(s_dv.h, r_done)
                tensor.matmul(out=ps_rb[:, 0:qw], lhsT=ones_f32[0:1, :], rhs=r_sb[0:1, 0:qw],
                              start=True, stop=True).then_inc(s_pe.h, 1)
            s_pe.inc(1)
            rb_done = s_pe.v

            @sec(blk, "vector")
            def _(vector, op=op, rb_done=rb_done, qw=qw, lst=s_sp[op].v):
                vector.wait_ge(s_pe.h, rb_done)
                if lst > 0:
                    vector.wait_ge(s_sp[op].h, lst)
                chain(vector, s_dv, [
                    lambda: vector.tensor_copy(out=rb_sb[:, 0:qw], in_=ps_rb[:, 0:qw]),
                    lambda: vector.tensor_tensor(out=o_sb[op][:, 0:qw], in0=ps_pv[:, 0:qw],
                                                 in1=rb_sb[:, 0:qw], op=ALU.mult),
                ])
            o_done = s_dv.v
            prev_norm_dv = o_done

            @sec(blk, "gpsimd")
            def _(gpsimd, hl=hl, bt0=bt0, op=op, qw=qw, o_done=o_done):
                gpsimd.wait_ge(s_dv.h, o_done)

                def store(hb):
                    gpsimd.dma_start(out=ash[hb + hl, :, bt0 * 128:bt0 * 128 + qw],
                                     in_=o_sb[op][:, 0:qw]).then_inc(s_sp[op].h, 16)
                bufs["bsel"](gpsimd, "gpsimd", lambda: store(0), lambda: store(8))
            s_sp[op].inc(16)
            last_ost[op] = s_sp[op].v
    return s_st.v



def emit_barrier(nc, BLK, sec, bufs, S, bar_idx, tag, wait_sems):
    s_bar, s_poll = S["bar"], S["poll"]
    fsh, one_i, flag_sb = bufs["fsh"], bufs["one_i"], bufs["flag_sb"]
    regs = bufs["regs"]
    blk = BLK.new(f"bar_{tag}")
    waits = [(s.h, s.v) for s in wait_sems]

    @sec(blk, "gpsimd")
    def _(gpsimd, waits=waits, bar_idx=bar_idx):
        for hh, v in waits:
            gpsimd.wait_ge(hh, v)
        gpsimd.wait_ge(S["one"].h, bufs["one_ld"])

        def store(row):
            gpsimd.dma_start(out=fsh[row:row + 1, 0:64], in_=one_i).then_inc(S["st"].h, 16)
        bufs["bsel"](gpsimd, "gpsimd",
                     lambda: store(2 * bar_idx), lambda: store(2 * bar_idx + 1))
    S["st"].inc(16)

    import os
    _NOPOLL = os.environ.get("MK_NOPOLL", "0") == "1"

    @sec(blk, "sync")
    def _(sync, bar_idx=bar_idx, waits=waits):
        for hh, v in waits:
            sync.wait_ge(hh, v)
        if _NOPOLL:
            sync.sem_inc(s_bar.h, 1)
            return
        r_cond = regs["sync"]["r_cond"]
        r_exp = regs["sync"]["r_exp"]
        r_t = regs["sync"]["r_t"]
        sync.reg_mov(r_cond, 1)
        with sync.While(r_cond):
            def poll(row):
                sync.dma_start(out=flag_sb[0:1, 0:1],
                               in_=fsh[row:row + 1, 0:1]).then_inc(s_poll.h, 16)
            bufs["bsel"](sync, "sync",
                         lambda: poll(2 * bar_idx + 1), lambda: poll(2 * bar_idx))
            sync.reg_add(r_exp, r_exp, 16)
            sync.wait_ge(s_poll.h, r_exp)
            sync.reg_load(r_t, flag_sb[0:1, 0:1])
            sync.reg_sub(r_cond, 1, r_t)
        sync.sem_inc(s_bar.h, 1)
    s_bar.inc(1)
    return s_bar.v



def emit_dequant(nc, BLK, sec, bufs, S, b2g, sg, b2d, sd, wshg, wshd):
    s_dv, s_std = S["dv"], S["std"]
    s_dq = [S["dq0"], S["dq1"]]
    byt, scb = bufs["dq_byt"], bufs["dq_scb"]
    nr8, n15, rb1, rb2, rm, ru, rv, bits, wt = (
        bufs["dq_nr8"], bufs["dq_n15"], bufs["dq_rb1"], bufs["dq_rb2"], bufs["dq_rm"],
        bufs["dq_ru"], bufs["dq_rv"], bufs["dq_bits"], bufs["dq_wt"])
    regs = bufs["regs"]

    tiles = [("g", g, q, 1408, 11) for g in range(16) for q in range(4)]
    tiles += [("d", g, 0, 1024, 8) for g in range(44)]
    for i, (which, g, q, w, nct) in enumerate(tiles):
        p = i % 2
        blk = BLK.new(f"dq{i}")

        @sec(blk, "sync")
        def _(sync, which=which, g=g, q=q, w=w, p=p, i=i, dec_prev=s_dv.v):
            if i >= 2:
                sync.wait_ge(s_dv.h, dec_prev)
            src = b2g if which == "g" else b2d
            sync.dma_start(out=byt[p][:, 0:w],
                           in_=src[g * 128:(g + 1) * 128, q * w:(q + 1) * w]).then_inc(s_dq[p].h, 16)

        @sec(blk, "scalar")
        def _(scalar, which=which, g=g, q=q, w=w, p=p, i=i, dec_prev=s_dv.v):
            if i >= 2:
                scalar.wait_ge(s_dv.h, dec_prev)
            ssc = sg if which == "g" else sd
            scalar.dma_start(out=scb[p][:, 0:w],
                             in_=ssc[g:g + 1, q * w:(q + 1) * w].broadcast_to([128, w])).then_inc(s_dq[p].h, 16)
        s_dq[p].inc(32)
        loaded = s_dq[p].v

        @sec(blk, "vector")
        def _(vector, w=w, p=p, i=i, loaded=loaded, st_prev=s_std.v):
            vector.wait_ge(s_dq[p].h, loaded)
            if i >= 2:
                vector.wait_ge(s_std.h, st_prev)
            B = byt[p][:, 0:w]
            chain(vector, s_dv, [
                lambda: vector.tensor_scalar(out=nr8[0:64, 0:w], in0=B[0:64], scalar1=15,
                                             scalar2=None, op0=ALU.bitwise_and),
                lambda: vector.tensor_scalar(out=nr8[64:128, 0:w], in0=B[64:128], scalar1=4,
                                             scalar2=None, op0=ALU.logical_shift_right),
                lambda: vector.tensor_copy(out=n15[:, 0:w], in_=nr8[:, 0:w]),
                lambda: vector.tensor_scalar(out=rb1[:, 0:w], in0=n15[:, 0:w], scalar1=8, scalar2=12,
                                             op0=ALU.bitwise_and, op1=ALU.logical_shift_left),
                lambda: vector.tensor_scalar(out=rb2[:, 0:w], in0=n15[:, 0:w], scalar1=7, scalar2=6,
                                             op0=ALU.bitwise_and, op1=ALU.logical_shift_left),
                lambda: vector.tensor_scalar(out=rm[:, 0:w], in0=n15[:, 0:w], scalar1=6,
                                             scalar2=None, op0=ALU.bitwise_and),
                lambda: vector.tensor_scalar(out=ru[:, 0:w], in0=rm[:, 0:w], scalar1=0, scalar2=16128,
                                             op0=ALU.is_equal, op1=ALU.mult),
                lambda: vector.tensor_scalar(out=rm[:, 0:w], in0=n15[:, 0:w], scalar1=7,
                                             scalar2=None, op0=ALU.bitwise_and),
                lambda: vector.tensor_scalar(out=rv[:, 0:w], in0=rm[:, 0:w], scalar1=1, scalar2=16064,
                                             op0=ALU.is_equal, op1=ALU.mult),
                lambda: vector.scalar_tensor_tensor(out=rb1[:, 0:w], in0=rb1[:, 0:w], scalar=16128.0,
                                                    in1=rb2[:, 0:w], op0=ALU.add, op1=ALU.add),
                lambda: vector.tensor_tensor(out=rb1[:, 0:w], in0=rb1[:, 0:w], in1=ru[:, 0:w],
                                             op=ALU.subtract),
                lambda: vector.tensor_tensor(out=bits[:, 0:w], in0=rb1[:, 0:w], in1=rv[:, 0:w],
                                             op=ALU.add),
                lambda: vector.tensor_tensor(out=wt[p][:, 0:w], in0=bits[:, 0:w].bitcast(BF16),
                                             in1=scb[p][:, 0:w], op=ALU.mult),
            ])
        dec = s_dv.v

        st_eng = "scalar" if (i & 1) else "sync"

        @sec(blk, st_eng)
        def _(eng, which=which, g=g, q=q, w=w, nct=nct, p=p, dec=dec, st_eng=st_eng):
            eng.wait_ge(s_dv.h, dec)
            src = wt[p][:, 0:w].rearrange("p (a b) -> p a b", a=nct)

            def store(half):
                if which == "g":
                    dst = wshg[:, q * nct:(q + 1) * nct, half, g * 128:(g + 1) * 128]
                else:
                    dst = wshd[:, half * 8:(half + 1) * 8, g, :]
                eng.dma_start(out=dst, in_=src).then_inc(s_std.h, 16)
            bufs["bsel"](eng, st_eng, lambda: store(0), lambda: store(1))
        s_std.inc(16)
    return s_std.v



def build_fused():
    nc = bass.Bass("TRN2", target_bir_lowering=False)
    x_in = nc.dram_tensor("x_in", [D, T_LOW], F32, kind="ExternalInput").ap()
    xc_in = nc.dram_tensor("xc_in", [D, T_CONE], F32, kind="ExternalInput").ap()
    cfg_in = nc.dram_tensor("cfg_in", [1, 16], I32, kind="ExternalInput").ap()
    one_in = nc.dram_tensor("one_in", [1, 64], I32, kind="ExternalInput").ap()
    b2g_in = nc.dram_tensor("b2g_in", [D, NI], U8, kind="ExternalInput").ap()
    sg_in = nc.dram_tensor("sg_in", [16, NI], F16, kind="ExternalInput").ap()
    b2d_in = nc.dram_tensor("b2d_in", [NI, D // 2], U8, kind="ExternalInput").ap()
    sd_in = nc.dram_tensor("sd_in", [44, D // 2], F16, kind="ExternalInput").ap()
    wqk_in = nc.dram_tensor("wqk_in", [128, 32, 2048], F32, kind="ExternalInput").ap()
    wv_in = nc.dram_tensor("wv_in", [128, 8, 4096], F32, kind="ExternalInput").ap()
    bqk_in = nc.dram_tensor("bqk_in", [128, 32], F32, kind="ExternalInput").ap()
    bv_in = nc.dram_tensor("bv_in", [1, D], F32, kind="ExternalInput").ap()
    cos_o = nc.dram_tensor("cos_o", [128, T_LOW], BF16, kind="ExternalInput").ap()
    sin_o = nc.dram_tensor("sin_o", [128, T_LOW], BF16, kind="ExternalInput").ap()
    cos_c = nc.dram_tensor("cos_c", [128, T_CONE], BF16, kind="ExternalInput").ap()
    sin_c = nc.dram_tensor("sin_c", [128, T_CONE], BF16, kind="ExternalInput").ap()
    mask_in = nc.dram_tensor("mask_in", [128, 128], BF16, kind="ExternalInput").ap()
    x_out = nc.dram_tensor("x_out", [D, T_LOW], F32, kind="ExternalOutput").ap()

    wshg = nc.dram_tensor("wshg", [128, 44, 2, 2048], BF16, kind="Internal", addr_space="Shared").ap()
    wshd = nc.dram_tensor("wshd", [128, 16, 44, 128], BF16, kind="Internal", addr_space="Shared").ap()
    ksh = nc.dram_tensor("ksh", [NH, 128, SEQ], BF16, kind="Internal", addr_space="Shared").ap()
    qsh = nc.dram_tensor("qsh", [NH, 128, SEQ], BF16, kind="Internal", addr_space="Shared").ap()
    vsh = nc.dram_tensor("vsh", [NH, 128, SEQ // 128, 128], BF16, kind="Internal", addr_space="Shared").ap()
    ash = nc.dram_tensor("ash", [NH, 128, SEQ], F32, kind="Internal", addr_space="Shared").ap()
    fsh = nc.dram_tensor("fsh", [8, 256], I32, kind="Internal", addr_space="Shared").ap()
    x1d = nc.dram_tensor("x1d", [D, T_LOW], F32, kind="Internal").ap()

    with ExitStack() as ctx:
        E = ctx.enter_context
        BLK = BlkMgr(nc)
        bufs = {}

        xtA = E(nc.sbuf_tensor("xtA", [128, 16 * 768], F32))
        bufs["xt"] = [xtA[:, c * 768:(c + 1) * 768] for c in range(16)]
        atpA = E(nc.sbuf_tensor("atpA", [128, 2 * 768], F32))
        bufs["atp"] = [atpA[:, 0:768], atpA[:, 768:1536]]
        xnA = E(nc.sbuf_tensor("xnA", [128, 16 * 768], BF16))
        bufs["xn"] = [xnA[:, c * 768:(c + 1) * 768] for c in range(16)]
        bufs["y_bf"] = [xnA[:, c * 768:(c + 1) * 768] for c in range(16)]
        hA = E(nc.sbuf_tensor("hA", [128, 44 * 768], BF16))
        bufs["h"] = [hA[:, i * 768:(i + 1) * 768] for i in range(44)]
        bufs["xbf"] = [hA[:, c * 768:c * 768 + 512] for c in range(16)]
        bufs["k_sb"] = hA[:, 0:4096]
        bufs["v_sb"] = hA[:, 4096:8192]
        bufs["q_sb"] = hA[:, 8192:8192 + 2816]
        bufs["pt"] = [hA[:, 11008 + i * 512:11008 + (i + 1) * 512] for i in range(2)]
        dqA = hA
        bufs["dq_nr8"] = dqA[:, 0:704].bitcast(U8)
        bufs["dq_byt"] = [dqA[:, 704 + i * 704:704 + (i + 1) * 704].bitcast(U8) for i in range(2)]
        bufs["dq_scb"] = [dqA[:, 2112 + i * 1408:2112 + (i + 1) * 1408].bitcast(F16) for i in range(2)]
        off = 4928
        for nm in ("dq_n15", "dq_rb1", "dq_rb2", "dq_rm", "dq_ru", "dq_rv", "dq_bits"):
            bufs[nm] = dqA[:, off:off + 1408].bitcast(U16)
            off += 1408
        bufs["dq_wt"] = [dqA[:, off + i * 1408:off + (i + 1) * 1408] for i in range(2)]

        wguA = [E(nc.sbuf_tensor(f"wgu{i}", [128, 4096], BF16)) for i in range(2)]
        bufs["wguA"] = wguA
        wdcA = [E(nc.sbuf_tensor(f"wdc{i}", [128, NI], BF16)) for i in range(2)]
        bufs["wdc"] = wdcA
        bufs["wcolf"] = [wdcA[0][:].bitcast(F32)[:, 0:2048], wdcA[1][:].bitcast(F32)[:, 0:2048]]
        bufs["wcol_bf"] = [wguA[0][:, 0:2048], wguA[0][:, 2048:4096]]
        bufs["rows"] = [E(nc.sbuf_tensor(f"row{i}", [1, 512], F32)) for i in range(3)]
        bufs["a_b"] = E(nc.sbuf_tensor("a_b", [128, 512], F32))
        sgbA = E(nc.sbuf_tensor("sgbA", [128, 768], BF16))
        bufs["sgb"] = [sgbA[:, 0:512], sgbA[:, 512:768]]
        bufs["tmp"] = E(nc.sbuf_tensor("tmp", [128, 512], F32))
        bufs["rb"] = E(nc.sbuf_tensor("rb", [128, 512], F32))
        bufs["b_b"] = bufs["rb"]
        bufs["qb2"] = [wdcA[0][:, 4096:5120].bitcast(F32), bufs["rb"][:]]
        bufs["qc"] = wdcA[1][:, 4096:5120].bitcast(F32)
        bufs["t2"] = bufs["a_b"]
        bufs["qk_bf"] = [wguA[1][:, 0:512], wguA[1][:, 512:1024]]
        bufs["v_bf"] = [wguA[1][:, 1024:1280], wguA[1][:, 1280:1536]]
        bufs["msk"] = wguA[1][:, 1536:1664]
        bufs["bvb"] = wguA[1][:, 1664:3712]
        bufs["xn_q"] = [hA[:, g * 1408:(g + 1) * 1408] for g in range(16)]
        bufs["tabc_q"] = hA[:, 22528:23936]
        bufs["tabs_q"] = hA[:, 23936:25344]
        bufs["wvb"] = hA[:, 25344:29440]
        bufs["xbf_q"] = [xnA[:, c * 768:c * 768 + 512] for c in range(16)]
        bufs["bqk_sb"] = E(nc.sbuf_tensor("bqk_sb", [128, 32], F32))
        bufs["ones_bf"] = E(nc.sbuf_tensor("ones_bf", [128, 2], BF16))
        bufs["ones_f32"] = E(nc.sbuf_tensor("ones_f32", [1, 128], F32))
        bufs["cst"] = E(nc.sbuf_tensor("cst", [1, 4], F32))
        bufs["r_sb"] = bufs["rows"][0]
        bufs["rb_sb"] = bufs["rb"]
        bufs["atmp"] = bufs["tmp"]
        bufs["o_sb"] = [bufs["qc"], bufs["t2"]]
        cfg_sb = E(nc.sbuf_tensor("cfg_sb", [1, 16], I32))
        bufs["one_i"] = E(nc.sbuf_tensor("one_i", [1, 64], I32))[:]
        bufs["flag_sb"] = E(nc.sbuf_tensor("flag_sb", [1, 4], I32))

        pbk = [E(nc.psum_tensor(f"pb{i}", [128, 512], F32)) for i in range(8)]
        bufs["pbk"] = pbk
        bufs["psA"], bufs["psB"], bufs["psC"], bufs["psD"] = pbk[0], pbk[1], pbk[2], pbk[3]
        bufs["psS1"], bufs["psS2"] = pbk[6], pbk[7]
        bufs["ps_st"] = [pbk[0], pbk[1]]
        bufs["ps_pv"] = pbk[2]
        bufs["ps_l"] = pbk[6]
        bufs["ps_rb"] = pbk[3]
        bufs["ksh"], bufs["vsh"], bufs["qsh"], bufs["ash"], bufs["fsh"] = ksh, vsh, qsh, ash, fsh
        bufs["mask_in"] = mask_in

        names = ["ld", "ld2", "pe", "dv", "sc", "st", "std", "bar", "poll", "one",
                 "lx", "tab", "wq", "wg0", "wg1", "wd0", "wd1", "kv", "as0", "as1",
                 "sp0", "sp1", "dq0", "dq1"]
        sems = {n: Sem(E(nc.semaphore(f"s_{n}"))) for n in names}

        eng_names = ["sync", "tensor", "vector", "scalar", "gpsimd"]
        rr = {}
        rbit = {}
        regs = {"sync": {}, "gpsimd": {}, "scalar": {}}
        bufs["regs"] = regs
        bufs["rbit"] = rbit

        def bsel(eng, en, fA, fB):
            with eng.If_eq(rbit[en], 0):
                fA()
            with eng.Else():
                fB()
        bufs["bsel"] = bsel

        blk = BLK.new("boot")

        @blk.sync
        def _(sync):
            sync.dma_start(out=cfg_sb[:], in_=cfg_in).then_inc(sems["ld"].h, 16)
            sems["ld"].inc(16)
            sync.wait_ge(sems["ld"].h, sems["ld"].v)
            r = sync.alloc_register("role_sync")
            sync.reg_load(r, cfg_sb[0:1, 0:1])
            rr["sync"] = r
            rbit["sync"] = sync.alloc_register("bit_sync")
            sync.reg_load(rbit["sync"], cfg_sb[0:1, 6:7])
            regs["sync"]["r_cond"] = sync.alloc_register("sy_cond")
            regs["sync"]["r_exp"] = sync.alloc_register("sy_exp")
            regs["sync"]["r_t"] = sync.alloc_register("sy_t")
            sync.reg_mov(regs["sync"]["r_exp"], 0)
        cfg_ld = sems["ld"].v

        @blk.tensor
        def _(tensor):
            tensor.wait_ge(sems["ld"].h, cfg_ld)
            r = tensor.alloc_register("role_tensor")
            tensor.reg_load(r, cfg_sb[0:1, 0:1])
            rr["tensor"] = r

        @blk.vector
        def _(vector):
            vector.wait_ge(sems["ld"].h, cfg_ld)
            r = vector.alloc_register("role_vector")
            vector.reg_load(r, cfg_sb[0:1, 0:1])
            rr["vector"] = r
            vector.memset(bufs["ones_f32"][:], 1.0)
            vector.memset(bufs["ones_bf"][:], 1.0)
            vector.memset(bufs["cst"][0:1, 0:1], LN_EPS)
            vector.memset(bufs["cst"][0:1, 1:2], RMS_EPS)
            vector.memset(bufs["cst"][0:1, 2:3], float(np.log(0.5))).then_inc(sems["dv"].h, 1)
        sems["dv"].inc(1)

        @blk.scalar
        def _(scalar):
            scalar.wait_ge(sems["ld"].h, cfg_ld)
            r = scalar.alloc_register("role_scalar")
            scalar.reg_load(r, cfg_sb[0:1, 0:1])
            rr["scalar"] = r
            rbit["scalar"] = scalar.alloc_register("bit_scalar")
            scalar.reg_load(rbit["scalar"], cfg_sb[0:1, 6:7])

        @blk.gpsimd
        def _(gpsimd):
            gpsimd.wait_ge(sems["ld"].h, cfg_ld)
            r = gpsimd.alloc_register("role_gpsimd")
            gpsimd.reg_load(r, cfg_sb[0:1, 0:1])
            rr["gpsimd"] = r
            rbit["gpsimd"] = gpsimd.alloc_register("bit_gpsimd")
            gpsimd.reg_load(rbit["gpsimd"], cfg_sb[0:1, 6:7])

        def mk_sec(role_id):
            def sec(blk, eng_name):
                def deco(f):
                    def inner(eng):
                        if role_id is None:
                            f(eng)
                        else:
                            with eng.If_eq(rr[eng_name], role_id):
                                f(eng)
                    getattr(blk, eng_name)(inner)
                return deco
            return sec

        sec_c = mk_sec(None)

        blk = BLK.new("loads")

        @blk.sync
        def _(sync):
            sync.dma_start(out=bufs["bqk_sb"][:], in_=bqk_in).then_inc(sems["ld"].h, 16)
            sync.dma_start(out=bufs["one_i"], in_=one_in).then_inc(sems["one"].h, 16)
        sems["ld"].inc(16)
        sems["one"].inc(16)
        bufs["one_ld"] = sems["one"].v
        bufs["bqk_ld"] = sems["ld"].v

        import os as _os
        if _os.environ.get("MK_NODEQ", "0") != "1":
            emit_dequant(nc, BLK, sec_c, bufs, sems, b2g_in, sg_in, b2d_in, sd_in, wshg, wshd)
        emit_barrier(nc, BLK, sec_c, bufs, sems, 0, "b1", [sems["std"]])
        b1 = sems["bar"].v

        STAGE = 5
        for role_id, tagr in ((0, "L"), (1, "H")):
            sec = mk_sec(role_id)
            S = {k: sems[k].clone() for k in sems}
            To = T_LOW if role_id == 0 else T_HIGH
            qt0, qt1 = (0, KT_LOW) if role_id == 0 else (KT_LOW, KT_HIGH)
            n_kt = KT_LOW if role_id == 0 else KT_HIGH
            tok_pair = (0, T_LOW) if role_id == 0 else (S_SPLIT, S_SPLIT + T_HIGH)
            cone_pair = (0, T_CONE)

            blk = BLK.new(f"{tagr}_start")

            @sec(blk, "tensor")
            def _(tensor, b1=b1):
                tensor.wait_ge(S["bar"].h, b1)

            @sec(blk, "gpsimd")
            def _(gpsimd, b1=b1):
                gpsimd.wait_ge(S["bar"].h, b1)

            if STAGE >= 2:
                emit_ffn(nc, BLK, sec, bufs, S, To, wshg, wshd, f"{tagr}f1",
                         x_src=x_in, x_dst=x1d)

            if STAGE >= 3:
                if role_id == 1:
                    for ci, (ct0, cw) in enumerate(((0, 768), (768, 640))):
                        emit_ffn(nc, BLK, sec, bufs, S, cw, wshg, wshd, f"Hc{ci}",
                                 x_src=xc_in[:, ct0:ct0 + cw])
                        cone_views = [bufs["xt"][c][:, 0:cw] for c in range(16)]
                        emit_qkv(nc, BLK, sec, bufs, S, cw, cone_views,
                                 cos_c, sin_c, ct0,
                                 cone_pair, ct0, wqk_in, wv_in, bv_in, ksh, vsh, qsh,
                                 f"Hc{ci}", with_q=False)

                emit_qkv(nc, BLK, sec, bufs, S, To, None,
                         cos_o, sin_o, 0,
                         tok_pair, 0, wqk_in, wv_in, bv_in, ksh, vsh, qsh, f"{tagr}o",
                         with_q=True, x_src=x1d)

                emit_barrier(nc, BLK, sec, bufs, S, 1, f"{tagr}b2", [S["st"], S["sp0"], S["sp1"]])
                b2 = S["bar"].v

            if STAGE >= 4:
                blk = BLK.new(f"{tagr}_preat")

                @sec(blk, "tensor")
                def _(tensor, b2=b2):
                    tensor.wait_ge(S["bar"].h, b2)

                @sec(blk, "sync")
                def _(sync, b2=b2):
                    sync.wait_ge(S["bar"].h, b2)

                emit_attn(nc, BLK, sec, bufs, S, qt0, qt1, n_kt, f"{tagr}a")

                emit_barrier(nc, BLK, sec, bufs, S, 2, f"{tagr}b3", [S["st"], S["sp0"], S["sp1"]])
                b3 = S["bar"].v

            if STAGE >= 5:
                emit_ffn(nc, BLK, sec, bufs, S, To, wshg, wshd, f"{tagr}f3",
                         x_src=x1d, x_dst=x_out,
                         attn_add={"pair": tok_pair, "b3": b3})

            blk = BLK.new(f"{tagr}_out")

            @sec(blk, "gpsimd")
            def _(gpsimd, st_fin=S["st"].v):
                gpsimd.wait_ge(S["st"].h, st_fin)

        BLK.close()
    return nc



_G128 = np.concatenate([np.arange(0, 128, 2), np.arange(1, 128, 2)])


def _permuted_rows(n):
    a = np.arange(n)
    return (a // 128) * 128 + _G128[a % 128]


def _rope_tables():
    inv = 1.0 / (10000.0 ** (np.arange(0, HD, 2, dtype=np.float64) / HD))
    t = np.arange(SEQ, dtype=np.float64)
    fr = np.outer(t, inv)
    emb = np.concatenate([fr, fr], axis=1)
    cos = np.cos(emb).astype(np.float32).T
    sin = np.sin(emb).astype(np.float32).T
    sinA = sin.copy()
    sinA[0:64] = -sin[0:64]
    return cos, sinA


def kernel(L0_attn_hidden_states, L0_attn_w_qkv, L0_attn_b_qkv,
           L0_ffn_gate_up_packed, L0_ffn_gate_up_scales,
           L0_ffn_down_packed, L0_ffn_down_scales):
    x = np.asarray(L0_attn_hidden_states)
    w_qkv = np.asarray(L0_attn_w_qkv)
    b_qkv = np.asarray(L0_attn_b_qkv)
    gup = np.asarray(L0_ffn_gate_up_packed)
    gus = np.asarray(L0_ffn_gate_up_scales)
    dnp = np.asarray(L0_ffn_down_packed)
    dns = np.asarray(L0_ffn_down_scales)
    cores = list(range(NC))

    pr_d = _permuted_rows(D)
    pr_io = _permuted_rows(2 * NI)
    pr_dm = _permuted_rows(D)
    byrow = lambda n: (np.arange(n) // 128) * 64 + (np.arange(n) % 128) % 64
    bytes_g = gup.view(np.uint8).reshape(2 * NI, 1024).T
    gb = bytes_g[byrow(D)][:, pr_io]
    bytes_d = dnp.view(np.uint8).reshape(D, 2816).T
    db = bytes_d[byrow(NI)][:, pr_dm]
    sg_p = gus[:, pr_io]
    sd_p = dns[:, pr_dm]

    cos, sinA = _rope_tables()
    wr = w_qkv.reshape(NH, 3, HD, D)
    wqkT = wr[:, :2].transpose(3, 0, 1, 2).reshape(D, 2 * D)[pr_d]
    wqk_cols = np.ascontiguousarray(
        wqkT.reshape(16, 128, 32, 128).transpose(1, 2, 0, 3).reshape(128, 32, 2048))
    wvT = wr[:, 2].transpose(2, 0, 1).reshape(D, D)[pr_d][:, _permuted_rows(D)]
    wv_cols = np.ascontiguousarray(
        wvT.reshape(16, 128, 8, 256).transpose(1, 2, 0, 3).reshape(128, 8, 4096))
    br = b_qkv.reshape(NH, 3, HD)
    bqk_t = np.ascontiguousarray(br[:, :2].reshape(NH * 2, HD).T)
    bv_row = np.ascontiguousarray(br[:, 2].reshape(1, D)[:, _permuted_rows(D)])
    maskT = np.triu(np.ones((128, 128), np.float32)).astype(BF)

    xT = np.ascontiguousarray(x.transpose(0, 2, 1)[:, pr_d, :])

    in_maps = []
    for c in cores:
        b = c // 4
        is_high = (c % 4) // 2
        bit = c % 2
        if not is_high:
            tok0, To, ctok0 = bit * T_LOW, T_LOW, 0
        else:
            tok0, To, ctok0 = S_SPLIT + bit * T_HIGH, T_HIGH, bit * T_CONE
        x_own = np.zeros((D, T_LOW), np.float32)
        x_own[:, 0:To] = xT[b][:, tok0:tok0 + To]
        xc = (np.ascontiguousarray(xT[b][:, ctok0:ctok0 + T_CONE]) if is_high
              else np.zeros((D, T_CONE), np.float32))
        cos_own = np.zeros((128, T_LOW), np.float32)
        sin_own = np.zeros((128, T_LOW), np.float32)
        cos_own[:, 0:To] = cos[:, tok0:tok0 + To]
        sin_own[:, 0:To] = sinA[:, tok0:tok0 + To]
        cfg = np.zeros((1, 16), np.int32)
        cfg[0, 0] = is_high
        cfg[0, 1] = tok0 // 128
        cfg[0, 2] = ctok0 // 128
        cfg[0, 3] = bit * 8
        cfg[0, 4] = bit * 44
        cfg[0, 5] = bit * 8
        cfg[0, 6] = bit
        cfg[0, 7] = 1 - bit
        gsl = slice(bit * NI, (bit + 1) * NI)
        dsl = slice(bit * (D // 2), (bit + 1) * (D // 2))
        in_maps.append({
            "x_in": np.ascontiguousarray(x_own),
            "xc_in": xc,
            "cfg_in": cfg,
            "one_in": np.ones((1, 64), np.int32),
            "b2g_in": np.ascontiguousarray(gb[:, gsl]),
            "sg_in": np.ascontiguousarray(sg_p[:, gsl]),
            "b2d_in": np.ascontiguousarray(db[:, dsl]),
            "sd_in": np.ascontiguousarray(sd_p[:, dsl]),
            "wqk_in": wqk_cols, "wv_in": wv_cols,
            "bqk_in": bqk_t, "bv_in": bv_row,
            "cos_o": cos_own.astype(BF), "sin_o": sin_own.astype(BF),
            "cos_c": np.ascontiguousarray(cos[:, ctok0:ctok0 + T_CONE]).astype(BF),
            "sin_c": np.ascontiguousarray(sinA[:, ctok0:ctok0 + T_CONE]).astype(BF),
            "mask_in": maskT,
        })

    nc = build_fused()
    res = run_bass_kernel_spmd(nc, in_maps, core_ids=cores).results

    outT = np.empty((NB, D, SEQ), np.float32)
    for c in cores:
        b = c // 4
        is_high = (c % 4) // 2
        bit = c % 2
        if not is_high:
            tok0, To = bit * T_LOW, T_LOW
        else:
            tok0, To = S_SPLIT + bit * T_HIGH, T_HIGH
        outT[b][:, tok0:tok0 + To] = res[c]["x_out"][:, 0:To]
    un = np.empty((NB, D, SEQ), np.float32)
    un[:, pr_d, :] = outT
    return np.ascontiguousarray(un.transpose(0, 2, 1))


# version 14
# speedup vs baseline: 1.4688x; 1.4688x over previous
import numpy as np
import ml_dtypes
from contextlib import ExitStack

import concourse.bass as bass
import concourse.mybir as mybir
from concourse.bass_utils import run_bass_kernel_spmd

F32 = mybir.dt.float32
BF16 = mybir.dt.bfloat16
F16 = mybir.dt.float16
I32 = mybir.dt.int32
U16 = mybir.dt.uint16
U8 = mybir.dt.uint8
AF = mybir.ActivationFunctionType
ALU = mybir.AluOpType
BF = ml_dtypes.bfloat16

NC = 8
D = 2048
NI = 5632
NH = 16
HD = 128
SEQ = 4096
NB = 2
LN_EPS = 1e-5
RMS_EPS = 1e-6
SM_SCALE = float(1.0 / np.sqrt(128.0))

S_SPLIT = 2816
T_LOW = S_SPLIT // 2
T_HIGH = (SEQ - S_SPLIT) // 2
T_CONE = S_SPLIT // 2
KT_LOW = S_SPLIT // 128
KT_HIGH = SEQ // 128


def ttiles(T):
    out, t0 = [], 0
    while t0 < T:
        out.append((t0, min(512, T - t0)))
        t0 += 512
    return out


class BlkMgr:
    def __init__(self, nc):
        self.nc = nc
        self.cm = None

    def new(self, name):
        if self.cm is not None:
            self.cm.__exit__(None, None, None)
        self.cm = self.nc.Block(name)
        return self.cm.__enter__()

    def close(self):
        if self.cm is not None:
            self.cm.__exit__(None, None, None)
            self.cm = None


class Sem:
    def __init__(self, handle):
        self.h = handle
        self.v = 0

    def inc(self, amt=1):
        self.v += amt
        return self.v

    def clone(self):
        s = Sem(self.h)
        s.v = self.v
        return s


def chain(eng, sem, fns):
    for k, f in enumerate(fns):
        if k > 0:
            eng.wait_ge(sem.h, sem.v)
        f().then_inc(sem.h, 1)
        sem.inc(1)



def emit_ln(nc, BLK, sec, bufs, S, x_views, t0, tw, tag, xn=None, xoff=0, xbf=None):
    s_pe, s_dv, s_sc = S["pe"], S["dv"], S["sc"]
    xn = xn if xn is not None else bufs["xn"]
    xbf = xbf if xbf is not None else bufs["xbf"]
    rows = bufs["rows"]
    a_b, b_b, tmp = bufs["a_b"], bufs["b_b"], bufs["tmp"]
    psS1, psS2, psC, psD = bufs["psS1"], bufs["psS2"], bufs["psC"], bufs["psD"]
    ones_bf, ones_f32, cst = bufs["ones_bf"], bufs["ones_f32"], bufs["cst"]

    blk = BLK.new(f"ln_{tag}")

    @sec(blk, "vector")
    def _(vector, dv_now=s_dv.v):
        vector.wait_ge(s_dv.h, dv_now)
        for c in range(16):
            ins = vector.tensor_copy(out=xbf[c][:, 0:tw], in_=x_views[c][:, t0:t0 + tw])
            if c == 15:
                ins.then_inc(s_dv.h, 1)
    s_dv.inc(1)
    cast_done = s_dv.v

    @sec(blk, "tensor")
    def _(tensor, cast_done=cast_done):
        tensor.wait_ge(s_dv.h, cast_done)
        for c in range(16):
            ins = tensor.matmul(out=psS1[0:1, 0:tw], lhsT=ones_bf[:, 0:1], rhs=xbf[c][:, 0:tw],
                                start=(c == 0), stop=(c == 15))
            if c == 15:
                ins.then_inc(s_pe.h, 1)
    s_pe.inc(1)
    s1_done = s_pe.v

    @sec(blk, "scalar")
    def _(scalar, s1_done=s1_done, sc_now=s_sc.v):
        scalar.wait_ge(s_pe.h, s1_done)
        scalar.wait_ge(s_sc.h, sc_now)
        for c in range(16):
            ins = scalar.activation(out=xn[c][:, xoff:xoff + tw], in_=xbf[c][:, 0:tw],
                                    func=AF.Square)
            if c == 15:
                ins.then_inc(s_sc.h, 1)
    s_sc.inc(1)
    sq_done = s_sc.v

    @sec(blk, "tensor")
    def _(tensor, sq_done=sq_done):
        tensor.wait_ge(s_sc.h, sq_done)
        for c in range(16):
            ins = tensor.matmul(out=psS2[0:1, 0:tw], lhsT=ones_bf[:, 0:1],
                                rhs=xn[c][:, xoff:xoff + tw],
                                start=(c == 0), stop=(c == 15))
            if c == 15:
                ins.then_inc(s_pe.h, 1)
    s_pe.inc(1)
    stats_done = s_pe.v

    blk = BLK.new(f"ln_{tag}_r")

    @sec(blk, "vector")
    def _(vector, stats_done=stats_done):
        vector.wait_ge(s_pe.h, stats_done)
        chain(vector, s_dv, [
            lambda: vector.tensor_scalar(out=rows[0][:, 0:tw], in0=psS1[0:1, 0:tw],
                                         scalar1=1.0 / D, scalar2=None, op0=ALU.mult),
            lambda: vector.tensor_scalar(out=rows[1][:, 0:tw], in0=psS2[0:1, 0:tw],
                                         scalar1=1.0 / D, scalar2=None, op0=ALU.mult),
            lambda: vector.tensor_tensor(out=rows[2][:, 0:tw], in0=rows[0][:, 0:tw],
                                         in1=rows[0][:, 0:tw], op=ALU.mult),
            lambda: vector.tensor_tensor(out=rows[2][:, 0:tw], in0=rows[1][:, 0:tw],
                                         in1=rows[2][:, 0:tw], op=ALU.subtract),
        ])
    var_done = s_dv.v

    @sec(blk, "scalar")
    def _(scalar, var_done=var_done):
        scalar.wait_ge(s_dv.h, var_done)
        chain(scalar, s_sc, [
            lambda: scalar.activation(out=rows[1][:, 0:tw], in_=rows[2][:, 0:tw], func=AF.Ln,
                                      bias=cst[0:1, 0:1]),
            lambda: scalar.activation(out=rows[2][:, 0:tw], in_=rows[1][:, 0:tw], func=AF.Exp,
                                      scale=-0.5),
        ])
    rstd_done = s_sc.v

    @sec(blk, "vector")
    def _(vector, rstd_done=rstd_done, dv_now=s_dv.v):
        vector.wait_ge(s_sc.h, rstd_done)
        vector.wait_ge(s_dv.h, dv_now)
        vector.tensor_tensor(out=rows[1][:, 0:tw], in0=rows[0][:, 0:tw], in1=rows[2][:, 0:tw],
                             op=ALU.mult).then_inc(s_dv.h, 1)
    s_dv.inc(1)
    mr_done = s_dv.v

    @sec(blk, "tensor")
    def _(tensor, mr_done=mr_done):
        tensor.wait_ge(s_dv.h, mr_done)
        tensor.matmul(out=psC[:, 0:tw], lhsT=ones_f32[0:1, :], rhs=rows[2][:, 0:tw],
                      start=True, stop=True)
        tensor.matmul(out=psD[:, 0:tw], lhsT=ones_f32[0:1, :], rhs=rows[1][:, 0:tw],
                      start=True, stop=True).then_inc(s_pe.h, 1)
    s_pe.inc(1)
    bc_done = s_pe.v

    blk = BLK.new(f"ln_{tag}_a")

    @sec(blk, "vector")
    def _(vector, bc_done=bc_done):
        vector.wait_ge(s_pe.h, bc_done)
        fns = [lambda: vector.tensor_copy(out=a_b[:, 0:tw], in_=psC[:, 0:tw]),
               lambda: vector.tensor_copy(out=b_b[:, 0:tw], in_=psD[:, 0:tw])]
        for c in range(16):
            fns.append(lambda c=c: vector.tensor_tensor(
                out=tmp[:, 0:tw], in0=x_views[c][:, t0:t0 + tw], in1=a_b[:, 0:tw], op=ALU.mult))
            fns.append(lambda c=c: vector.tensor_tensor(
                out=xn[c][:, xoff:xoff + tw], in0=tmp[:, 0:tw], in1=b_b[:, 0:tw], op=ALU.subtract))
        chain(vector, s_dv, fns)
    return s_dv.v



def emit_ffn(nc, BLK, sec, bufs, S, T, wshg4, wshd4, tag,
             x_views=None, x_src=None, x_dst=None, attn_add=None):
    s_pe, s_dv, s_sc = S["pe"], S["dv"], S["sc"]
    s_st = S["st"]
    s_lx = S["lx"]
    s_wg = [S["wg0"], S["wg1"]]
    s_wd = [S["wd0"], S["wd1"]]
    s_as = [S["as0"], S["as1"]]
    s_sp = [S["sp0"], S["sp1"]]
    xn, rows = bufs["xn"], bufs["rows"]
    pbk = bufs["pbk"]
    ones_bf, ones_f32, cst = bufs["ones_bf"], bufs["ones_f32"], bufs["cst"]
    wguA, wdcA = bufs["wguA"], bufs["wdc"]
    h, y_bf, sgb = bufs["h"], bufs["y_bf"], bufs["sgb"]
    rb, tmp = bufs["rb"], bufs["tmp"]
    xt, atp, ash = bufs["xt"], bufs["atp"], bufs["ash"]

    last_gu = {0: 0, 1: 0}
    last_dn = {0: 0, 1: 0}
    gbank_free = {0: 0, 1: 0}
    ubank_free = {0: 0, 1: 0}
    dbank_free = {k: (0, 0) for k in (2, 3)}
    sgb_free = {0: 0, 1: 0}
    sqb_free = {0: 0, 1: 0}
    DB = [2, 3]
    CH = 768
    FST = 4
    DN = 7
    prev_st = s_st.v

    for ck, c0 in enumerate(range(0, T, CH)):
        cw = min(CH, T - c0)
        subs = ttiles(cw)
        nsub = len(subs)
        if x_src is not None:
            xv = [xt[c][:, 0:cw] for c in range(16)]
            blk = BLK.new(f"xl_{tag}{ck}")

            @sec(blk, "scalar")
            def _(scalar, ck=ck, c0=c0, cw=cw, prev_st=prev_st, dv_now=s_dv.v,
                  aa=attn_add):
                if ck == 0:
                    scalar.wait_ge(s_dv.h, dv_now)
                    if aa is not None:
                        scalar.wait_ge(S["bar"].h, aa["b3"])
                scalar.wait_ge(s_st.h, prev_st)
                for c in range(16):
                    scalar.dma_start(out=xt[c][:, 0:cw],
                                     in_=x_src[c * 128:(c + 1) * 128, c0:c0 + cw]
                                     ).then_inc(s_lx.h, 16)
            s_lx.inc(16 * 16)
            x_loaded = s_lx.v

            if attn_add is not None:
                pair = attn_add["pair"]
                atp_free = {0: 0, 1: 0}
                add_vals = []
                ash_vals = []
                blk = BLK.new(f"aa_{tag}{ck}")

                @sec(blk, "scalar")
                def _(scalar, c0=c0, cw=cw, pair=pair, base_dv=s_dv.v):
                    for c in range(16):
                        if c >= 2:
                            scalar.wait_ge(s_dv.h, base_dv + c - 1)

                        def load(base, c=c):
                            scalar.dma_start(out=atp[c & 1][:, 0:cw],
                                             in_=ash[c, :, base + c0:base + c0 + cw]
                                             ).then_inc(s_as[c & 1].h, 16)
                        bufs["bsel"](scalar, "scalar",
                                     lambda c=c: load(pair[0], c), lambda c=c: load(pair[1], c))
                for c in range(16):
                    s_as[c & 1].inc(16)
                    ash_vals.append(s_as[c & 1].v)

                @sec(blk, "vector")
                def _(vector, cw=cw, x_loaded=x_loaded, ash_vals=tuple(ash_vals),
                      dv_now=s_dv.v):
                    vector.wait_ge(s_dv.h, dv_now)
                    vector.wait_ge(s_lx.h, x_loaded)
                    for c in range(16):
                        vector.wait_ge(s_as[c & 1].h, ash_vals[c])
                        vector.tensor_tensor(out=xt[c][:, 0:cw], in0=xt[c][:, 0:cw],
                                             in1=atp[c & 1][:, 0:cw],
                                             op=ALU.add).then_inc(s_dv.h, 1)
                for c in range(16):
                    s_dv.inc(1)
            else:
                blk = BLK.new(f"xw_{tag}{ck}")

                @sec(blk, "vector")
                def _(vector, x_loaded=x_loaded, dv_now=s_dv.v):
                    vector.wait_ge(s_dv.h, dv_now)
                    vector.wait_ge(s_lx.h, x_loaded)
                    vector.engine_nop().then_inc(s_dv.h, 1)
                s_dv.inc(1)
        else:
            xv = [x_views[c][:, c0:c0 + cw] for c in range(16)]

        xn_ready = 0
        for si, (s0, sw) in enumerate(subs):
            xn_ready = emit_ln(nc, BLK, sec, bufs, S, xv, s0, sw,
                               f"{tag}c{ck}s{si}", xoff=s0)

        for i in range(44 if FST >= 2 else 0):
            par = i & 1
            blk = BLK.new(f"gu_{tag}{ck}_{i}")

            @sec(blk, "sync")
            def _(sync, i=i, par=par, lw=last_gu[par],
                  sp_now=(s_sp[0].v, s_sp[1].v)):
                if lw > 0:
                    sync.wait_ge(s_pe.h, lw)
                else:
                    if sp_now[0] > 0:
                        sync.wait_ge(s_sp[0].h, sp_now[0])
                    if sp_now[1] > 0:
                        sync.wait_ge(s_sp[1].h, sp_now[1])
                sync.dma_start(out=wguA[par][:],
                               in_=wshg4[:, i, :, :].rearrange("p a b -> p (a b)")
                               ).then_inc(s_wg[par].h, 16)
            s_wg[par].inc(16)
            gu_loaded = s_wg[par].v

            plan = []
            for si, (s0, sw) in enumerate(subs):
                bi = (i * nsub + si) & 1
                plan.append((si, s0, sw, bi, gbank_free[bi], ubank_free[bi]))
            gate_done, up_done = {}, {}
            for si in range(nsub):
                s_pe.inc(1)
                gate_done[si] = s_pe.v
                s_pe.inc(1)
                up_done[si] = s_pe.v

            @sec(blk, "tensor")
            def _(tensor, i=i, par=par, gu_loaded=gu_loaded, xn_ready=xn_ready,
                  plan=tuple(plan)):
                tensor.wait_ge(s_wg[par].h, gu_loaded)
                if i == 0:
                    tensor.wait_ge(s_dv.h, xn_ready)
                for (si, s0, sw, bi, gfree, ufree) in plan:
                    if gfree > 0:
                        tensor.wait_ge(s_sc.h, gfree)
                    for g in range(16):
                        ins = tensor.matmul(out=pbk[bi][:, 0:sw],
                                            lhsT=wguA[par][:, g * 128:(g + 1) * 128],
                                            rhs=xn[g][:, s0:s0 + sw],
                                            start=(g == 0), stop=(g == 15))
                        if g == 15:
                            ins.then_inc(s_pe.h, 1)
                    if ufree > 0:
                        tensor.wait_ge(s_dv.h, ufree)
                    for g in range(16):
                        ins = tensor.matmul(out=pbk[2 + bi][:, 0:sw],
                                            lhsT=wguA[par][:, 2048 + g * 128:2048 + (g + 1) * 128],
                                            rhs=xn[g][:, s0:s0 + sw],
                                            start=(g == 0), stop=(g == 15))
                        if g == 15:
                            ins.then_inc(s_pe.h, 1)
            last_gu[par] = up_done[nsub - 1]

            splan = [(si, s0, sw, (i * nsub + si) & 1, gate_done[si], sgb_free[si])
                     for si, (s0, sw) in enumerate(subs)]

            @sec(blk, "scalar")
            def _(scalar, splan=tuple(splan)):
                for (si, s0, sw, bi, gd, sfree) in splan:
                    scalar.wait_ge(s_pe.h, gd)
                    if sfree > 0:
                        scalar.wait_ge(s_dv.h, sfree)
                    scalar.activation(out=sgb[si][:, 0:sw], in_=pbk[bi][:, 0:sw],
                                      func=AF.Silu).then_inc(s_sc.h, 1)
            silu_done = {}
            for si in range(nsub):
                s_sc.inc(1)
                silu_done[si] = s_sc.v

            vplan = [(si, s0, sw, (i * nsub + si) & 1, silu_done[si], up_done[si])
                     for si, (s0, sw) in enumerate(subs)]

            @sec(blk, "vector")
            def _(vector, i=i, vplan=tuple(vplan), dv_now=s_dv.v):
                vector.wait_ge(s_dv.h, dv_now)
                for (si, s0, sw, bi, sd, ud) in vplan:
                    vector.wait_ge(s_sc.h, sd)
                    vector.wait_ge(s_pe.h, ud)
                    vector.tensor_tensor(out=h[i][:, s0:s0 + sw], in0=sgb[si][:, 0:sw],
                                         in1=pbk[2 + bi][:, 0:sw],
                                         op=ALU.mult).then_inc(s_dv.h, 1)
            base_dv = s_dv.v
            for si in range(nsub):
                s_dv.inc(1)
            for si in range(nsub):
                bi = (i * nsub + si) & 1
                gbank_free[bi] = silu_done[si]
                ubank_free[bi] = base_dv + si + 1
                sgb_free[si] = base_dv + si + 1

        h_all = s_dv.v

        stats_done = {}
        sq_vals = {}
        dmm_vals = {}
        for m in range(17 if FST >= 3 else 0):
            par = m & 1
            blk = BLK.new(f"dn_{tag}{ck}_{m}")
            if m < 16:
                @sec(blk, "gpsimd")
                def _(gpsimd, m=m, par=par, lw=last_dn[par]):
                    if lw > 0:
                        gpsimd.wait_ge(s_pe.h, lw)
                    gpsimd.dma_start(out=wdcA[par][:],
                                     in_=wshd4[:, m, :, :].rearrange("p a b -> p (a b)")
                                     ).then_inc(s_wd[par].h, 16)
                s_wd[par].inc(16)
                d_loaded = s_wd[par].v

                dplan = []
                for si, (s0, sw) in enumerate(subs):
                    bi = DB[(m * nsub + si) % 2]
                    dplan.append((si, s0, sw, bi) + dbank_free[bi])
                tplan = []
                if m > 0 and (DN & 2):
                    for si, (s0, sw) in enumerate(subs):
                        tplan.append((si, s0, sw, m - 1, sq_vals[(m - 1, si)]))

                @sec(blk, "tensor")
                def _(tensor, m=m, par=par, d_loaded=d_loaded, h_all=h_all,
                      dplan=tuple(dplan), tplan=tuple(tplan)):
                    for (si, s0, sw, mm, sqv) in tplan:
                        tensor.wait_ge(s_sc.h, sqv)
                        tensor.matmul(out=pbk[6 + si][0:1, 0:sw], lhsT=ones_bf[:, 0:1],
                                      rhs=sgb[si][:, 0:sw], start=(mm == 0),
                                      stop=(mm == 15))
                    tensor.wait_ge(s_wd[par].h, d_loaded)
                    if m == 0:
                        tensor.wait_ge(s_dv.h, h_all)
                    for (si, s0, sw, bi, fsc, fdv) in dplan:
                        if fsc > 0:
                            tensor.wait_ge(s_sc.h, fsc)
                        if fdv > 0:
                            tensor.wait_ge(s_dv.h, fdv)
                        for g in range(44):
                            ins = tensor.matmul(out=pbk[bi][:, 0:sw],
                                                lhsT=wdcA[par][:, g * 128:(g + 1) * 128],
                                                rhs=h[g][:, s0:s0 + sw],
                                                start=(g == 0), stop=(g == 43))
                            if g == 43:
                                ins.then_inc(s_pe.h, 1)
                for si in range(nsub):
                    s_pe.inc(1)
                    dmm_vals[(m, si)] = s_pe.v
                last_dn[par] = dmm_vals[(m, nsub - 1)]

                splan = [(si, s0, sw, DB[(m * nsub + si) % 2], dmm_vals[(m, si)])
                         for si, (s0, sw) in enumerate(subs)]

                if DN & 1:
                    @sec(blk, "scalar")
                    def _(scalar, splan=tuple(splan)):
                        for (si, s0, sw, bi, dv_) in splan:
                            scalar.wait_ge(s_pe.h, dv_)
                            scalar.activation(out=sgb[si][:, 0:sw], in_=pbk[bi][:, 0:sw],
                                              func=AF.Square).then_inc(s_sc.h, 1)
                    for si in range(nsub):
                        s_sc.inc(1)
                        sq_vals[(m, si)] = s_sc.v

                yplan = [(si, s0, sw, DB[(m * nsub + si) % 2], dmm_vals[(m, si)])
                         for si, (s0, sw) in enumerate(subs)]

                if DN & 4:
                    @sec(blk, "vector")
                    def _(vector, m=m, yplan=tuple(yplan), dv_now=s_dv.v,
                          sqv=tuple(sq_vals.get((m, si), 0) for si in range(nsub))):
                        vector.wait_ge(s_dv.h, dv_now)
                        for (si, s0, sw, bi, dv_) in yplan:
                            vector.wait_ge(s_pe.h, dv_)
                            if sqv[si] > 0:
                                vector.wait_ge(s_sc.h, sqv[si])
                            vector.tensor_copy(out=y_bf[m][:, s0:s0 + sw],
                                               in_=pbk[bi][:, 0:sw]).then_inc(s_dv.h, 1)
                    base_dv = s_dv.v
                    for si in range(nsub):
                        s_dv.inc(1)
                    for si in range(nsub):
                        bi = DB[(m * nsub + si) % 2]
                        dbank_free[bi] = (0, base_dv + si + 1)
            elif DN & 2:
                tplan = [(si, s0, sw, sq_vals[(15, si)]) for si, (s0, sw) in enumerate(subs)]

                @sec(blk, "tensor")
                def _(tensor, tplan=tuple(tplan)):
                    for (si, s0, sw, sqv) in tplan:
                        tensor.wait_ge(s_sc.h, sqv)
                        ins = tensor.matmul(out=pbk[6 + si][0:1, 0:sw], lhsT=ones_bf[:, 0:1],
                                            rhs=sgb[si][:, 0:sw], start=False, stop=True)
                        ins.then_inc(s_pe.h, 1)
                for si in range(nsub):
                    s_pe.inc(1)
                    stats_done[si] = s_pe.v

        if FST < 4:
            continue
        y_consumed = {(15, si): (sq_vals[(15, si)], dbank_free[DB[(15 * nsub + si) % 2]][1])
                      for si in range(nsub)}
        for si, (s0, sw) in enumerate(subs):
            blk = BLK.new(f"rms_{tag}{ck}_{si}")

            @sec(blk, "scalar")
            def _(scalar, si=si, sw=sw, sd=stats_done[si]):
                scalar.wait_ge(s_pe.h, sd)
                chain(scalar, s_sc, [
                    lambda: scalar.activation(out=rows[0][:, 0:sw], in_=pbk[6 + si][0:1, 0:sw],
                                              func=AF.Ln, scale=1.0 / D, bias=cst[0:1, 1:2]),
                    lambda: scalar.activation(out=rows[1][:, 0:sw], in_=rows[0][:, 0:sw],
                                              func=AF.Exp, scale=-0.5, bias=cst[0:1, 2:3]),
                ])
            rr_done = s_sc.v

            @sec(blk, "tensor")
            def _(tensor, si=si, sw=sw, rr_done=rr_done,
                  yc=y_consumed[(15, si)]):
                tensor.wait_ge(s_sc.h, rr_done)
                tensor.wait_ge(s_sc.h, yc[0])
                tensor.wait_ge(s_dv.h, yc[1])
                tensor.matmul(out=pbk[si][:, 0:sw], lhsT=ones_f32[0:1, :],
                              rhs=rows[1][:, 0:sw], start=True, stop=True).then_inc(s_pe.h, 1)
            s_pe.inc(1)
            rbc_done = s_pe.v

            @sec(blk, "vector")
            def _(vector, si=si, s0=s0, sw=sw, xv=tuple(xv), rbc_done=rbc_done, dv_now=s_dv.v):
                vector.wait_ge(s_pe.h, rbc_done)
                vector.wait_ge(s_dv.h, dv_now)
                fns = [lambda: vector.tensor_copy(out=rb[:, 0:sw], in_=pbk[si][:, 0:sw])]
                for c in range(16):
                    fns.append(lambda c=c: vector.tensor_copy(
                        out=tmp[:, 0:sw], in_=y_bf[c][:, s0:s0 + sw]))
                    fns.append(lambda c=c: vector.tensor_tensor(
                        out=tmp[:, 0:sw], in0=tmp[:, 0:sw], in1=rb[:, 0:sw], op=ALU.mult))
                    fns.append(lambda c=c: vector.tensor_tensor(
                        out=xv[c][:, s0:s0 + sw], in0=xv[c][:, s0:s0 + sw],
                        in1=tmp[:, 0:sw], op=ALU.add))
                chain(vector, s_dv, fns)

        if x_dst is not None:
            apply_done = s_dv.v
            blk = BLK.new(f"xs_{tag}{ck}")

            @sec(blk, "gpsimd")
            def _(gpsimd, c0=c0, cw=cw, apply_done=apply_done):
                gpsimd.wait_ge(s_dv.h, apply_done)
                for c in range(16):
                    gpsimd.dma_start(out=x_dst[c * 128:(c + 1) * 128, c0:c0 + cw],
                                     in_=xt[c][:, 0:cw]).then_inc(s_st.h, 16)
            s_st.inc(16 * 16)
            prev_st = s_st.v
    return s_dv.v



def emit_qkv(nc, BLK, sec, bufs, S, T, x_views, cos_dram, sin_dram, tab_base, tok_pair,
             tok_const, wqk, wv, bv_in, ksh, vsh, qsh, tag, with_q=True, x_src=None):
    s_pe, s_dv, s_sc, s_st = S["pe"], S["dv"], S["sc"], S["st"]
    s_lx, s_tab, s_wq = S["lx"], S["tab"], S["wq"]
    s_sp = [S["sp0"], S["sp1"]]
    xt = bufs["xt"]
    xn_q = bufs["xn_q"]
    tabcos, tabsin = bufs["tabc_q"], bufs["tabs_q"]
    pbk = bufs["pbk"]
    wcolf, wcol_bf = bufs["wcolf"], bufs["wcol_bf"]
    bqk_sb, bvb = bufs["bqk_sb"], bufs["bvb"]
    qb2, qc, t2 = bufs["qb2"], bufs["qc"], bufs["t2"]
    qk_bf, v_bf = bufs["qk_bf"], bufs["v_bf"]
    wvb = bufs["wvb"]
    tiles = ttiles(T)

    blk = BLK.new(f"qkv_{tag}_bv")

    @sec(blk, "gpsimd")
    def _(gpsimd, dv_now=S["dv"].v, pe_now=s_pe.v):
        gpsimd.wait_ge(S["dv"].h, dv_now)
        gpsimd.wait_ge(s_pe.h, pe_now)
        gpsimd.dma_start(out=bvb, in_=bv_in.broadcast_to([128, D])).then_inc(S["ld2"].h, 16)
    S["ld2"].inc(16)
    bv_ready = S["ld2"].v

    xn_ready = 0
    for tt, (t0, tw) in enumerate(tiles):
        if x_src is not None:
            xv_t = [xt[c][:, 0:tw] for c in range(16)]
            blk = BLK.new(f"qxl_{tag}{tt}")

            @sec(blk, "scalar")
            def _(scalar, t0=t0, tw=tw, tt=tt, dv_now=s_dv.v, st_now=s_st.v):
                scalar.wait_ge(s_dv.h, dv_now)
                if tt == 0:
                    scalar.wait_ge(s_st.h, st_now)
                for c in range(16):
                    scalar.dma_start(out=xt[c][:, 0:tw],
                                     in_=x_src[c * 128:(c + 1) * 128, t0:t0 + tw]
                                     ).then_inc(s_lx.h, 16)
            s_lx.inc(16 * 16)
            qx_loaded = s_lx.v
            blk = BLK.new(f"qxw_{tag}{tt}")

            @sec(blk, "vector")
            def _(vector, qx_loaded=qx_loaded, dv_now=s_dv.v):
                vector.wait_ge(s_dv.h, dv_now)
                vector.wait_ge(s_lx.h, qx_loaded)
                vector.engine_nop().then_inc(s_dv.h, 1)
            s_dv.inc(1)
            xn_ready = emit_ln(nc, BLK, sec, bufs, S, xv_t, 0, tw, f"{tag}q{tt}",
                               xn=xn_q, xoff=t0, xbf=bufs["xbf_q"])
        else:
            xn_ready = emit_ln(nc, BLK, sec, bufs, S, x_views, t0, tw, f"{tag}q{tt}",
                               xn=xn_q, xoff=t0, xbf=bufs["xbf_q"])

    blk = BLK.new(f"qkv_{tag}_tab")

    @sec(blk, "sync")
    def _(sync, T=T, dv_now=S["dv"].v, pe_now=s_pe.v):
        sync.wait_ge(S["dv"].h, dv_now)
        sync.wait_ge(s_pe.h, pe_now)
        sync.dma_start(out=tabcos[:, 0:T],
                       in_=cos_dram[:, tab_base:tab_base + T]).then_inc(s_tab.h, 16)
        sync.dma_start(out=tabsin[:, 0:T],
                       in_=sin_dram[:, tab_base:tab_base + T]).then_inc(s_tab.h, 16)
    s_tab.inc(32)
    tab_ready = s_tab.v

    js = [j for j in range(32) if with_q or (j % 2 == 1)]
    bufs["rope_prev"] = {0: 0, 1: 0}
    cast_done = {0: 0, 1: 0}
    load_done = {0: 0, 1: 0}
    mmref = {0: 0, 1: 0}
    loadref = {0: 0, 1: 0}
    cnt = 0
    for ji, j in enumerate(js):
        par = ji & 1
        h_idx = j // 2
        is_q = (j % 2) == 0
        blk = BLK.new(f"qk_{tag}_{j}")

        @sec(blk, "gpsimd")
        def _(gpsimd, j=j, par=par, lr=loadref[par], pe_now=s_pe.v):
            if lr > 0:
                gpsimd.wait_ge(s_dv.h, lr)
            else:
                gpsimd.wait_ge(s_pe.h, pe_now)
            gpsimd.dma_start(out=wcolf[par][:], in_=wqk[:, j, :]).then_inc(s_wq.h, 16)
        s_wq.inc(16)
        load_done[par] = s_wq.v

        @sec(blk, "vector")
        def _(vector, par=par, ld=load_done[par], mr=mmref[par], xn_ready=xn_ready,
              ji=ji):
            vector.wait_ge(s_wq.h, ld)
            if mr > 0:
                vector.wait_ge(s_pe.h, mr)
            if ji == 0:
                vector.wait_ge(s_dv.h, xn_ready)
            vector.tensor_copy(out=wcol_bf[par][:], in_=wcolf[par][:]).then_inc(s_dv.h, 1)
        s_dv.inc(1)
        cast_done[par] = s_dv.v
        loadref[par] = cast_done[par]

        for tt, (t0, tw) in enumerate(tiles):
            op = cnt & 1
            ps = pbk[op]
            blk = BLK.new(f"qk_{tag}_{j}_t{tt}")

            @sec(blk, "tensor")
            def _(tensor, par=par, ps=ps, t0=t0, tw=tw, cd=cast_done[par], sc_now=s_sc.v):
                tensor.wait_ge(s_dv.h, cd)
                tensor.wait_ge(s_sc.h, sc_now)
                for g in range(16):
                    ins = tensor.matmul(out=ps[:, 0:tw],
                                        lhsT=wcol_bf[par][:, g * 128:(g + 1) * 128],
                                        rhs=xn_q[g][:, t0:t0 + tw],
                                        start=(g == 0), stop=(g == 15))
                    if g == 15:
                        ins.then_inc(s_pe.h, 1)
            s_pe.inc(1)
            mm_done = s_pe.v
            mmref[par] = mm_done

            @sec(blk, "scalar")
            def _(scalar, j=j, ps=ps, tw=tw, mm_done=mm_done, op=op,
                  rope_prev=bufs["rope_prev"][cnt & 1], sc_now=s_sc.v):
                scalar.wait_ge(s_pe.h, mm_done)
                scalar.wait_ge(S["ld"].h, bufs["bqk_ld"])
                if rope_prev > 0:
                    scalar.wait_ge(s_dv.h, rope_prev)
                scalar.wait_ge(s_sc.h, sc_now)
                scalar.activation(out=qb2[op][:, 0:tw], in_=ps[:, 0:tw], func=AF.Identity,
                                  bias=bqk_sb[:, j:j + 1]).then_inc(s_sc.h, 1)
            s_sc.inc(1)
            qb_done = s_sc.v

            @sec(blk, "vector")
            def _(vector, t0=t0, tw=tw, qb_done=qb_done, op=op, lst=s_sp[op].v,
                  tab_ready=tab_ready, qb=qb2[cnt & 1]):
                vector.wait_ge(s_sc.h, qb_done)
                if lst > 0:
                    vector.wait_ge(s_sp[op].h, lst)
                vector.wait_ge(s_tab.h, tab_ready)
                chain(vector, s_dv, [
                    lambda: vector.tensor_copy(out=t2[0:64, 0:tw], in_=qb[64:128, 0:tw]),
                    lambda: vector.tensor_copy(out=t2[64:128, 0:tw], in_=qb[0:64, 0:tw]),
                    lambda: vector.tensor_tensor(out=qc[:, 0:tw], in0=qb[:, 0:tw],
                                                 in1=tabcos[:, t0:t0 + tw], op=ALU.mult),
                    lambda: vector.tensor_tensor(out=t2[:, 0:tw], in0=t2[:, 0:tw],
                                                 in1=tabsin[:, t0:t0 + tw], op=ALU.mult),
                    lambda: vector.tensor_tensor(out=qk_bf[op][:, 0:tw], in0=qc[:, 0:tw],
                                                 in1=t2[:, 0:tw], op=ALU.add),
                ])
            rope_done = s_dv.v
            bufs["rope_prev"][cnt & 1] = rope_done

            @sec(blk, "gpsimd")
            def _(gpsimd, h_idx=h_idx, is_q=is_q, t0=t0, tw=tw, rope_done=rope_done, op=op):
                gpsimd.wait_ge(s_dv.h, rope_done)
                dst = qsh if is_q else ksh

                def store(base):
                    c0 = base + tok_const + t0
                    gpsimd.dma_start(out=dst[h_idx, :, c0:c0 + tw],
                                     in_=qk_bf[op][:, 0:tw]).then_inc(s_sp[op].h, 16)
                bufs["bsel"](gpsimd, "gpsimd", lambda: store(tok_pair[0]), lambda: store(tok_pair[1]))
            s_sp[op].inc(16)
            cnt += 1

    wv_mm = 0
    for fc in range(8):
        for hf in range(2):
            par = hf
            blk = BLK.new(f"v_{tag}_{fc}_{hf}")

            @sec(blk, "gpsimd")
            def _(gpsimd, fc=fc, hf=hf, par=par, lr=loadref[par]):
                if lr > 0:
                    gpsimd.wait_ge(s_dv.h, lr)
                gpsimd.dma_start(out=wcolf[par][:],
                                 in_=wv[:, fc, hf * 2048:(hf + 1) * 2048]).then_inc(s_wq.h, 16)
            s_wq.inc(16)
            ld = s_wq.v

            @sec(blk, "vector")
            def _(vector, hf=hf, par=par, ld=ld, wv_mm=wv_mm):
                vector.wait_ge(s_wq.h, ld)
                if wv_mm > 0:
                    vector.wait_ge(s_pe.h, wv_mm)
                vector.tensor_copy(out=wvb[:, hf * 2048:(hf + 1) * 2048],
                                   in_=wcolf[par][:]).then_inc(s_dv.h, 1)
            s_dv.inc(1)
            loadref[par] = s_dv.v
        v_cast = s_dv.v

        for tt, (t0, tw) in enumerate(tiles):
            for t in range((tw + 127) // 128):
                sw = min(128, tw - t * 128)
                op = cnt & 1
                ps = pbk[op]
                blk = BLK.new(f"v_{tag}_{fc}_{tt}_{t}")

                @sec(blk, "tensor")
                def _(tensor, t0=t0, t=t, sw=sw, ps=ps, v_cast=v_cast, dv_now=s_dv.v,
                      sc_now=s_sc.v):
                    tensor.wait_ge(s_dv.h, v_cast)
                    tensor.wait_ge(s_dv.h, dv_now)
                    tensor.wait_ge(s_sc.h, sc_now)
                    for g in range(16):
                        ins = tensor.matmul(out=ps[0:sw, 0:256],
                                            lhsT=xn_q[g][:, t0 + t * 128:t0 + t * 128 + sw],
                                            rhs=wvb[:, g * 256:(g + 1) * 256],
                                            start=(g == 0), stop=(g == 15))
                        if g == 15:
                            ins.then_inc(s_pe.h, 1)
                s_pe.inc(1)
                vmm_done = s_pe.v
                wv_mm = vmm_done

                @sec(blk, "vector")
                def _(vector, fc=fc, sw=sw, ps=ps, vmm_done=vmm_done, op=op,
                      lst=s_sp[op].v, dv_now=s_dv.v, bv_ready=bv_ready):
                    vector.wait_ge(s_pe.h, vmm_done)
                    if lst > 0:
                        vector.wait_ge(s_sp[op].h, lst)
                    vector.wait_ge(s_dv.h, dv_now)
                    vector.wait_ge(S["ld2"].h, bv_ready)
                    vector.tensor_tensor(out=v_bf[op & 1][0:sw, 0:256], in0=ps[0:sw, 0:256],
                                         in1=bvb[0:sw, fc * 256:(fc + 1) * 256],
                                         op=ALU.add).then_inc(s_dv.h, 1)
                s_dv.inc(1)
                vevac = s_dv.v

                @sec(blk, "gpsimd")
                def _(gpsimd, fc=fc, t0=t0, t=t, sw=sw, vevac=vevac, op=op):
                    gpsimd.wait_ge(s_dv.h, vevac)

                    def store(base):
                        jj = (base + tok_const + t0) // 128 + t
                        for s in range(2):
                            h_i = fc * 2 + s
                            gpsimd.dma_start(out=vsh[h_i, 0:sw, jj, :],
                                             in_=v_bf[op & 1][0:sw, s * 128:(s + 1) * 128]
                                             ).then_inc(s_sp[op].h, 16)
                    bufs["bsel"](gpsimd, "gpsimd", lambda: store(tok_pair[0]), lambda: store(tok_pair[1]))
                s_sp[op].inc(32)
                cnt += 1
    return s_sp[0].v



def emit_attn(nc, BLK, sec, bufs, S, qt0, qt1, n_kt, tag):
    s_pe, s_dv, s_sc, s_st = S["pe"], S["dv"], S["sc"], S["st"]
    s_kv, s_one = S["kv"], S["one"]
    s_sp = [S["sp0"], S["sp1"]]
    ksh, vsh, qsh, ash = bufs["ksh"], bufs["vsh"], bufs["qsh"], bufs["ash"]
    k_sb, v_sb, q_sb = bufs["k_sb"], bufs["v_sb"], bufs["q_sb"]
    pt, msk = bufs["pt"], bufs["msk"]
    ones_bf, ones_f32 = bufs["ones_bf"], bufs["ones_f32"]
    r_sb, rb_sb, o_sb = bufs["r_sb"], bufs["rb_sb"], bufs["o_sb"]
    ps_st, ps_pv, ps_l, ps_rb = bufs["ps_st"], bufs["ps_pv"], bufs["ps_l"], bufs["ps_rb"]
    qlen = (qt1 - qt0) * 128

    blocks = []
    t = qt0
    while t < qt1:
        blocks.append((t, min(t + 4, qt1)))
        t = min(t + 4, qt1)

    blk = BLK.new(f"at_{tag}_msk")

    @sec(blk, "sync")
    def _(sync):
        sync.dma_start(out=msk, in_=bufs["mask_in"]).then_inc(s_one.h, 16)
    s_one.inc(16)
    msk_ld = s_one.v

    last_exp = {0: 0, 1: 0}
    last_lpv = {0: 0, 1: 0}
    last_ost = {0: 0, 1: 0}
    prev_norm_dv = 0
    for hl in range(8):
        blk = BLK.new(f"at_{tag}{hl}_ld")

        @sec(blk, "sync")
        def _(sync, hl=hl, pe_now=s_pe.v):
            if hl > 0:
                sync.wait_ge(s_pe.h, pe_now)

            def load(hb):
                hh = hb + hl
                sync.dma_start(out=k_sb[:, 0:n_kt * 128],
                               in_=ksh[hh, :, 0:n_kt * 128]).then_inc(s_kv.h, 16)
                sync.dma_start(out=v_sb[:, 0:n_kt * 128],
                               in_=vsh[hh, :, 0:n_kt, :].rearrange("p j f -> p (j f)")
                               ).then_inc(s_kv.h, 16)
                sync.dma_start(out=q_sb[:, 0:qlen],
                               in_=qsh[hh, :, qt0 * 128:qt1 * 128]).then_inc(s_kv.h, 16)
            bufs["bsel"](sync, "sync", lambda: load(0), lambda: load(8))
        s_kv.inc(48)
        kv_loaded = s_kv.v

        for (bt0, bt1) in blocks:
            qw = (bt1 - bt0) * 128
            boff = (bt0 - qt0) * 128
            nkt_b = bt1
            for kt in range(nkt_b):
                par = kt % 2
                scol0 = max(0, (kt - bt0)) * 128
                diag = kt >= bt0
                blk = BLK.new(f"at_{tag}{hl}_{bt0}_{kt}")

                @sec(blk, "tensor")
                def _(tensor, kt=kt, par=par, scol0=scol0, boff=boff, qw=qw,
                      kv_loaded=kv_loaded, le=last_exp[par], first=(kt == 0 and bt0 == qt0)):
                    if first:
                        tensor.wait_ge(s_kv.h, kv_loaded)
                    if le > 0:
                        tensor.wait_ge(s_sc.h, le)
                    tensor.matmul(out=ps_st[par][:, scol0:qw],
                                  lhsT=k_sb[:, kt * 128:(kt + 1) * 128],
                                  rhs=q_sb[:, boff + scol0:boff + qw],
                                  start=True, stop=True).then_inc(s_pe.h, 1)
                s_pe.inc(1)
                st_done = s_pe.v

                @sec(blk, "scalar")
                def _(scalar, par=par, scol0=scol0, qw=qw, st_done=st_done, lp=last_lpv[par]):
                    scalar.wait_ge(s_pe.h, st_done)
                    if lp > 0:
                        scalar.wait_ge(s_pe.h, lp)
                    scalar.activation(out=pt[par][:, scol0:qw], in_=ps_st[par][:, scol0:qw],
                                      func=AF.Exp, scale=SM_SCALE).then_inc(s_sc.h, 1)
                s_sc.inc(1)
                last_exp[par] = s_sc.v
                exp_done = s_sc.v

                if diag:
                    @sec(blk, "vector")
                    def _(vector, par=par, scol0=scol0, exp_done=exp_done, dv_now=s_dv.v,
                          msk_ld=msk_ld):
                        vector.wait_ge(s_sc.h, exp_done)
                        vector.wait_ge(s_dv.h, dv_now)
                        vector.wait_ge(s_one.h, msk_ld)
                        vector.tensor_tensor(out=pt[par][:, scol0:scol0 + 128],
                                             in0=pt[par][:, scol0:scol0 + 128],
                                             in1=msk[:], op=ALU.mult).then_inc(s_dv.h, 1)
                    s_dv.inc(1)
                    rsem, rval = s_dv, s_dv.v
                else:
                    rsem, rval = s_sc, exp_done

                @sec(blk, "tensor")
                def _(tensor, par=par, scol0=scol0, qw=qw, kt=kt, nkt_b=nkt_b,
                      rsem=rsem, rval=rval, pnorm=prev_norm_dv):
                    tensor.wait_ge(rsem.h, rval)
                    if kt == 0 and pnorm > 0:
                        tensor.wait_ge(s_dv.h, pnorm)
                    tensor.matmul(out=ps_l[0:1, scol0:qw], lhsT=ones_bf[:, 0:1],
                                  rhs=pt[par][:, scol0:qw], start=(kt == 0), stop=(kt == nkt_b - 1))
                    tensor.matmul(out=ps_pv[:, scol0:qw], lhsT=v_sb[:, kt * 128:(kt + 1) * 128],
                                  rhs=pt[par][:, scol0:qw], start=(kt == 0),
                                  stop=(kt == nkt_b - 1)).then_inc(s_pe.h, 1)
                s_pe.inc(1)
                last_lpv[par] = s_pe.v

            pv_done = s_pe.v
            op = (bt0 // 4) % 2
            blk = BLK.new(f"at_{tag}{hl}_{bt0}_n")

            @sec(blk, "vector")
            def _(vector, pv_done=pv_done, qw=qw, dv_now=s_dv.v):
                vector.wait_ge(s_pe.h, pv_done)
                vector.wait_ge(s_dv.h, dv_now)
                vector.reciprocal(out=r_sb[0:1, 0:qw], in_=ps_l[0:1, 0:qw]).then_inc(s_dv.h, 1)
            s_dv.inc(1)
            r_done = s_dv.v

            @sec(blk, "tensor")
            def _(tensor, r_done=r_done, qw=qw):
                tensor.wait_ge(s_dv.h, r_done)
                tensor.matmul(out=ps_rb[:, 0:qw], lhsT=ones_f32[0:1, :], rhs=r_sb[0:1, 0:qw],
                              start=True, stop=True).then_inc(s_pe.h, 1)
            s_pe.inc(1)
            rb_done = s_pe.v

            @sec(blk, "vector")
            def _(vector, op=op, rb_done=rb_done, qw=qw, lst=s_sp[op].v):
                vector.wait_ge(s_pe.h, rb_done)
                if lst > 0:
                    vector.wait_ge(s_sp[op].h, lst)
                chain(vector, s_dv, [
                    lambda: vector.tensor_copy(out=rb_sb[:, 0:qw], in_=ps_rb[:, 0:qw]),
                    lambda: vector.tensor_tensor(out=o_sb[op][:, 0:qw], in0=ps_pv[:, 0:qw],
                                                 in1=rb_sb[:, 0:qw], op=ALU.mult),
                ])
            o_done = s_dv.v
            prev_norm_dv = o_done

            @sec(blk, "gpsimd")
            def _(gpsimd, hl=hl, bt0=bt0, op=op, qw=qw, o_done=o_done):
                gpsimd.wait_ge(s_dv.h, o_done)

                def store(hb):
                    gpsimd.dma_start(out=ash[hb + hl, :, bt0 * 128:bt0 * 128 + qw],
                                     in_=o_sb[op][:, 0:qw]).then_inc(s_sp[op].h, 16)
                bufs["bsel"](gpsimd, "gpsimd", lambda: store(0), lambda: store(8))
            s_sp[op].inc(16)
            last_ost[op] = s_sp[op].v
    return s_st.v



def emit_barrier(nc, BLK, sec, bufs, S, bar_idx, tag, wait_sems):
    s_bar, s_poll = S["bar"], S["poll"]
    fsh, one_i, flag_sb = bufs["fsh"], bufs["one_i"], bufs["flag_sb"]
    regs = bufs["regs"]
    blk = BLK.new(f"bar_{tag}")
    waits = [(s.h, s.v) for s in wait_sems]

    @sec(blk, "gpsimd")
    def _(gpsimd, waits=waits, bar_idx=bar_idx):
        for hh, v in waits:
            gpsimd.wait_ge(hh, v)
        gpsimd.wait_ge(S["one"].h, bufs["one_ld"])

        def store(row):
            gpsimd.dma_start(out=fsh[row:row + 1, 0:64], in_=one_i).then_inc(S["st"].h, 16)
        bufs["bsel"](gpsimd, "gpsimd",
                     lambda: store(2 * bar_idx), lambda: store(2 * bar_idx + 1))
    S["st"].inc(16)

    import os
    _NOPOLL = os.environ.get("MK_NOPOLL", "0") == "1"

    @sec(blk, "sync")
    def _(sync, bar_idx=bar_idx, waits=waits):
        for hh, v in waits:
            sync.wait_ge(hh, v)
        if _NOPOLL:
            sync.sem_inc(s_bar.h, 1)
            return
        r_cond = regs["sync"]["r_cond"]
        r_exp = regs["sync"]["r_exp"]
        r_t = regs["sync"]["r_t"]
        sync.reg_mov(r_cond, 1)
        with sync.While(r_cond):
            def poll(row):
                sync.dma_start(out=flag_sb[0:1, 0:1],
                               in_=fsh[row:row + 1, 0:1]).then_inc(s_poll.h, 16)
            bufs["bsel"](sync, "sync",
                         lambda: poll(2 * bar_idx + 1), lambda: poll(2 * bar_idx))
            sync.reg_add(r_exp, r_exp, 16)
            sync.wait_ge(s_poll.h, r_exp)
            sync.reg_load(r_t, flag_sb[0:1, 0:1])
            sync.reg_sub(r_cond, 1, r_t)
        sync.sem_inc(s_bar.h, 1)
    s_bar.inc(1)
    return s_bar.v



def emit_dequant(nc, BLK, sec, bufs, S, b2g, sg, b2d, sd, wshg, wshd):
    s_dv, s_std = S["dv"], S["std"]
    s_dq = [S["dq0"], S["dq1"]]
    byt, scb = bufs["dq_byt"], bufs["dq_scb"]
    nr8, n15, rb1, rb2, rm, ru, rv, bits, wt = (
        bufs["dq_nr8"], bufs["dq_n15"], bufs["dq_rb1"], bufs["dq_rb2"], bufs["dq_rm"],
        bufs["dq_ru"], bufs["dq_rv"], bufs["dq_bits"], bufs["dq_wt"])
    regs = bufs["regs"]

    tiles = [("g", g, q, 1408, 11) for g in range(16) for q in range(4)]
    tiles += [("d", g, 0, 1024, 8) for g in range(44)]
    for i, (which, g, q, w, nct) in enumerate(tiles):
        p = i % 2
        blk = BLK.new(f"dq{i}")

        @sec(blk, "sync")
        def _(sync, which=which, g=g, q=q, w=w, p=p, i=i, dec_prev=s_dv.v):
            if i >= 2:
                sync.wait_ge(s_dv.h, dec_prev)
            src = b2g if which == "g" else b2d
            sync.dma_start(out=byt[p][:, 0:w],
                           in_=src[g * 128:(g + 1) * 128, q * w:(q + 1) * w]).then_inc(s_dq[p].h, 16)

        @sec(blk, "scalar")
        def _(scalar, which=which, g=g, q=q, w=w, p=p, i=i, dec_prev=s_dv.v):
            if i >= 2:
                scalar.wait_ge(s_dv.h, dec_prev)
            ssc = sg if which == "g" else sd
            scalar.dma_start(out=scb[p][:, 0:w],
                             in_=ssc[g:g + 1, q * w:(q + 1) * w].broadcast_to([128, w])).then_inc(s_dq[p].h, 16)
        s_dq[p].inc(32)
        loaded = s_dq[p].v

        @sec(blk, "vector")
        def _(vector, w=w, p=p, i=i, loaded=loaded, st_prev=s_std.v):
            vector.wait_ge(s_dq[p].h, loaded)
            if i >= 2:
                vector.wait_ge(s_std.h, st_prev)
            B = byt[p][:, 0:w]
            chain(vector, s_dv, [
                lambda: vector.tensor_scalar(out=nr8[0:64, 0:w], in0=B[0:64], scalar1=15,
                                             scalar2=None, op0=ALU.bitwise_and),
                lambda: vector.tensor_scalar(out=nr8[64:128, 0:w], in0=B[64:128], scalar1=4,
                                             scalar2=None, op0=ALU.logical_shift_right),
                lambda: vector.tensor_copy(out=n15[:, 0:w], in_=nr8[:, 0:w]),
                lambda: vector.tensor_scalar(out=rb1[:, 0:w], in0=n15[:, 0:w], scalar1=8, scalar2=12,
                                             op0=ALU.bitwise_and, op1=ALU.logical_shift_left),
                lambda: vector.tensor_scalar(out=rb2[:, 0:w], in0=n15[:, 0:w], scalar1=7, scalar2=6,
                                             op0=ALU.bitwise_and, op1=ALU.logical_shift_left),
                lambda: vector.tensor_scalar(out=rm[:, 0:w], in0=n15[:, 0:w], scalar1=6,
                                             scalar2=None, op0=ALU.bitwise_and),
                lambda: vector.tensor_scalar(out=ru[:, 0:w], in0=rm[:, 0:w], scalar1=0, scalar2=16128,
                                             op0=ALU.is_equal, op1=ALU.mult),
                lambda: vector.tensor_scalar(out=rm[:, 0:w], in0=n15[:, 0:w], scalar1=7,
                                             scalar2=None, op0=ALU.bitwise_and),
                lambda: vector.tensor_scalar(out=rv[:, 0:w], in0=rm[:, 0:w], scalar1=1, scalar2=16064,
                                             op0=ALU.is_equal, op1=ALU.mult),
                lambda: vector.scalar_tensor_tensor(out=rb1[:, 0:w], in0=rb1[:, 0:w], scalar=16128.0,
                                                    in1=rb2[:, 0:w], op0=ALU.add, op1=ALU.add),
                lambda: vector.tensor_tensor(out=rb1[:, 0:w], in0=rb1[:, 0:w], in1=ru[:, 0:w],
                                             op=ALU.subtract),
                lambda: vector.tensor_tensor(out=bits[:, 0:w], in0=rb1[:, 0:w], in1=rv[:, 0:w],
                                             op=ALU.add),
                lambda: vector.tensor_tensor(out=wt[p][:, 0:w], in0=bits[:, 0:w].bitcast(BF16),
                                             in1=scb[p][:, 0:w], op=ALU.mult),
            ])
        dec = s_dv.v

        st_eng = "scalar" if (i & 1) else "sync"

        @sec(blk, st_eng)
        def _(eng, which=which, g=g, q=q, w=w, nct=nct, p=p, dec=dec, st_eng=st_eng):
            eng.wait_ge(s_dv.h, dec)
            src = wt[p][:, 0:w].rearrange("p (a b) -> p a b", a=nct)

            def store(half):
                if which == "g":
                    dst = wshg[:, q * nct:(q + 1) * nct, half, g * 128:(g + 1) * 128]
                else:
                    dst = wshd[:, half * 8:(half + 1) * 8, g, :]
                eng.dma_start(out=dst, in_=src).then_inc(s_std.h, 16)
            bufs["bsel"](eng, st_eng, lambda: store(0), lambda: store(1))
        s_std.inc(16)
    return s_std.v



def build_fused():
    nc = bass.Bass("TRN2", target_bir_lowering=False)
    x_in = nc.dram_tensor("x_in", [D, T_LOW], F32, kind="ExternalInput").ap()
    xc_in = nc.dram_tensor("xc_in", [D, T_CONE], F32, kind="ExternalInput").ap()
    cfg_in = nc.dram_tensor("cfg_in", [1, 16], I32, kind="ExternalInput").ap()
    one_in = nc.dram_tensor("one_in", [1, 64], I32, kind="ExternalInput").ap()
    b2g_in = nc.dram_tensor("b2g_in", [D, NI], U8, kind="ExternalInput").ap()
    sg_in = nc.dram_tensor("sg_in", [16, NI], F16, kind="ExternalInput").ap()
    b2d_in = nc.dram_tensor("b2d_in", [NI, D // 2], U8, kind="ExternalInput").ap()
    sd_in = nc.dram_tensor("sd_in", [44, D // 2], F16, kind="ExternalInput").ap()
    wqk_in = nc.dram_tensor("wqk_in", [128, 32, 2048], F32, kind="ExternalInput").ap()
    wv_in = nc.dram_tensor("wv_in", [128, 8, 4096], F32, kind="ExternalInput").ap()
    bqk_in = nc.dram_tensor("bqk_in", [128, 32], F32, kind="ExternalInput").ap()
    bv_in = nc.dram_tensor("bv_in", [1, D], F32, kind="ExternalInput").ap()
    cos_o = nc.dram_tensor("cos_o", [128, T_LOW], BF16, kind="ExternalInput").ap()
    sin_o = nc.dram_tensor("sin_o", [128, T_LOW], BF16, kind="ExternalInput").ap()
    cos_c = nc.dram_tensor("cos_c", [128, T_CONE], BF16, kind="ExternalInput").ap()
    sin_c = nc.dram_tensor("sin_c", [128, T_CONE], BF16, kind="ExternalInput").ap()
    mask_in = nc.dram_tensor("mask_in", [128, 128], BF16, kind="ExternalInput").ap()
    x_out = nc.dram_tensor("x_out", [D, T_LOW], F32, kind="ExternalOutput").ap()

    wshg = nc.dram_tensor("wshg", [128, 44, 2, 2048], BF16, kind="Internal", addr_space="Shared").ap()
    wshd = nc.dram_tensor("wshd", [128, 16, 44, 128], BF16, kind="Internal", addr_space="Shared").ap()
    ksh = nc.dram_tensor("ksh", [NH, 128, SEQ], BF16, kind="Internal", addr_space="Shared").ap()
    qsh = nc.dram_tensor("qsh", [NH, 128, SEQ], BF16, kind="Internal", addr_space="Shared").ap()
    vsh = nc.dram_tensor("vsh", [NH, 128, SEQ // 128, 128], BF16, kind="Internal", addr_space="Shared").ap()
    ash = nc.dram_tensor("ash", [NH, 128, SEQ], F32, kind="Internal", addr_space="Shared").ap()
    fsh = nc.dram_tensor("fsh", [8, 256], I32, kind="Internal", addr_space="Shared").ap()
    x1d = nc.dram_tensor("x1d", [D, T_LOW], F32, kind="Internal").ap()

    with ExitStack() as ctx:
        E = ctx.enter_context
        BLK = BlkMgr(nc)
        bufs = {}

        xtA = E(nc.sbuf_tensor("xtA", [128, 16 * 768], F32))
        bufs["xt"] = [xtA[:, c * 768:(c + 1) * 768] for c in range(16)]
        atpA = E(nc.sbuf_tensor("atpA", [128, 2 * 768], F32))
        bufs["atp"] = [atpA[:, 0:768], atpA[:, 768:1536]]
        xnA = E(nc.sbuf_tensor("xnA", [128, 16 * 768], BF16))
        bufs["xn"] = [xnA[:, c * 768:(c + 1) * 768] for c in range(16)]
        bufs["y_bf"] = [xnA[:, c * 768:(c + 1) * 768] for c in range(16)]
        hA = E(nc.sbuf_tensor("hA", [128, 44 * 768], BF16))
        bufs["h"] = [hA[:, i * 768:(i + 1) * 768] for i in range(44)]
        bufs["xbf"] = [hA[:, c * 768:c * 768 + 512] for c in range(16)]
        bufs["k_sb"] = hA[:, 0:4096]
        bufs["v_sb"] = hA[:, 4096:8192]
        bufs["q_sb"] = hA[:, 8192:8192 + 2816]
        bufs["pt"] = [hA[:, 11008 + i * 512:11008 + (i + 1) * 512] for i in range(2)]
        dqA = hA
        bufs["dq_nr8"] = dqA[:, 0:704].bitcast(U8)
        bufs["dq_byt"] = [dqA[:, 704 + i * 704:704 + (i + 1) * 704].bitcast(U8) for i in range(2)]
        bufs["dq_scb"] = [dqA[:, 2112 + i * 1408:2112 + (i + 1) * 1408].bitcast(F16) for i in range(2)]
        off = 4928
        for nm in ("dq_n15", "dq_rb1", "dq_rb2", "dq_rm", "dq_ru", "dq_rv", "dq_bits"):
            bufs[nm] = dqA[:, off:off + 1408].bitcast(U16)
            off += 1408
        bufs["dq_wt"] = [dqA[:, off + i * 1408:off + (i + 1) * 1408] for i in range(2)]

        wguA = [E(nc.sbuf_tensor(f"wgu{i}", [128, 4096], BF16)) for i in range(2)]
        bufs["wguA"] = wguA
        wdcA = [E(nc.sbuf_tensor(f"wdc{i}", [128, NI], BF16)) for i in range(2)]
        bufs["wdc"] = wdcA
        bufs["wcolf"] = [wdcA[0][:].bitcast(F32)[:, 0:2048], wdcA[1][:].bitcast(F32)[:, 0:2048]]
        bufs["wcol_bf"] = [wguA[0][:, 0:2048], wguA[0][:, 2048:4096]]
        bufs["rows"] = [E(nc.sbuf_tensor(f"row{i}", [1, 512], F32)) for i in range(3)]
        bufs["a_b"] = E(nc.sbuf_tensor("a_b", [128, 512], F32))
        sgbA = E(nc.sbuf_tensor("sgbA", [128, 768], BF16))
        bufs["sgb"] = [sgbA[:, 0:512], sgbA[:, 512:768]]
        bufs["tmp"] = E(nc.sbuf_tensor("tmp", [128, 512], F32))
        bufs["rb"] = E(nc.sbuf_tensor("rb", [128, 512], F32))
        bufs["b_b"] = bufs["rb"]
        bufs["qb2"] = [wdcA[0][:, 4096:5120].bitcast(F32), bufs["rb"][:]]
        bufs["qc"] = wdcA[1][:, 4096:5120].bitcast(F32)
        bufs["t2"] = bufs["a_b"]
        bufs["qk_bf"] = [wguA[1][:, 0:512], wguA[1][:, 512:1024]]
        bufs["v_bf"] = [wguA[1][:, 1024:1280], wguA[1][:, 1280:1536]]
        bufs["msk"] = wguA[1][:, 1536:1664]
        bufs["bvb"] = wguA[1][:, 1664:3712]
        bufs["xn_q"] = [hA[:, g * 1408:(g + 1) * 1408] for g in range(16)]
        bufs["tabc_q"] = hA[:, 22528:23936]
        bufs["tabs_q"] = hA[:, 23936:25344]
        bufs["wvb"] = hA[:, 25344:29440]
        bufs["xbf_q"] = [xnA[:, c * 768:c * 768 + 512] for c in range(16)]
        bufs["bqk_sb"] = E(nc.sbuf_tensor("bqk_sb", [128, 32], F32))
        bufs["ones_bf"] = E(nc.sbuf_tensor("ones_bf", [128, 2], BF16))
        bufs["ones_f32"] = E(nc.sbuf_tensor("ones_f32", [1, 128], F32))
        bufs["cst"] = E(nc.sbuf_tensor("cst", [1, 4], F32))
        bufs["r_sb"] = bufs["rows"][0]
        bufs["rb_sb"] = bufs["rb"]
        bufs["atmp"] = bufs["tmp"]
        bufs["o_sb"] = [bufs["qc"], bufs["t2"]]
        cfg_sb = E(nc.sbuf_tensor("cfg_sb", [1, 16], I32))
        bufs["one_i"] = E(nc.sbuf_tensor("one_i", [1, 64], I32))[:]
        bufs["flag_sb"] = E(nc.sbuf_tensor("flag_sb", [1, 4], I32))

        pbk = [E(nc.psum_tensor(f"pb{i}", [128, 512], F32)) for i in range(8)]
        bufs["pbk"] = pbk
        bufs["psA"], bufs["psB"], bufs["psC"], bufs["psD"] = pbk[0], pbk[1], pbk[2], pbk[3]
        bufs["psS1"], bufs["psS2"] = pbk[6], pbk[7]
        bufs["ps_st"] = [pbk[0], pbk[1]]
        bufs["ps_pv"] = pbk[2]
        bufs["ps_l"] = pbk[6]
        bufs["ps_rb"] = pbk[3]
        bufs["ksh"], bufs["vsh"], bufs["qsh"], bufs["ash"], bufs["fsh"] = ksh, vsh, qsh, ash, fsh
        bufs["mask_in"] = mask_in

        names = ["ld", "ld2", "pe", "dv", "sc", "st", "std", "bar", "poll", "one",
                 "lx", "tab", "wq", "wg0", "wg1", "wd0", "wd1", "kv", "as0", "as1",
                 "sp0", "sp1", "dq0", "dq1"]
        sems = {n: Sem(E(nc.semaphore(f"s_{n}"))) for n in names}

        eng_names = ["sync", "tensor", "vector", "scalar", "gpsimd"]
        rr = {}
        rbit = {}
        regs = {"sync": {}, "gpsimd": {}, "scalar": {}}
        bufs["regs"] = regs
        bufs["rbit"] = rbit

        def bsel(eng, en, fA, fB):
            with eng.If_eq(rbit[en], 0):
                fA()
            with eng.Else():
                fB()
        bufs["bsel"] = bsel

        blk = BLK.new("boot")

        @blk.sync
        def _(sync):
            sync.dma_start(out=cfg_sb[:], in_=cfg_in).then_inc(sems["ld"].h, 16)
            sems["ld"].inc(16)
            sync.wait_ge(sems["ld"].h, sems["ld"].v)
            r = sync.alloc_register("role_sync")
            sync.reg_load(r, cfg_sb[0:1, 0:1])
            rr["sync"] = r
            rbit["sync"] = sync.alloc_register("bit_sync")
            sync.reg_load(rbit["sync"], cfg_sb[0:1, 6:7])
            regs["sync"]["r_cond"] = sync.alloc_register("sy_cond")
            regs["sync"]["r_exp"] = sync.alloc_register("sy_exp")
            regs["sync"]["r_t"] = sync.alloc_register("sy_t")
            sync.reg_mov(regs["sync"]["r_exp"], 0)
        cfg_ld = sems["ld"].v

        @blk.tensor
        def _(tensor):
            tensor.wait_ge(sems["ld"].h, cfg_ld)
            r = tensor.alloc_register("role_tensor")
            tensor.reg_load(r, cfg_sb[0:1, 0:1])
            rr["tensor"] = r

        @blk.vector
        def _(vector):
            vector.wait_ge(sems["ld"].h, cfg_ld)
            r = vector.alloc_register("role_vector")
            vector.reg_load(r, cfg_sb[0:1, 0:1])
            rr["vector"] = r
            vector.memset(bufs["ones_f32"][:], 1.0)
            vector.memset(bufs["ones_bf"][:], 1.0)
            vector.memset(bufs["cst"][0:1, 0:1], LN_EPS)
            vector.memset(bufs["cst"][0:1, 1:2], RMS_EPS)
            vector.memset(bufs["cst"][0:1, 2:3], float(np.log(0.5))).then_inc(sems["dv"].h, 1)
        sems["dv"].inc(1)

        @blk.scalar
        def _(scalar):
            scalar.wait_ge(sems["ld"].h, cfg_ld)
            r = scalar.alloc_register("role_scalar")
            scalar.reg_load(r, cfg_sb[0:1, 0:1])
            rr["scalar"] = r
            rbit["scalar"] = scalar.alloc_register("bit_scalar")
            scalar.reg_load(rbit["scalar"], cfg_sb[0:1, 6:7])

        @blk.gpsimd
        def _(gpsimd):
            gpsimd.wait_ge(sems["ld"].h, cfg_ld)
            r = gpsimd.alloc_register("role_gpsimd")
            gpsimd.reg_load(r, cfg_sb[0:1, 0:1])
            rr["gpsimd"] = r
            rbit["gpsimd"] = gpsimd.alloc_register("bit_gpsimd")
            gpsimd.reg_load(rbit["gpsimd"], cfg_sb[0:1, 6:7])

        def mk_sec(role_id):
            def sec(blk, eng_name):
                def deco(f):
                    def inner(eng):
                        if role_id is None:
                            f(eng)
                        else:
                            with eng.If_eq(rr[eng_name], role_id):
                                f(eng)
                    getattr(blk, eng_name)(inner)
                return deco
            return sec

        sec_c = mk_sec(None)

        blk = BLK.new("loads")

        @blk.sync
        def _(sync):
            sync.dma_start(out=bufs["bqk_sb"][:], in_=bqk_in).then_inc(sems["ld"].h, 16)
            sync.dma_start(out=bufs["one_i"], in_=one_in).then_inc(sems["one"].h, 16)
        sems["ld"].inc(16)
        sems["one"].inc(16)
        bufs["one_ld"] = sems["one"].v
        bufs["bqk_ld"] = sems["ld"].v

        import os as _os
        if _os.environ.get("MK_NODEQ", "0") != "1":
            emit_dequant(nc, BLK, sec_c, bufs, sems, b2g_in, sg_in, b2d_in, sd_in, wshg, wshd)
        emit_barrier(nc, BLK, sec_c, bufs, sems, 0, "b1", [sems["std"]])
        b1 = sems["bar"].v

        STAGE = 5
        for role_id, tagr in ((1, "H"), (0, "L")):
            sec = mk_sec(role_id)
            S = {k: sems[k].clone() for k in sems}
            To = T_LOW if role_id == 0 else T_HIGH
            qt0, qt1 = (0, KT_LOW) if role_id == 0 else (KT_LOW, KT_HIGH)
            n_kt = KT_LOW if role_id == 0 else KT_HIGH
            tok_pair = (0, T_LOW) if role_id == 0 else (S_SPLIT, S_SPLIT + T_HIGH)
            cone_pair = (0, T_CONE)

            blk = BLK.new(f"{tagr}_start")

            @sec(blk, "tensor")
            def _(tensor, b1=b1):
                tensor.wait_ge(S["bar"].h, b1)

            @sec(blk, "gpsimd")
            def _(gpsimd, b1=b1):
                gpsimd.wait_ge(S["bar"].h, b1)

            if STAGE >= 2:
                emit_ffn(nc, BLK, sec, bufs, S, To, wshg, wshd, f"{tagr}f1",
                         x_src=x_in, x_dst=x1d)

            if STAGE >= 3:
                if role_id == 1:
                    for ci, (ct0, cw) in enumerate(((0, 768), (768, 640))):
                        emit_ffn(nc, BLK, sec, bufs, S, cw, wshg, wshd, f"Hc{ci}",
                                 x_src=xc_in[:, ct0:ct0 + cw])
                        cone_views = [bufs["xt"][c][:, 0:cw] for c in range(16)]
                        emit_qkv(nc, BLK, sec, bufs, S, cw, cone_views,
                                 cos_c, sin_c, ct0,
                                 cone_pair, ct0, wqk_in, wv_in, bv_in, ksh, vsh, qsh,
                                 f"Hc{ci}", with_q=False)

                emit_qkv(nc, BLK, sec, bufs, S, To, None,
                         cos_o, sin_o, 0,
                         tok_pair, 0, wqk_in, wv_in, bv_in, ksh, vsh, qsh, f"{tagr}o",
                         with_q=True, x_src=x1d)

                emit_barrier(nc, BLK, sec, bufs, S, 1, f"{tagr}b2", [S["st"], S["sp0"], S["sp1"]])
                b2 = S["bar"].v

            if STAGE >= 4:
                blk = BLK.new(f"{tagr}_preat")

                @sec(blk, "tensor")
                def _(tensor, b2=b2):
                    tensor.wait_ge(S["bar"].h, b2)

                @sec(blk, "sync")
                def _(sync, b2=b2):
                    sync.wait_ge(S["bar"].h, b2)

                emit_attn(nc, BLK, sec, bufs, S, qt0, qt1, n_kt, f"{tagr}a")

                emit_barrier(nc, BLK, sec, bufs, S, 2, f"{tagr}b3", [S["st"], S["sp0"], S["sp1"]])
                b3 = S["bar"].v

            if STAGE >= 5:
                emit_ffn(nc, BLK, sec, bufs, S, To, wshg, wshd, f"{tagr}f3",
                         x_src=x1d, x_dst=x_out,
                         attn_add={"pair": tok_pair, "b3": b3})

            blk = BLK.new(f"{tagr}_out")

            @sec(blk, "gpsimd")
            def _(gpsimd, st_fin=S["st"].v):
                gpsimd.wait_ge(S["st"].h, st_fin)

        BLK.close()
    return nc



_G128 = np.concatenate([np.arange(0, 128, 2), np.arange(1, 128, 2)])


def _permuted_rows(n):
    a = np.arange(n)
    return (a // 128) * 128 + _G128[a % 128]


def _rope_tables():
    inv = 1.0 / (10000.0 ** (np.arange(0, HD, 2, dtype=np.float64) / HD))
    t = np.arange(SEQ, dtype=np.float64)
    fr = np.outer(t, inv)
    emb = np.concatenate([fr, fr], axis=1)
    cos = np.cos(emb).astype(np.float32).T
    sin = np.sin(emb).astype(np.float32).T
    sinA = sin.copy()
    sinA[0:64] = -sin[0:64]
    return cos, sinA


def kernel(L0_attn_hidden_states, L0_attn_w_qkv, L0_attn_b_qkv,
           L0_ffn_gate_up_packed, L0_ffn_gate_up_scales,
           L0_ffn_down_packed, L0_ffn_down_scales):
    x = np.asarray(L0_attn_hidden_states)
    w_qkv = np.asarray(L0_attn_w_qkv)
    b_qkv = np.asarray(L0_attn_b_qkv)
    gup = np.asarray(L0_ffn_gate_up_packed)
    gus = np.asarray(L0_ffn_gate_up_scales)
    dnp = np.asarray(L0_ffn_down_packed)
    dns = np.asarray(L0_ffn_down_scales)
    cores = list(range(NC))

    pr_d = _permuted_rows(D)
    pr_io = _permuted_rows(2 * NI)
    pr_dm = _permuted_rows(D)
    byrow = lambda n: (np.arange(n) // 128) * 64 + (np.arange(n) % 128) % 64
    bytes_g = gup.view(np.uint8).reshape(2 * NI, 1024).T
    gb = bytes_g[byrow(D)][:, pr_io]
    bytes_d = dnp.view(np.uint8).reshape(D, 2816).T
    db = bytes_d[byrow(NI)][:, pr_dm]
    sg_p = gus[:, pr_io]
    sd_p = dns[:, pr_dm]

    cos, sinA = _rope_tables()
    wr = w_qkv.reshape(NH, 3, HD, D)
    wqkT = wr[:, :2].transpose(3, 0, 1, 2).reshape(D, 2 * D)[pr_d]
    wqk_cols = np.ascontiguousarray(
        wqkT.reshape(16, 128, 32, 128).transpose(1, 2, 0, 3).reshape(128, 32, 2048))
    wvT = wr[:, 2].transpose(2, 0, 1).reshape(D, D)[pr_d][:, _permuted_rows(D)]
    wv_cols = np.ascontiguousarray(
        wvT.reshape(16, 128, 8, 256).transpose(1, 2, 0, 3).reshape(128, 8, 4096))
    br = b_qkv.reshape(NH, 3, HD)
    bqk_t = np.ascontiguousarray(br[:, :2].reshape(NH * 2, HD).T)
    bv_row = np.ascontiguousarray(br[:, 2].reshape(1, D)[:, _permuted_rows(D)])
    maskT = np.triu(np.ones((128, 128), np.float32)).astype(BF)

    xT = np.ascontiguousarray(x.transpose(0, 2, 1)[:, pr_d, :])

    in_maps = []
    for c in cores:
        b = c // 4
        is_high = (c % 4) // 2
        bit = c % 2
        if not is_high:
            tok0, To, ctok0 = bit * T_LOW, T_LOW, 0
        else:
            tok0, To, ctok0 = S_SPLIT + bit * T_HIGH, T_HIGH, bit * T_CONE
        x_own = np.zeros((D, T_LOW), np.float32)
        x_own[:, 0:To] = xT[b][:, tok0:tok0 + To]
        xc = (np.ascontiguousarray(xT[b][:, ctok0:ctok0 + T_CONE]) if is_high
              else np.zeros((D, T_CONE), np.float32))
        cos_own = np.zeros((128, T_LOW), np.float32)
        sin_own = np.zeros((128, T_LOW), np.float32)
        cos_own[:, 0:To] = cos[:, tok0:tok0 + To]
        sin_own[:, 0:To] = sinA[:, tok0:tok0 + To]
        cfg = np.zeros((1, 16), np.int32)
        cfg[0, 0] = is_high
        cfg[0, 1] = tok0 // 128
        cfg[0, 2] = ctok0 // 128
        cfg[0, 3] = bit * 8
        cfg[0, 4] = bit * 44
        cfg[0, 5] = bit * 8
        cfg[0, 6] = bit
        cfg[0, 7] = 1 - bit
        gsl = slice(bit * NI, (bit + 1) * NI)
        dsl = slice(bit * (D // 2), (bit + 1) * (D // 2))
        in_maps.append({
            "x_in": np.ascontiguousarray(x_own),
            "xc_in": xc,
            "cfg_in": cfg,
            "one_in": np.ones((1, 64), np.int32),
            "b2g_in": np.ascontiguousarray(gb[:, gsl]),
            "sg_in": np.ascontiguousarray(sg_p[:, gsl]),
            "b2d_in": np.ascontiguousarray(db[:, dsl]),
            "sd_in": np.ascontiguousarray(sd_p[:, dsl]),
            "wqk_in": wqk_cols, "wv_in": wv_cols,
            "bqk_in": bqk_t, "bv_in": bv_row,
            "cos_o": cos_own.astype(BF), "sin_o": sin_own.astype(BF),
            "cos_c": np.ascontiguousarray(cos[:, ctok0:ctok0 + T_CONE]).astype(BF),
            "sin_c": np.ascontiguousarray(sinA[:, ctok0:ctok0 + T_CONE]).astype(BF),
            "mask_in": maskT,
        })

    nc = build_fused()
    res = run_bass_kernel_spmd(nc, in_maps, core_ids=cores).results

    outT = np.empty((NB, D, SEQ), np.float32)
    for c in cores:
        b = c // 4
        is_high = (c % 4) // 2
        bit = c % 2
        if not is_high:
            tok0, To = bit * T_LOW, T_LOW
        else:
            tok0, To = S_SPLIT + bit * T_HIGH, T_HIGH
        outT[b][:, tok0:tok0 + To] = res[c]["x_out"][:, 0:To]
    un = np.empty((NB, D, SEQ), np.float32)
    un[:, pr_d, :] = outT
    return np.ascontiguousarray(un.transpose(0, 2, 1))


# version 15
# speedup vs baseline: 1.6206x; 1.1034x over previous
import numpy as np
import ml_dtypes
from contextlib import ExitStack

import concourse.bass as bass
import concourse.mybir as mybir
from concourse.bass_utils import run_bass_kernel_spmd

F32 = mybir.dt.float32
BF16 = mybir.dt.bfloat16
F16 = mybir.dt.float16
I32 = mybir.dt.int32
U16 = mybir.dt.uint16
U8 = mybir.dt.uint8
AF = mybir.ActivationFunctionType
ALU = mybir.AluOpType
BF = ml_dtypes.bfloat16

NC = 8
D = 2048
NI = 5632
NH = 16
HD = 128
SEQ = 4096
NB = 2
LN_EPS = 1e-5
RMS_EPS = 1e-6
SM_SCALE = float(1.0 / np.sqrt(128.0))

S_SPLIT = 2816
T_LOW = S_SPLIT // 2
T_HIGH = (SEQ - S_SPLIT) // 2
T_CONE = S_SPLIT // 2
KT_LOW = S_SPLIT // 128
KT_HIGH = SEQ // 128


def ttiles(T):
    out, t0 = [], 0
    while t0 < T:
        out.append((t0, min(512, T - t0)))
        t0 += 512
    return out


class BlkMgr:
    def __init__(self, nc):
        self.nc = nc
        self.cm = None

    def new(self, name):
        if self.cm is not None:
            self.cm.__exit__(None, None, None)
        self.cm = self.nc.Block(name)
        return self.cm.__enter__()

    def close(self):
        if self.cm is not None:
            self.cm.__exit__(None, None, None)
            self.cm = None


class Sem:
    def __init__(self, handle):
        self.h = handle
        self.v = 0

    def inc(self, amt=1):
        self.v += amt
        return self.v

    def clone(self):
        s = Sem(self.h)
        s.v = self.v
        return s


def chain(eng, sem, fns):
    for k, f in enumerate(fns):
        if k > 0:
            eng.wait_ge(sem.h, sem.v)
        f().then_inc(sem.h, 1)
        sem.inc(1)



def emit_ln(nc, BLK, sec, bufs, S, x_views, t0, tw, tag, xn=None, xoff=0, xbf=None):
    s_pe, s_dv, s_sc = S["pe"], S["dv"], S["sc"]
    xn = xn if xn is not None else bufs["xn"]
    xbf = xbf if xbf is not None else bufs["xbf"]
    rows = bufs["rows"]
    a_b, b_b, tmp = bufs["a_b"], bufs["b_b"], bufs["tmp"]
    psS1, psS2, psC, psD = bufs["psS1"], bufs["psS2"], bufs["psC"], bufs["psD"]
    ones_bf, ones_f32, cst = bufs["ones_bf"], bufs["ones_f32"], bufs["cst"]

    blk = BLK.new(f"ln_{tag}")

    @sec(blk, "vector")
    def _(vector, dv_now=s_dv.v):
        vector.wait_ge(s_dv.h, dv_now)
        for c in range(16):
            ins = vector.tensor_copy(out=xbf[c][:, 0:tw], in_=x_views[c][:, t0:t0 + tw])
            if c == 15:
                ins.then_inc(s_dv.h, 1)
    s_dv.inc(1)
    cast_done = s_dv.v

    @sec(blk, "tensor")
    def _(tensor, cast_done=cast_done):
        tensor.wait_ge(s_dv.h, cast_done)
        for c in range(16):
            ins = tensor.matmul(out=psS1[0:1, 0:tw], lhsT=ones_bf[:, 0:1], rhs=xbf[c][:, 0:tw],
                                start=(c == 0), stop=(c == 15))
            if c == 15:
                ins.then_inc(s_pe.h, 1)
    s_pe.inc(1)
    s1_done = s_pe.v

    @sec(blk, "scalar")
    def _(scalar, s1_done=s1_done, sc_now=s_sc.v):
        scalar.wait_ge(s_pe.h, s1_done)
        scalar.wait_ge(s_sc.h, sc_now)
        for c in range(16):
            ins = scalar.activation(out=xn[c][:, xoff:xoff + tw], in_=xbf[c][:, 0:tw],
                                    func=AF.Square)
            if c == 15:
                ins.then_inc(s_sc.h, 1)
    s_sc.inc(1)
    sq_done = s_sc.v

    @sec(blk, "tensor")
    def _(tensor, sq_done=sq_done):
        tensor.wait_ge(s_sc.h, sq_done)
        for c in range(16):
            ins = tensor.matmul(out=psS2[0:1, 0:tw], lhsT=ones_bf[:, 0:1],
                                rhs=xn[c][:, xoff:xoff + tw],
                                start=(c == 0), stop=(c == 15))
            if c == 15:
                ins.then_inc(s_pe.h, 1)
    s_pe.inc(1)
    stats_done = s_pe.v

    blk = BLK.new(f"ln_{tag}_r")

    @sec(blk, "vector")
    def _(vector, stats_done=stats_done):
        vector.wait_ge(s_pe.h, stats_done)
        chain(vector, s_dv, [
            lambda: vector.tensor_scalar(out=rows[0][:, 0:tw], in0=psS1[0:1, 0:tw],
                                         scalar1=1.0 / D, scalar2=None, op0=ALU.mult),
            lambda: vector.tensor_scalar(out=rows[1][:, 0:tw], in0=psS2[0:1, 0:tw],
                                         scalar1=1.0 / D, scalar2=None, op0=ALU.mult),
            lambda: vector.tensor_tensor(out=rows[2][:, 0:tw], in0=rows[0][:, 0:tw],
                                         in1=rows[0][:, 0:tw], op=ALU.mult),
            lambda: vector.tensor_tensor(out=rows[2][:, 0:tw], in0=rows[1][:, 0:tw],
                                         in1=rows[2][:, 0:tw], op=ALU.subtract),
        ])
    var_done = s_dv.v

    @sec(blk, "scalar")
    def _(scalar, var_done=var_done):
        scalar.wait_ge(s_dv.h, var_done)
        chain(scalar, s_sc, [
            lambda: scalar.activation(out=rows[1][:, 0:tw], in_=rows[2][:, 0:tw], func=AF.Ln,
                                      bias=cst[0:1, 0:1]),
            lambda: scalar.activation(out=rows[2][:, 0:tw], in_=rows[1][:, 0:tw], func=AF.Exp,
                                      scale=-0.5),
        ])
    rstd_done = s_sc.v

    @sec(blk, "vector")
    def _(vector, rstd_done=rstd_done, dv_now=s_dv.v):
        vector.wait_ge(s_sc.h, rstd_done)
        vector.wait_ge(s_dv.h, dv_now)
        vector.tensor_tensor(out=rows[1][:, 0:tw], in0=rows[0][:, 0:tw], in1=rows[2][:, 0:tw],
                             op=ALU.mult).then_inc(s_dv.h, 1)
    s_dv.inc(1)
    mr_done = s_dv.v

    @sec(blk, "tensor")
    def _(tensor, mr_done=mr_done):
        tensor.wait_ge(s_dv.h, mr_done)
        tensor.matmul(out=psC[:, 0:tw], lhsT=ones_f32[0:1, :], rhs=rows[2][:, 0:tw],
                      start=True, stop=True)
        tensor.matmul(out=psD[:, 0:tw], lhsT=ones_f32[0:1, :], rhs=rows[1][:, 0:tw],
                      start=True, stop=True).then_inc(s_pe.h, 1)
    s_pe.inc(1)
    bc_done = s_pe.v

    blk = BLK.new(f"ln_{tag}_a")

    @sec(blk, "vector")
    def _(vector, bc_done=bc_done):
        vector.wait_ge(s_pe.h, bc_done)
        fns = [lambda: vector.tensor_copy(out=a_b[:, 0:tw], in_=psC[:, 0:tw]),
               lambda: vector.tensor_copy(out=b_b[:, 0:tw], in_=psD[:, 0:tw])]
        for c in range(16):
            fns.append(lambda c=c: vector.tensor_tensor(
                out=tmp[:, 0:tw], in0=x_views[c][:, t0:t0 + tw], in1=a_b[:, 0:tw], op=ALU.mult))
            fns.append(lambda c=c: vector.tensor_tensor(
                out=xn[c][:, xoff:xoff + tw], in0=tmp[:, 0:tw], in1=b_b[:, 0:tw], op=ALU.subtract))
        chain(vector, s_dv, fns)
    return s_dv.v



def emit_ffn(nc, BLK, sec, bufs, S, T, wshg4, wshd4, tag,
             x_views=None, x_src=None, x_dst=None, attn_add=None):
    s_pe, s_dv, s_sc = S["pe"], S["dv"], S["sc"]
    s_st = S["st"]
    s_lx = S["lx"]
    s_wg = [S["wg0"], S["wg1"]]
    s_wd = [S["wd0"], S["wd1"]]
    s_as = [S["as0"], S["as1"]]
    s_sp = [S["sp0"], S["sp1"]]
    xn, rows = bufs["xn"], bufs["rows"]
    pbk = bufs["pbk"]
    ones_bf, ones_f32, cst = bufs["ones_bf"], bufs["ones_f32"], bufs["cst"]
    wguA, wdcA = bufs["wguA"], bufs["wdc"]
    h, y_bf, sgb = bufs["h"], bufs["y_bf"], bufs["sgb"]
    rb, tmp = bufs["rb"], bufs["tmp"]
    xt, atp, ash = bufs["xt"], bufs["atp"], bufs["ash"]

    last_gu = {0: 0, 1: 0}
    last_dn = {0: 0, 1: 0}
    gbank_free = {0: 0, 1: 0}
    ubank_free = {0: 0, 1: 0}
    dbank_free = {k: (0, 0) for k in (2, 3)}
    sgb_free = {0: 0, 1: 0}
    sqb_free = {0: 0, 1: 0}
    DB = [2, 3]
    CH = 768
    FST = 4
    DN = 7
    prev_st = s_st.v

    for ck, c0 in enumerate(range(0, T, CH)):
        cw = min(CH, T - c0)
        subs = ttiles(cw)
        nsub = len(subs)
        if x_src is not None:
            xv = [xt[c][:, 0:cw] for c in range(16)]
            blk = BLK.new(f"xl_{tag}{ck}")

            @sec(blk, "scalar")
            def _(scalar, ck=ck, c0=c0, cw=cw, prev_st=prev_st, dv_now=s_dv.v,
                  aa=attn_add):
                if ck == 0:
                    scalar.wait_ge(s_dv.h, dv_now)
                    if aa is not None:
                        scalar.wait_ge(S["bar"].h, aa["b3"])
                scalar.wait_ge(s_st.h, prev_st)
                for c in range(16):
                    scalar.dma_start(out=xt[c][:, 0:cw],
                                     in_=x_src[c * 128:(c + 1) * 128, c0:c0 + cw]
                                     ).then_inc(s_lx.h, 16)
            s_lx.inc(16 * 16)
            x_loaded = s_lx.v

            if attn_add is not None:
                pair = attn_add["pair"]
                atp_free = {0: 0, 1: 0}
                add_vals = []
                ash_vals = []
                blk = BLK.new(f"aa_{tag}{ck}")

                @sec(blk, "scalar")
                def _(scalar, c0=c0, cw=cw, pair=pair, base_dv=s_dv.v):
                    for c in range(16):
                        if c >= 2:
                            scalar.wait_ge(s_dv.h, base_dv + c - 1)

                        def load(base, c=c):
                            scalar.dma_start(out=atp[c & 1][:, 0:cw],
                                             in_=ash[c, :, base + c0:base + c0 + cw]
                                             ).then_inc(s_as[c & 1].h, 16)
                        bufs["bsel"](scalar, "scalar",
                                     lambda c=c: load(pair[0], c), lambda c=c: load(pair[1], c))
                for c in range(16):
                    s_as[c & 1].inc(16)
                    ash_vals.append(s_as[c & 1].v)

                @sec(blk, "vector")
                def _(vector, cw=cw, x_loaded=x_loaded, ash_vals=tuple(ash_vals),
                      dv_now=s_dv.v):
                    vector.wait_ge(s_dv.h, dv_now)
                    vector.wait_ge(s_lx.h, x_loaded)
                    for c in range(16):
                        vector.wait_ge(s_as[c & 1].h, ash_vals[c])
                        vector.tensor_tensor(out=xt[c][:, 0:cw], in0=xt[c][:, 0:cw],
                                             in1=atp[c & 1][:, 0:cw],
                                             op=ALU.add).then_inc(s_dv.h, 1)
                for c in range(16):
                    s_dv.inc(1)
            else:
                blk = BLK.new(f"xw_{tag}{ck}")

                @sec(blk, "vector")
                def _(vector, x_loaded=x_loaded, dv_now=s_dv.v):
                    vector.wait_ge(s_dv.h, dv_now)
                    vector.wait_ge(s_lx.h, x_loaded)
                    vector.engine_nop().then_inc(s_dv.h, 1)
                s_dv.inc(1)
        else:
            xv = [x_views[c][:, c0:c0 + cw] for c in range(16)]

        xn_ready = 0
        for si, (s0, sw) in enumerate(subs):
            xn_ready = emit_ln(nc, BLK, sec, bufs, S, xv, s0, sw,
                               f"{tag}c{ck}s{si}", xoff=s0)

        for i in range(44 if FST >= 2 else 0):
            par = i & 1
            blk = BLK.new(f"gu_{tag}{ck}_{i}")

            @sec(blk, "sync")
            def _(sync, i=i, par=par, lw=last_gu[par],
                  sp_now=(s_sp[0].v, s_sp[1].v)):
                if lw > 0:
                    sync.wait_ge(s_pe.h, lw)
                else:
                    if sp_now[0] > 0:
                        sync.wait_ge(s_sp[0].h, sp_now[0])
                    if sp_now[1] > 0:
                        sync.wait_ge(s_sp[1].h, sp_now[1])
                sync.dma_start(out=wguA[par][:],
                               in_=wshg4[:, i, :, :].rearrange("p a b -> p (a b)")
                               ).then_inc(s_wg[par].h, 16)
            s_wg[par].inc(16)
            gu_loaded = s_wg[par].v

            plan = []
            for si, (s0, sw) in enumerate(subs):
                bi = (i * nsub + si) & 1
                plan.append((si, s0, sw, bi, gbank_free[bi], ubank_free[bi]))
            gate_done, up_done = {}, {}
            for si in range(nsub):
                s_pe.inc(1)
                gate_done[si] = s_pe.v
                s_pe.inc(1)
                up_done[si] = s_pe.v

            @sec(blk, "tensor")
            def _(tensor, i=i, par=par, gu_loaded=gu_loaded, xn_ready=xn_ready,
                  plan=tuple(plan)):
                tensor.wait_ge(s_wg[par].h, gu_loaded)
                if i == 0:
                    tensor.wait_ge(s_dv.h, xn_ready)
                for (si, s0, sw, bi, gfree, ufree) in plan:
                    if gfree > 0:
                        tensor.wait_ge(s_sc.h, gfree)
                    for g in range(16):
                        ins = tensor.matmul(out=pbk[bi][:, 0:sw],
                                            lhsT=wguA[par][:, g * 128:(g + 1) * 128],
                                            rhs=xn[g][:, s0:s0 + sw],
                                            start=(g == 0), stop=(g == 15))
                        if g == 15:
                            ins.then_inc(s_pe.h, 1)
                    if ufree > 0:
                        tensor.wait_ge(s_dv.h, ufree)
                    for g in range(16):
                        ins = tensor.matmul(out=pbk[2 + bi][:, 0:sw],
                                            lhsT=wguA[par][:, 2048 + g * 128:2048 + (g + 1) * 128],
                                            rhs=xn[g][:, s0:s0 + sw],
                                            start=(g == 0), stop=(g == 15))
                        if g == 15:
                            ins.then_inc(s_pe.h, 1)
            last_gu[par] = up_done[nsub - 1]

            splan = [(si, s0, sw, (i * nsub + si) & 1, gate_done[si], sgb_free[si])
                     for si, (s0, sw) in enumerate(subs)]

            @sec(blk, "scalar")
            def _(scalar, splan=tuple(splan)):
                for (si, s0, sw, bi, gd, sfree) in splan:
                    scalar.wait_ge(s_pe.h, gd)
                    if sfree > 0:
                        scalar.wait_ge(s_dv.h, sfree)
                    scalar.activation(out=sgb[si][:, 0:sw], in_=pbk[bi][:, 0:sw],
                                      func=AF.Silu).then_inc(s_sc.h, 1)
            silu_done = {}
            for si in range(nsub):
                s_sc.inc(1)
                silu_done[si] = s_sc.v

            vplan = [(si, s0, sw, (i * nsub + si) & 1, silu_done[si], up_done[si])
                     for si, (s0, sw) in enumerate(subs)]

            @sec(blk, "vector")
            def _(vector, i=i, vplan=tuple(vplan), dv_now=s_dv.v):
                vector.wait_ge(s_dv.h, dv_now)
                for (si, s0, sw, bi, sd, ud) in vplan:
                    vector.wait_ge(s_sc.h, sd)
                    vector.wait_ge(s_pe.h, ud)
                    vector.tensor_tensor(out=h[i][:, s0:s0 + sw], in0=sgb[si][:, 0:sw],
                                         in1=pbk[2 + bi][:, 0:sw],
                                         op=ALU.mult).then_inc(s_dv.h, 1)
            base_dv = s_dv.v
            for si in range(nsub):
                s_dv.inc(1)
            for si in range(nsub):
                bi = (i * nsub + si) & 1
                gbank_free[bi] = silu_done[si]
                ubank_free[bi] = base_dv + si + 1
                sgb_free[si] = base_dv + si + 1

        h_all = s_dv.v

        stats_done = {}
        sq_vals = {}
        dmm_vals = {}
        for m in range(17 if FST >= 3 else 0):
            par = m & 1
            blk = BLK.new(f"dn_{tag}{ck}_{m}")
            if m < 16:
                @sec(blk, "gpsimd")
                def _(gpsimd, m=m, par=par, lw=last_dn[par]):
                    if lw > 0:
                        gpsimd.wait_ge(s_pe.h, lw)
                    gpsimd.dma_start(out=wdcA[par][:],
                                     in_=wshd4[:, m, :, :].rearrange("p a b -> p (a b)")
                                     ).then_inc(s_wd[par].h, 16)
                s_wd[par].inc(16)
                d_loaded = s_wd[par].v

                dplan = []
                for si, (s0, sw) in enumerate(subs):
                    bi = DB[(m * nsub + si) % 2]
                    dplan.append((si, s0, sw, bi) + dbank_free[bi])
                tplan = []
                if m > 0 and (DN & 2):
                    for si, (s0, sw) in enumerate(subs):
                        tplan.append((si, s0, sw, m - 1, sq_vals[(m - 1, si)]))

                @sec(blk, "tensor")
                def _(tensor, m=m, par=par, d_loaded=d_loaded, h_all=h_all,
                      dplan=tuple(dplan), tplan=tuple(tplan)):
                    for (si, s0, sw, mm, sqv) in tplan:
                        tensor.wait_ge(s_sc.h, sqv)
                        tensor.matmul(out=pbk[6 + si][0:1, 0:sw], lhsT=ones_bf[:, 0:1],
                                      rhs=sgb[si][:, 0:sw], start=(mm == 0),
                                      stop=(mm == 15))
                    tensor.wait_ge(s_wd[par].h, d_loaded)
                    if m == 0:
                        tensor.wait_ge(s_dv.h, h_all)
                    for (si, s0, sw, bi, fsc, fdv) in dplan:
                        if fsc > 0:
                            tensor.wait_ge(s_sc.h, fsc)
                        if fdv > 0:
                            tensor.wait_ge(s_dv.h, fdv)
                        for g in range(44):
                            ins = tensor.matmul(out=pbk[bi][:, 0:sw],
                                                lhsT=wdcA[par][:, g * 128:(g + 1) * 128],
                                                rhs=h[g][:, s0:s0 + sw],
                                                start=(g == 0), stop=(g == 43))
                            if g == 43:
                                ins.then_inc(s_pe.h, 1)
                for si in range(nsub):
                    s_pe.inc(1)
                    dmm_vals[(m, si)] = s_pe.v
                last_dn[par] = dmm_vals[(m, nsub - 1)]

                splan = [(si, s0, sw, DB[(m * nsub + si) % 2], dmm_vals[(m, si)])
                         for si, (s0, sw) in enumerate(subs)]

                if DN & 1:
                    @sec(blk, "scalar")
                    def _(scalar, splan=tuple(splan)):
                        for (si, s0, sw, bi, dv_) in splan:
                            scalar.wait_ge(s_pe.h, dv_)
                            scalar.activation(out=sgb[si][:, 0:sw], in_=pbk[bi][:, 0:sw],
                                              func=AF.Square).then_inc(s_sc.h, 1)
                    for si in range(nsub):
                        s_sc.inc(1)
                        sq_vals[(m, si)] = s_sc.v

                yplan = [(si, s0, sw, DB[(m * nsub + si) % 2], dmm_vals[(m, si)])
                         for si, (s0, sw) in enumerate(subs)]

                if DN & 4:
                    @sec(blk, "vector")
                    def _(vector, m=m, yplan=tuple(yplan), dv_now=s_dv.v,
                          sqv=tuple(sq_vals.get((m, si), 0) for si in range(nsub))):
                        vector.wait_ge(s_dv.h, dv_now)
                        for (si, s0, sw, bi, dv_) in yplan:
                            vector.wait_ge(s_pe.h, dv_)
                            if sqv[si] > 0:
                                vector.wait_ge(s_sc.h, sqv[si])
                            vector.tensor_copy(out=y_bf[m][:, s0:s0 + sw],
                                               in_=pbk[bi][:, 0:sw]).then_inc(s_dv.h, 1)
                    base_dv = s_dv.v
                    for si in range(nsub):
                        s_dv.inc(1)
                    for si in range(nsub):
                        bi = DB[(m * nsub + si) % 2]
                        dbank_free[bi] = (0, base_dv + si + 1)
            elif DN & 2:
                tplan = [(si, s0, sw, sq_vals[(15, si)]) for si, (s0, sw) in enumerate(subs)]

                @sec(blk, "tensor")
                def _(tensor, tplan=tuple(tplan)):
                    for (si, s0, sw, sqv) in tplan:
                        tensor.wait_ge(s_sc.h, sqv)
                        ins = tensor.matmul(out=pbk[6 + si][0:1, 0:sw], lhsT=ones_bf[:, 0:1],
                                            rhs=sgb[si][:, 0:sw], start=False, stop=True)
                        ins.then_inc(s_pe.h, 1)
                for si in range(nsub):
                    s_pe.inc(1)
                    stats_done[si] = s_pe.v

        if FST < 4:
            continue
        y_consumed = {(15, si): (sq_vals[(15, si)], dbank_free[DB[(15 * nsub + si) % 2]][1])
                      for si in range(nsub)}
        for si, (s0, sw) in enumerate(subs):
            blk = BLK.new(f"rms_{tag}{ck}_{si}")

            @sec(blk, "scalar")
            def _(scalar, si=si, sw=sw, sd=stats_done[si]):
                scalar.wait_ge(s_pe.h, sd)
                chain(scalar, s_sc, [
                    lambda: scalar.activation(out=rows[0][:, 0:sw], in_=pbk[6 + si][0:1, 0:sw],
                                              func=AF.Ln, scale=1.0 / D, bias=cst[0:1, 1:2]),
                    lambda: scalar.activation(out=rows[1][:, 0:sw], in_=rows[0][:, 0:sw],
                                              func=AF.Exp, scale=-0.5, bias=cst[0:1, 2:3]),
                ])
            rr_done = s_sc.v

            @sec(blk, "tensor")
            def _(tensor, si=si, sw=sw, rr_done=rr_done,
                  yc=y_consumed[(15, si)]):
                tensor.wait_ge(s_sc.h, rr_done)
                tensor.wait_ge(s_sc.h, yc[0])
                tensor.wait_ge(s_dv.h, yc[1])
                tensor.matmul(out=pbk[si][:, 0:sw], lhsT=ones_f32[0:1, :],
                              rhs=rows[1][:, 0:sw], start=True, stop=True).then_inc(s_pe.h, 1)
            s_pe.inc(1)
            rbc_done = s_pe.v

            @sec(blk, "vector")
            def _(vector, si=si, s0=s0, sw=sw, xv=tuple(xv), rbc_done=rbc_done, dv_now=s_dv.v):
                vector.wait_ge(s_pe.h, rbc_done)
                vector.wait_ge(s_dv.h, dv_now)
                fns = [lambda: vector.tensor_copy(out=rb[:, 0:sw], in_=pbk[si][:, 0:sw])]
                for c in range(16):
                    fns.append(lambda c=c: vector.tensor_copy(
                        out=tmp[:, 0:sw], in_=y_bf[c][:, s0:s0 + sw]))
                    fns.append(lambda c=c: vector.tensor_tensor(
                        out=tmp[:, 0:sw], in0=tmp[:, 0:sw], in1=rb[:, 0:sw], op=ALU.mult))
                    fns.append(lambda c=c: vector.tensor_tensor(
                        out=xv[c][:, s0:s0 + sw], in0=xv[c][:, s0:s0 + sw],
                        in1=tmp[:, 0:sw], op=ALU.add))
                chain(vector, s_dv, fns)

        if x_dst is not None:
            apply_done = s_dv.v
            blk = BLK.new(f"xs_{tag}{ck}")

            @sec(blk, "gpsimd")
            def _(gpsimd, c0=c0, cw=cw, apply_done=apply_done):
                gpsimd.wait_ge(s_dv.h, apply_done)
                for c in range(16):
                    gpsimd.dma_start(out=x_dst[c * 128:(c + 1) * 128, c0:c0 + cw],
                                     in_=xt[c][:, 0:cw]).then_inc(s_st.h, 16)
            s_st.inc(16 * 16)
            prev_st = s_st.v
    return s_dv.v



def emit_qkv(nc, BLK, sec, bufs, S, T, x_views, cos_dram, sin_dram, tab_base, tok_pair,
             tok_const, wqk, wv, bv_in, ksh, vsh, qsh, tag, with_q=True, x_src=None):
    s_pe, s_dv, s_sc, s_st = S["pe"], S["dv"], S["sc"], S["st"]
    s_lx, s_tab, s_wq = S["lx"], S["tab"], S["wq"]
    s_sp = [S["sp0"], S["sp1"]]
    xt = bufs["xt"]
    xn_q = bufs["xn_q"]
    tabcos, tabsin = bufs["tabc_q"], bufs["tabs_q"]
    pbk = bufs["pbk"]
    wcolf, wcol_bf = bufs["wcolf"], bufs["wcol_bf"]
    bqk_sb, bvb = bufs["bqk_sb"], bufs["bvb"]
    qb2, qc, t2 = bufs["qb2"], bufs["qc"], bufs["t2"]
    qk_bf, v_bf = bufs["qk_bf"], bufs["v_bf"]
    wvb = bufs["wvb"]
    tiles = ttiles(T)

    blk = BLK.new(f"qkv_{tag}_bv")

    @sec(blk, "gpsimd")
    def _(gpsimd, dv_now=S["dv"].v, pe_now=s_pe.v):
        gpsimd.wait_ge(S["dv"].h, dv_now)
        gpsimd.wait_ge(s_pe.h, pe_now)
        gpsimd.dma_start(out=bvb, in_=bv_in.broadcast_to([128, D])).then_inc(S["ld2"].h, 16)
    S["ld2"].inc(16)
    bv_ready = S["ld2"].v

    xn_ready = 0
    for tt, (t0, tw) in enumerate(tiles):
        if x_src is not None:
            xv_t = [xt[c][:, 0:tw] for c in range(16)]
            blk = BLK.new(f"qxl_{tag}{tt}")

            @sec(blk, "scalar")
            def _(scalar, t0=t0, tw=tw, tt=tt, dv_now=s_dv.v, st_now=s_st.v):
                scalar.wait_ge(s_dv.h, dv_now)
                if tt == 0:
                    scalar.wait_ge(s_st.h, st_now)
                for c in range(16):
                    scalar.dma_start(out=xt[c][:, 0:tw],
                                     in_=x_src[c * 128:(c + 1) * 128, t0:t0 + tw]
                                     ).then_inc(s_lx.h, 16)
            s_lx.inc(16 * 16)
            qx_loaded = s_lx.v
            blk = BLK.new(f"qxw_{tag}{tt}")

            @sec(blk, "vector")
            def _(vector, qx_loaded=qx_loaded, dv_now=s_dv.v):
                vector.wait_ge(s_dv.h, dv_now)
                vector.wait_ge(s_lx.h, qx_loaded)
                vector.engine_nop().then_inc(s_dv.h, 1)
            s_dv.inc(1)
            xn_ready = emit_ln(nc, BLK, sec, bufs, S, xv_t, 0, tw, f"{tag}q{tt}",
                               xn=xn_q, xoff=t0, xbf=bufs["xbf_q"])
        else:
            xn_ready = emit_ln(nc, BLK, sec, bufs, S, x_views, t0, tw, f"{tag}q{tt}",
                               xn=xn_q, xoff=t0, xbf=bufs["xbf_q"])

    blk = BLK.new(f"qkv_{tag}_tab")

    @sec(blk, "sync")
    def _(sync, T=T, dv_now=S["dv"].v, pe_now=s_pe.v):
        sync.wait_ge(S["dv"].h, dv_now)
        sync.wait_ge(s_pe.h, pe_now)
        sync.dma_start(out=tabcos[:, 0:T],
                       in_=cos_dram[:, tab_base:tab_base + T]).then_inc(s_tab.h, 16)
        sync.dma_start(out=tabsin[:, 0:T],
                       in_=sin_dram[:, tab_base:tab_base + T]).then_inc(s_tab.h, 16)
    s_tab.inc(32)
    tab_ready = s_tab.v

    js = [j for j in range(32) if with_q or (j % 2 == 1)]
    bufs["rope_prev"] = {0: 0, 1: 0}
    cast_done = {0: 0, 1: 0}
    load_done = {0: 0, 1: 0}
    mmref = {0: 0, 1: 0}
    loadref = {0: 0, 1: 0}
    cnt = 0
    for ji, j in enumerate(js):
        par = ji & 1
        h_idx = j // 2
        is_q = (j % 2) == 0
        blk = BLK.new(f"qk_{tag}_{j}")

        @sec(blk, "gpsimd")
        def _(gpsimd, j=j, par=par, lr=loadref[par], pe_now=s_pe.v):
            if lr > 0:
                gpsimd.wait_ge(s_dv.h, lr)
            else:
                gpsimd.wait_ge(s_pe.h, pe_now)
            gpsimd.dma_start(out=wcolf[par][:], in_=wqk[:, j, :]).then_inc(s_wq.h, 16)
        s_wq.inc(16)
        load_done[par] = s_wq.v

        @sec(blk, "vector")
        def _(vector, par=par, ld=load_done[par], mr=mmref[par], xn_ready=xn_ready,
              ji=ji):
            vector.wait_ge(s_wq.h, ld)
            if mr > 0:
                vector.wait_ge(s_pe.h, mr)
            if ji == 0:
                vector.wait_ge(s_dv.h, xn_ready)
            vector.tensor_copy(out=wcol_bf[par][:], in_=wcolf[par][:]).then_inc(s_dv.h, 1)
        s_dv.inc(1)
        cast_done[par] = s_dv.v
        loadref[par] = cast_done[par]

        for tt, (t0, tw) in enumerate(tiles):
            op = cnt & 1
            ps = pbk[op]
            blk = BLK.new(f"qk_{tag}_{j}_t{tt}")

            @sec(blk, "tensor")
            def _(tensor, par=par, ps=ps, t0=t0, tw=tw, cd=cast_done[par], sc_now=s_sc.v):
                tensor.wait_ge(s_dv.h, cd)
                tensor.wait_ge(s_sc.h, sc_now)
                for g in range(16):
                    ins = tensor.matmul(out=ps[:, 0:tw],
                                        lhsT=wcol_bf[par][:, g * 128:(g + 1) * 128],
                                        rhs=xn_q[g][:, t0:t0 + tw],
                                        start=(g == 0), stop=(g == 15))
                    if g == 15:
                        ins.then_inc(s_pe.h, 1)
            s_pe.inc(1)
            mm_done = s_pe.v
            mmref[par] = mm_done

            @sec(blk, "scalar")
            def _(scalar, j=j, ps=ps, tw=tw, mm_done=mm_done, op=op,
                  rope_prev=bufs["rope_prev"][cnt & 1], sc_now=s_sc.v):
                scalar.wait_ge(s_pe.h, mm_done)
                scalar.wait_ge(S["ld"].h, bufs["bqk_ld"])
                if rope_prev > 0:
                    scalar.wait_ge(s_dv.h, rope_prev)
                scalar.wait_ge(s_sc.h, sc_now)
                scalar.activation(out=qb2[op][:, 0:tw], in_=ps[:, 0:tw], func=AF.Identity,
                                  bias=bqk_sb[:, j:j + 1]).then_inc(s_sc.h, 1)
            s_sc.inc(1)
            qb_done = s_sc.v

            @sec(blk, "vector")
            def _(vector, t0=t0, tw=tw, qb_done=qb_done, op=op, lst=s_sp[op].v,
                  tab_ready=tab_ready, qb=qb2[cnt & 1]):
                vector.wait_ge(s_sc.h, qb_done)
                if lst > 0:
                    vector.wait_ge(s_sp[op].h, lst)
                vector.wait_ge(s_tab.h, tab_ready)
                chain(vector, s_dv, [
                    lambda: vector.tensor_copy(out=t2[0:64, 0:tw], in_=qb[64:128, 0:tw]),
                    lambda: vector.tensor_copy(out=t2[64:128, 0:tw], in_=qb[0:64, 0:tw]),
                    lambda: vector.tensor_tensor(out=qc[:, 0:tw], in0=qb[:, 0:tw],
                                                 in1=tabcos[:, t0:t0 + tw], op=ALU.mult),
                    lambda: vector.tensor_tensor(out=t2[:, 0:tw], in0=t2[:, 0:tw],
                                                 in1=tabsin[:, t0:t0 + tw], op=ALU.mult),
                    lambda: vector.tensor_tensor(out=qk_bf[op][:, 0:tw], in0=qc[:, 0:tw],
                                                 in1=t2[:, 0:tw], op=ALU.add),
                ])
            rope_done = s_dv.v
            bufs["rope_prev"][cnt & 1] = rope_done

            @sec(blk, "gpsimd")
            def _(gpsimd, h_idx=h_idx, is_q=is_q, t0=t0, tw=tw, rope_done=rope_done, op=op):
                gpsimd.wait_ge(s_dv.h, rope_done)
                dst = qsh if is_q else ksh

                def store(base):
                    c0 = base + tok_const + t0
                    gpsimd.dma_start(out=dst[h_idx, :, c0:c0 + tw],
                                     in_=qk_bf[op][:, 0:tw]).then_inc(s_sp[op].h, 16)
                bufs["bsel"](gpsimd, "gpsimd", lambda: store(tok_pair[0]), lambda: store(tok_pair[1]))
            s_sp[op].inc(16)
            cnt += 1

    wv_mm = 0
    for fc in range(8):
        for hf in range(2):
            par = hf
            blk = BLK.new(f"v_{tag}_{fc}_{hf}")

            @sec(blk, "gpsimd")
            def _(gpsimd, fc=fc, hf=hf, par=par, lr=loadref[par]):
                if lr > 0:
                    gpsimd.wait_ge(s_dv.h, lr)
                gpsimd.dma_start(out=wcolf[par][:],
                                 in_=wv[:, fc, hf * 2048:(hf + 1) * 2048]).then_inc(s_wq.h, 16)
            s_wq.inc(16)
            ld = s_wq.v

            @sec(blk, "vector")
            def _(vector, hf=hf, par=par, ld=ld, wv_mm=wv_mm):
                vector.wait_ge(s_wq.h, ld)
                if wv_mm > 0:
                    vector.wait_ge(s_pe.h, wv_mm)
                vector.tensor_copy(out=wvb[:, hf * 2048:(hf + 1) * 2048],
                                   in_=wcolf[par][:]).then_inc(s_dv.h, 1)
            s_dv.inc(1)
            loadref[par] = s_dv.v
        v_cast = s_dv.v

        for tt, (t0, tw) in enumerate(tiles):
            for t in range((tw + 127) // 128):
                sw = min(128, tw - t * 128)
                op = cnt & 1
                ps = pbk[op]
                blk = BLK.new(f"v_{tag}_{fc}_{tt}_{t}")

                @sec(blk, "tensor")
                def _(tensor, t0=t0, t=t, sw=sw, ps=ps, v_cast=v_cast, dv_now=s_dv.v,
                      sc_now=s_sc.v):
                    tensor.wait_ge(s_dv.h, v_cast)
                    tensor.wait_ge(s_dv.h, dv_now)
                    tensor.wait_ge(s_sc.h, sc_now)
                    for g in range(16):
                        ins = tensor.matmul(out=ps[0:sw, 0:256],
                                            lhsT=xn_q[g][:, t0 + t * 128:t0 + t * 128 + sw],
                                            rhs=wvb[:, g * 256:(g + 1) * 256],
                                            start=(g == 0), stop=(g == 15))
                        if g == 15:
                            ins.then_inc(s_pe.h, 1)
                s_pe.inc(1)
                vmm_done = s_pe.v
                wv_mm = vmm_done

                @sec(blk, "vector")
                def _(vector, fc=fc, sw=sw, ps=ps, vmm_done=vmm_done, op=op,
                      lst=s_sp[op].v, dv_now=s_dv.v, bv_ready=bv_ready):
                    vector.wait_ge(s_pe.h, vmm_done)
                    if lst > 0:
                        vector.wait_ge(s_sp[op].h, lst)
                    vector.wait_ge(s_dv.h, dv_now)
                    vector.wait_ge(S["ld2"].h, bv_ready)
                    vector.tensor_tensor(out=v_bf[op & 1][0:sw, 0:256], in0=ps[0:sw, 0:256],
                                         in1=bvb[0:sw, fc * 256:(fc + 1) * 256],
                                         op=ALU.add).then_inc(s_dv.h, 1)
                s_dv.inc(1)
                vevac = s_dv.v

                @sec(blk, "gpsimd")
                def _(gpsimd, fc=fc, t0=t0, t=t, sw=sw, vevac=vevac, op=op):
                    gpsimd.wait_ge(s_dv.h, vevac)

                    def store(base):
                        jj = (base + tok_const + t0) // 128 + t
                        for s in range(2):
                            h_i = fc * 2 + s
                            gpsimd.dma_start(out=vsh[h_i, 0:sw, jj, :],
                                             in_=v_bf[op & 1][0:sw, s * 128:(s + 1) * 128]
                                             ).then_inc(s_sp[op].h, 16)
                    bufs["bsel"](gpsimd, "gpsimd", lambda: store(tok_pair[0]), lambda: store(tok_pair[1]))
                s_sp[op].inc(32)
                cnt += 1
    return s_sp[0].v



def emit_attn(nc, BLK, sec, bufs, S, qt0, qt1, n_kt, tag):
    s_pe, s_dv, s_sc, s_st = S["pe"], S["dv"], S["sc"], S["st"]
    s_kv, s_one = S["kv"], S["one"]
    s_sp = [S["sp0"], S["sp1"]]
    ksh, vsh, qsh, ash = bufs["ksh"], bufs["vsh"], bufs["qsh"], bufs["ash"]
    k_sb, v_sb, q_sb = bufs["k_sb"], bufs["v_sb"], bufs["q_sb"]
    pt, msk = bufs["pt"], bufs["msk"]
    ones_bf, ones_f32 = bufs["ones_bf"], bufs["ones_f32"]
    r_sb, rb_sb, o_sb = bufs["r_sb"], bufs["rb_sb"], bufs["o_sb"]
    ps_st, ps_pv, ps_l, ps_rb = bufs["ps_st"], bufs["ps_pv"], bufs["ps_l"], bufs["ps_rb"]
    qlen = (qt1 - qt0) * 128

    blocks = []
    t = qt0
    while t < qt1:
        blocks.append((t, min(t + 4, qt1)))
        t = min(t + 4, qt1)

    blk = BLK.new(f"at_{tag}_msk")

    @sec(blk, "sync")
    def _(sync):
        sync.dma_start(out=msk, in_=bufs["mask_in"]).then_inc(s_one.h, 16)
    s_one.inc(16)
    msk_ld = s_one.v

    last_exp = {k: 0 for k in range(4)}
    last_lpv = {k: 0 for k in range(4)}
    last_ost = {0: 0, 1: 0}
    prev_norm_dv = 0
    for hl in range(8):
        blk = BLK.new(f"at_{tag}{hl}_ld")

        @sec(blk, "sync")
        def _(sync, hl=hl, pe_now=s_pe.v):
            if hl > 0:
                sync.wait_ge(s_pe.h, pe_now)

            def load(hb):
                hh = hb + hl
                sync.dma_start(out=k_sb[:, 0:n_kt * 128],
                               in_=ksh[hh, :, 0:n_kt * 128]).then_inc(s_kv.h, 16)
                sync.dma_start(out=v_sb[:, 0:n_kt * 128],
                               in_=vsh[hh, :, 0:n_kt, :].rearrange("p j f -> p (j f)")
                               ).then_inc(s_kv.h, 16)
                sync.dma_start(out=q_sb[:, 0:qlen],
                               in_=qsh[hh, :, qt0 * 128:qt1 * 128]).then_inc(s_kv.h, 16)
            bufs["bsel"](sync, "sync", lambda: load(0), lambda: load(8))
        s_kv.inc(48)
        kv_loaded = s_kv.v

        for (bt0, bt1) in blocks:
            qw = (bt1 - bt0) * 128
            boff = (bt0 - qt0) * 128
            nkt_b = bt1

            def smm(tensor, kt, le, first, kv_loaded=kv_loaded, boff=boff, qw=qw, bt0=bt0):
                par = kt % 4
                scol0 = max(0, (kt - bt0)) * 128
                if first:
                    tensor.wait_ge(s_kv.h, kv_loaded)
                if le > 0:
                    tensor.wait_ge(s_sc.h, le)
                tensor.matmul(out=ps_st[par][:, scol0:qw],
                              lhsT=k_sb[:, kt * 128:(kt + 1) * 128],
                              rhs=q_sb[:, boff + scol0:boff + qw],
                              start=True, stop=True).then_inc(s_pe.h, 1)

            st_vals = {}
            blk = BLK.new(f"at_{tag}{hl}_{bt0}_h")

            @sec(blk, "tensor")
            def _(tensor, le=last_exp[0], first=(bt0 == qt0)):
                smm(tensor, 0, le, first)
            s_pe.inc(1)
            st_vals[0] = s_pe.v

            for kt in range(nkt_b):
                par = kt % 4
                scol0 = max(0, (kt - bt0)) * 128
                diag = kt >= bt0
                blk = BLK.new(f"at_{tag}{hl}_{bt0}_{kt}")

                @sec(blk, "scalar")
                def _(scalar, par=par, scol0=scol0, qw=qw, st_done=st_vals[kt],
                      lp=last_lpv[par]):
                    scalar.wait_ge(s_pe.h, st_done)
                    if lp > 0:
                        scalar.wait_ge(s_pe.h, lp)
                    scalar.activation(out=pt[par][:, scol0:qw], in_=ps_st[par][:, scol0:qw],
                                      func=AF.Exp, scale=SM_SCALE).then_inc(s_sc.h, 1)
                s_sc.inc(1)
                last_exp[par] = s_sc.v
                exp_done = s_sc.v

                if diag:
                    @sec(blk, "vector")
                    def _(vector, par=par, scol0=scol0, exp_done=exp_done, dv_now=s_dv.v,
                          msk_ld=msk_ld):
                        vector.wait_ge(s_sc.h, exp_done)
                        vector.wait_ge(s_dv.h, dv_now)
                        vector.wait_ge(s_one.h, msk_ld)
                        vector.tensor_tensor(out=pt[par][:, scol0:scol0 + 128],
                                             in0=pt[par][:, scol0:scol0 + 128],
                                             in1=msk[:], op=ALU.mult).then_inc(s_dv.h, 1)
                    s_dv.inc(1)
                    rsem, rval = s_dv, s_dv.v
                else:
                    rsem, rval = s_sc, exp_done

                has_next = kt + 1 < nkt_b
                le_next = last_exp[(kt + 1) % 4] if has_next else 0

                @sec(blk, "tensor")
                def _(tensor, par=par, scol0=scol0, qw=qw, kt=kt, nkt_b=nkt_b,
                      rsem=rsem, rval=rval, pnorm=prev_norm_dv,
                      has_next=has_next, le_next=le_next):
                    if has_next:
                        smm(tensor, kt + 1, le_next, False)
                    tensor.wait_ge(rsem.h, rval)
                    if kt == 0 and pnorm > 0:
                        tensor.wait_ge(s_dv.h, pnorm)
                    tensor.matmul(out=ps_l[0:1, scol0:qw], lhsT=ones_bf[:, 0:1],
                                  rhs=pt[par][:, scol0:qw], start=(kt == 0), stop=(kt == nkt_b - 1))
                    tensor.matmul(out=ps_pv[:, scol0:qw], lhsT=v_sb[:, kt * 128:(kt + 1) * 128],
                                  rhs=pt[par][:, scol0:qw], start=(kt == 0),
                                  stop=(kt == nkt_b - 1)).then_inc(s_pe.h, 1)
                if has_next:
                    s_pe.inc(1)
                    st_vals[kt + 1] = s_pe.v
                s_pe.inc(1)
                last_lpv[par] = s_pe.v

            pv_done = s_pe.v
            op = (bt0 // 4) % 2
            blk = BLK.new(f"at_{tag}{hl}_{bt0}_n")

            @sec(blk, "vector")
            def _(vector, pv_done=pv_done, qw=qw, dv_now=s_dv.v):
                vector.wait_ge(s_pe.h, pv_done)
                vector.wait_ge(s_dv.h, dv_now)
                vector.reciprocal(out=r_sb[0:1, 0:qw], in_=ps_l[0:1, 0:qw]).then_inc(s_dv.h, 1)
            s_dv.inc(1)
            r_done = s_dv.v

            @sec(blk, "tensor")
            def _(tensor, r_done=r_done, qw=qw):
                tensor.wait_ge(s_dv.h, r_done)
                tensor.matmul(out=ps_rb[:, 0:qw], lhsT=ones_f32[0:1, :], rhs=r_sb[0:1, 0:qw],
                              start=True, stop=True).then_inc(s_pe.h, 1)
            s_pe.inc(1)
            rb_done = s_pe.v

            @sec(blk, "vector")
            def _(vector, op=op, rb_done=rb_done, qw=qw, lst=s_sp[op].v):
                vector.wait_ge(s_pe.h, rb_done)
                if lst > 0:
                    vector.wait_ge(s_sp[op].h, lst)
                chain(vector, s_dv, [
                    lambda: vector.tensor_copy(out=rb_sb[:, 0:qw], in_=ps_rb[:, 0:qw]),
                    lambda: vector.tensor_tensor(out=o_sb[op][:, 0:qw], in0=ps_pv[:, 0:qw],
                                                 in1=rb_sb[:, 0:qw], op=ALU.mult),
                ])
            o_done = s_dv.v
            prev_norm_dv = o_done

            @sec(blk, "gpsimd")
            def _(gpsimd, hl=hl, bt0=bt0, op=op, qw=qw, o_done=o_done):
                gpsimd.wait_ge(s_dv.h, o_done)

                def store(hb):
                    gpsimd.dma_start(out=ash[hb + hl, :, bt0 * 128:bt0 * 128 + qw],
                                     in_=o_sb[op][:, 0:qw]).then_inc(s_sp[op].h, 16)
                bufs["bsel"](gpsimd, "gpsimd", lambda: store(0), lambda: store(8))
            s_sp[op].inc(16)
            last_ost[op] = s_sp[op].v
    return s_st.v



def emit_barrier(nc, BLK, sec, bufs, S, bar_idx, tag, wait_sems):
    s_bar, s_poll = S["bar"], S["poll"]
    fsh, one_i, flag_sb = bufs["fsh"], bufs["one_i"], bufs["flag_sb"]
    regs = bufs["regs"]
    blk = BLK.new(f"bar_{tag}")
    waits = [(s.h, s.v) for s in wait_sems]

    @sec(blk, "gpsimd")
    def _(gpsimd, waits=waits, bar_idx=bar_idx):
        for hh, v in waits:
            gpsimd.wait_ge(hh, v)
        gpsimd.wait_ge(S["one"].h, bufs["one_ld"])

        def store(row):
            gpsimd.dma_start(out=fsh[row:row + 1, 0:64], in_=one_i).then_inc(S["st"].h, 16)
        bufs["bsel"](gpsimd, "gpsimd",
                     lambda: store(2 * bar_idx), lambda: store(2 * bar_idx + 1))
    S["st"].inc(16)

    import os
    _NOPOLL = os.environ.get("MK_NOPOLL", "0") == "1"

    @sec(blk, "sync")
    def _(sync, bar_idx=bar_idx, waits=waits):
        for hh, v in waits:
            sync.wait_ge(hh, v)
        if _NOPOLL:
            sync.sem_inc(s_bar.h, 1)
            return
        r_cond = regs["sync"]["r_cond"]
        r_exp = regs["sync"]["r_exp"]
        r_t = regs["sync"]["r_t"]
        sync.reg_mov(r_cond, 1)
        with sync.While(r_cond):
            def poll(row):
                sync.dma_start(out=flag_sb[0:1, 0:1],
                               in_=fsh[row:row + 1, 0:1]).then_inc(s_poll.h, 16)
            bufs["bsel"](sync, "sync",
                         lambda: poll(2 * bar_idx + 1), lambda: poll(2 * bar_idx))
            sync.reg_add(r_exp, r_exp, 16)
            sync.wait_ge(s_poll.h, r_exp)
            sync.reg_load(r_t, flag_sb[0:1, 0:1])
            sync.reg_sub(r_cond, 1, r_t)
        sync.sem_inc(s_bar.h, 1)
    s_bar.inc(1)
    return s_bar.v



def emit_dequant(nc, BLK, sec, bufs, S, b2g, sg, b2d, sd, wshg, wshd):
    s_dv, s_std = S["dv"], S["std"]
    s_dq = [S["dq0"], S["dq1"]]
    byt, scb = bufs["dq_byt"], bufs["dq_scb"]
    nr8, n15, rb1, rb2, rm, ru, rv, bits, wt = (
        bufs["dq_nr8"], bufs["dq_n15"], bufs["dq_rb1"], bufs["dq_rb2"], bufs["dq_rm"],
        bufs["dq_ru"], bufs["dq_rv"], bufs["dq_bits"], bufs["dq_wt"])
    regs = bufs["regs"]

    tiles = [("g", g, q, 1408, 11) for g in range(16) for q in range(4)]
    tiles += [("d", g, 0, 1024, 8) for g in range(44)]
    for i, (which, g, q, w, nct) in enumerate(tiles):
        p = i % 2
        blk = BLK.new(f"dq{i}")

        @sec(blk, "sync")
        def _(sync, which=which, g=g, q=q, w=w, p=p, i=i, dec_prev=s_dv.v):
            if i >= 2:
                sync.wait_ge(s_dv.h, dec_prev)
            src = b2g if which == "g" else b2d
            sync.dma_start(out=byt[p][:, 0:w],
                           in_=src[g * 128:(g + 1) * 128, q * w:(q + 1) * w]).then_inc(s_dq[p].h, 16)

        @sec(blk, "scalar")
        def _(scalar, which=which, g=g, q=q, w=w, p=p, i=i, dec_prev=s_dv.v):
            if i >= 2:
                scalar.wait_ge(s_dv.h, dec_prev)
            ssc = sg if which == "g" else sd
            scalar.dma_start(out=scb[p][:, 0:w],
                             in_=ssc[g:g + 1, q * w:(q + 1) * w].broadcast_to([128, w])).then_inc(s_dq[p].h, 16)
        s_dq[p].inc(32)
        loaded = s_dq[p].v

        @sec(blk, "vector")
        def _(vector, w=w, p=p, i=i, loaded=loaded, st_prev=s_std.v):
            vector.wait_ge(s_dq[p].h, loaded)
            if i >= 2:
                vector.wait_ge(s_std.h, st_prev)
            B = byt[p][:, 0:w]
            chain(vector, s_dv, [
                lambda: vector.tensor_scalar(out=nr8[0:64, 0:w], in0=B[0:64], scalar1=15,
                                             scalar2=None, op0=ALU.bitwise_and),
                lambda: vector.tensor_scalar(out=nr8[64:128, 0:w], in0=B[64:128], scalar1=4,
                                             scalar2=None, op0=ALU.logical_shift_right),
                lambda: vector.tensor_copy(out=n15[:, 0:w], in_=nr8[:, 0:w]),
                lambda: vector.tensor_scalar(out=rb1[:, 0:w], in0=n15[:, 0:w], scalar1=8, scalar2=12,
                                             op0=ALU.bitwise_and, op1=ALU.logical_shift_left),
                lambda: vector.tensor_scalar(out=rb2[:, 0:w], in0=n15[:, 0:w], scalar1=7, scalar2=6,
                                             op0=ALU.bitwise_and, op1=ALU.logical_shift_left),
                lambda: vector.tensor_scalar(out=rm[:, 0:w], in0=n15[:, 0:w], scalar1=6,
                                             scalar2=None, op0=ALU.bitwise_and),
                lambda: vector.tensor_scalar(out=ru[:, 0:w], in0=rm[:, 0:w], scalar1=0, scalar2=16128,
                                             op0=ALU.is_equal, op1=ALU.mult),
                lambda: vector.tensor_scalar(out=rm[:, 0:w], in0=n15[:, 0:w], scalar1=7,
                                             scalar2=None, op0=ALU.bitwise_and),
                lambda: vector.tensor_scalar(out=rv[:, 0:w], in0=rm[:, 0:w], scalar1=1, scalar2=16064,
                                             op0=ALU.is_equal, op1=ALU.mult),
                lambda: vector.scalar_tensor_tensor(out=rb1[:, 0:w], in0=rb1[:, 0:w], scalar=16128.0,
                                                    in1=rb2[:, 0:w], op0=ALU.add, op1=ALU.add),
                lambda: vector.tensor_tensor(out=rb1[:, 0:w], in0=rb1[:, 0:w], in1=ru[:, 0:w],
                                             op=ALU.subtract),
                lambda: vector.tensor_tensor(out=bits[:, 0:w], in0=rb1[:, 0:w], in1=rv[:, 0:w],
                                             op=ALU.add),
                lambda: vector.tensor_tensor(out=wt[p][:, 0:w], in0=bits[:, 0:w].bitcast(BF16),
                                             in1=scb[p][:, 0:w], op=ALU.mult),
            ])
        dec = s_dv.v

        st_eng = "scalar" if (i & 1) else "sync"

        @sec(blk, st_eng)
        def _(eng, which=which, g=g, q=q, w=w, nct=nct, p=p, dec=dec, st_eng=st_eng):
            eng.wait_ge(s_dv.h, dec)
            src = wt[p][:, 0:w].rearrange("p (a b) -> p a b", a=nct)

            def store(half):
                if which == "g":
                    dst = wshg[:, q * nct:(q + 1) * nct, half, g * 128:(g + 1) * 128]
                else:
                    dst = wshd[:, half * 8:(half + 1) * 8, g, :]
                eng.dma_start(out=dst, in_=src).then_inc(s_std.h, 16)
            bufs["bsel"](eng, st_eng, lambda: store(0), lambda: store(1))
        s_std.inc(16)
    return s_std.v



def build_fused():
    nc = bass.Bass("TRN2", target_bir_lowering=False)
    x_in = nc.dram_tensor("x_in", [D, T_LOW], F32, kind="ExternalInput").ap()
    xc_in = nc.dram_tensor("xc_in", [D, T_CONE], F32, kind="ExternalInput").ap()
    cfg_in = nc.dram_tensor("cfg_in", [1, 16], I32, kind="ExternalInput").ap()
    one_in = nc.dram_tensor("one_in", [1, 64], I32, kind="ExternalInput").ap()
    b2g_in = nc.dram_tensor("b2g_in", [D, NI], U8, kind="ExternalInput").ap()
    sg_in = nc.dram_tensor("sg_in", [16, NI], F16, kind="ExternalInput").ap()
    b2d_in = nc.dram_tensor("b2d_in", [NI, D // 2], U8, kind="ExternalInput").ap()
    sd_in = nc.dram_tensor("sd_in", [44, D // 2], F16, kind="ExternalInput").ap()
    wqk_in = nc.dram_tensor("wqk_in", [128, 32, 2048], F32, kind="ExternalInput").ap()
    wv_in = nc.dram_tensor("wv_in", [128, 8, 4096], F32, kind="ExternalInput").ap()
    bqk_in = nc.dram_tensor("bqk_in", [128, 32], F32, kind="ExternalInput").ap()
    bv_in = nc.dram_tensor("bv_in", [1, D], F32, kind="ExternalInput").ap()
    cos_o = nc.dram_tensor("cos_o", [128, T_LOW], BF16, kind="ExternalInput").ap()
    sin_o = nc.dram_tensor("sin_o", [128, T_LOW], BF16, kind="ExternalInput").ap()
    cos_c = nc.dram_tensor("cos_c", [128, T_CONE], BF16, kind="ExternalInput").ap()
    sin_c = nc.dram_tensor("sin_c", [128, T_CONE], BF16, kind="ExternalInput").ap()
    mask_in = nc.dram_tensor("mask_in", [128, 128], BF16, kind="ExternalInput").ap()
    x_out = nc.dram_tensor("x_out", [D, T_LOW], F32, kind="ExternalOutput").ap()

    wshg = nc.dram_tensor("wshg", [128, 44, 2, 2048], BF16, kind="Internal", addr_space="Shared").ap()
    wshd = nc.dram_tensor("wshd", [128, 16, 44, 128], BF16, kind="Internal", addr_space="Shared").ap()
    ksh = nc.dram_tensor("ksh", [NH, 128, SEQ], BF16, kind="Internal", addr_space="Shared").ap()
    qsh = nc.dram_tensor("qsh", [NH, 128, SEQ], BF16, kind="Internal", addr_space="Shared").ap()
    vsh = nc.dram_tensor("vsh", [NH, 128, SEQ // 128, 128], BF16, kind="Internal", addr_space="Shared").ap()
    ash = nc.dram_tensor("ash", [NH, 128, SEQ], F32, kind="Internal", addr_space="Shared").ap()
    fsh = nc.dram_tensor("fsh", [8, 256], I32, kind="Internal", addr_space="Shared").ap()
    x1d = nc.dram_tensor("x1d", [D, T_LOW], F32, kind="Internal").ap()

    with ExitStack() as ctx:
        E = ctx.enter_context
        BLK = BlkMgr(nc)
        bufs = {}

        xtA = E(nc.sbuf_tensor("xtA", [128, 16 * 768], F32))
        bufs["xt"] = [xtA[:, c * 768:(c + 1) * 768] for c in range(16)]
        atpA = E(nc.sbuf_tensor("atpA", [128, 2 * 768], F32))
        bufs["atp"] = [atpA[:, 0:768], atpA[:, 768:1536]]
        xnA = E(nc.sbuf_tensor("xnA", [128, 16 * 768], BF16))
        bufs["xn"] = [xnA[:, c * 768:(c + 1) * 768] for c in range(16)]
        bufs["y_bf"] = [xnA[:, c * 768:(c + 1) * 768] for c in range(16)]
        hA = E(nc.sbuf_tensor("hA", [128, 44 * 768], BF16))
        bufs["h"] = [hA[:, i * 768:(i + 1) * 768] for i in range(44)]
        bufs["xbf"] = [hA[:, c * 768:c * 768 + 512] for c in range(16)]
        bufs["k_sb"] = hA[:, 0:4096]
        bufs["v_sb"] = hA[:, 4096:8192]
        bufs["q_sb"] = hA[:, 8192:8192 + 2816]
        bufs["pt"] = [hA[:, 11008 + i * 512:11008 + (i + 1) * 512] for i in range(4)]
        dqA = hA
        bufs["dq_nr8"] = dqA[:, 0:704].bitcast(U8)
        bufs["dq_byt"] = [dqA[:, 704 + i * 704:704 + (i + 1) * 704].bitcast(U8) for i in range(2)]
        bufs["dq_scb"] = [dqA[:, 2112 + i * 1408:2112 + (i + 1) * 1408].bitcast(F16) for i in range(2)]
        off = 4928
        for nm in ("dq_n15", "dq_rb1", "dq_rb2", "dq_rm", "dq_ru", "dq_rv", "dq_bits"):
            bufs[nm] = dqA[:, off:off + 1408].bitcast(U16)
            off += 1408
        bufs["dq_wt"] = [dqA[:, off + i * 1408:off + (i + 1) * 1408] for i in range(2)]

        wguA = [E(nc.sbuf_tensor(f"wgu{i}", [128, 4096], BF16)) for i in range(2)]
        bufs["wguA"] = wguA
        wdcA = [E(nc.sbuf_tensor(f"wdc{i}", [128, NI], BF16)) for i in range(2)]
        bufs["wdc"] = wdcA
        bufs["wcolf"] = [wdcA[0][:].bitcast(F32)[:, 0:2048], wdcA[1][:].bitcast(F32)[:, 0:2048]]
        bufs["wcol_bf"] = [wguA[0][:, 0:2048], wguA[0][:, 2048:4096]]
        bufs["rows"] = [E(nc.sbuf_tensor(f"row{i}", [1, 512], F32)) for i in range(3)]
        bufs["a_b"] = E(nc.sbuf_tensor("a_b", [128, 512], F32))
        sgbA = E(nc.sbuf_tensor("sgbA", [128, 768], BF16))
        bufs["sgb"] = [sgbA[:, 0:512], sgbA[:, 512:768]]
        bufs["tmp"] = E(nc.sbuf_tensor("tmp", [128, 512], F32))
        bufs["rb"] = E(nc.sbuf_tensor("rb", [128, 512], F32))
        bufs["b_b"] = bufs["rb"]
        bufs["qb2"] = [wdcA[0][:, 4096:5120].bitcast(F32), bufs["rb"][:]]
        bufs["qc"] = wdcA[1][:, 4096:5120].bitcast(F32)
        bufs["t2"] = bufs["a_b"]
        bufs["qk_bf"] = [wguA[1][:, 0:512], wguA[1][:, 512:1024]]
        bufs["v_bf"] = [wguA[1][:, 1024:1280], wguA[1][:, 1280:1536]]
        bufs["msk"] = wguA[1][:, 1536:1664]
        bufs["bvb"] = wguA[1][:, 1664:3712]
        bufs["xn_q"] = [hA[:, g * 1408:(g + 1) * 1408] for g in range(16)]
        bufs["tabc_q"] = hA[:, 22528:23936]
        bufs["tabs_q"] = hA[:, 23936:25344]
        bufs["wvb"] = hA[:, 25344:29440]
        bufs["xbf_q"] = [xnA[:, c * 768:c * 768 + 512] for c in range(16)]
        bufs["bqk_sb"] = E(nc.sbuf_tensor("bqk_sb", [128, 32], F32))
        bufs["ones_bf"] = E(nc.sbuf_tensor("ones_bf", [128, 2], BF16))
        bufs["ones_f32"] = E(nc.sbuf_tensor("ones_f32", [1, 128], F32))
        bufs["cst"] = E(nc.sbuf_tensor("cst", [1, 4], F32))
        bufs["r_sb"] = bufs["rows"][0]
        bufs["rb_sb"] = bufs["rb"]
        bufs["atmp"] = bufs["tmp"]
        bufs["o_sb"] = [bufs["qc"], bufs["t2"]]
        cfg_sb = E(nc.sbuf_tensor("cfg_sb", [1, 16], I32))
        bufs["one_i"] = E(nc.sbuf_tensor("one_i", [1, 64], I32))[:]
        bufs["flag_sb"] = E(nc.sbuf_tensor("flag_sb", [1, 4], I32))

        pbk = [E(nc.psum_tensor(f"pb{i}", [128, 512], F32)) for i in range(8)]
        bufs["pbk"] = pbk
        bufs["psA"], bufs["psB"], bufs["psC"], bufs["psD"] = pbk[0], pbk[1], pbk[2], pbk[3]
        bufs["psS1"], bufs["psS2"] = pbk[6], pbk[7]
        bufs["ps_st"] = [pbk[0], pbk[1], pbk[4], pbk[5]]
        bufs["ps_pv"] = pbk[2]
        bufs["ps_l"] = pbk[6]
        bufs["ps_rb"] = pbk[3]
        bufs["ksh"], bufs["vsh"], bufs["qsh"], bufs["ash"], bufs["fsh"] = ksh, vsh, qsh, ash, fsh
        bufs["mask_in"] = mask_in

        names = ["ld", "ld2", "pe", "dv", "sc", "st", "std", "bar", "poll", "one",
                 "lx", "tab", "wq", "wg0", "wg1", "wd0", "wd1", "kv", "as0", "as1",
                 "sp0", "sp1", "dq0", "dq1"]
        sems = {n: Sem(E(nc.semaphore(f"s_{n}"))) for n in names}

        eng_names = ["sync", "tensor", "vector", "scalar", "gpsimd"]
        rr = {}
        rbit = {}
        regs = {"sync": {}, "gpsimd": {}, "scalar": {}}
        bufs["regs"] = regs
        bufs["rbit"] = rbit

        def bsel(eng, en, fA, fB):
            with eng.If_eq(rbit[en], 0):
                fA()
            with eng.Else():
                fB()
        bufs["bsel"] = bsel

        blk = BLK.new("boot")

        @blk.sync
        def _(sync):
            sync.dma_start(out=cfg_sb[:], in_=cfg_in).then_inc(sems["ld"].h, 16)
            sems["ld"].inc(16)
            sync.wait_ge(sems["ld"].h, sems["ld"].v)
            r = sync.alloc_register("role_sync")
            sync.reg_load(r, cfg_sb[0:1, 0:1])
            rr["sync"] = r
            rbit["sync"] = sync.alloc_register("bit_sync")
            sync.reg_load(rbit["sync"], cfg_sb[0:1, 6:7])
            regs["sync"]["r_cond"] = sync.alloc_register("sy_cond")
            regs["sync"]["r_exp"] = sync.alloc_register("sy_exp")
            regs["sync"]["r_t"] = sync.alloc_register("sy_t")
            sync.reg_mov(regs["sync"]["r_exp"], 0)
        cfg_ld = sems["ld"].v

        @blk.tensor
        def _(tensor):
            tensor.wait_ge(sems["ld"].h, cfg_ld)
            r = tensor.alloc_register("role_tensor")
            tensor.reg_load(r, cfg_sb[0:1, 0:1])
            rr["tensor"] = r

        @blk.vector
        def _(vector):
            vector.wait_ge(sems["ld"].h, cfg_ld)
            r = vector.alloc_register("role_vector")
            vector.reg_load(r, cfg_sb[0:1, 0:1])
            rr["vector"] = r
            vector.memset(bufs["ones_f32"][:], 1.0)
            vector.memset(bufs["ones_bf"][:], 1.0)
            vector.memset(bufs["cst"][0:1, 0:1], LN_EPS)
            vector.memset(bufs["cst"][0:1, 1:2], RMS_EPS)
            vector.memset(bufs["cst"][0:1, 2:3], float(np.log(0.5))).then_inc(sems["dv"].h, 1)
        sems["dv"].inc(1)

        @blk.scalar
        def _(scalar):
            scalar.wait_ge(sems["ld"].h, cfg_ld)
            r = scalar.alloc_register("role_scalar")
            scalar.reg_load(r, cfg_sb[0:1, 0:1])
            rr["scalar"] = r
            rbit["scalar"] = scalar.alloc_register("bit_scalar")
            scalar.reg_load(rbit["scalar"], cfg_sb[0:1, 6:7])

        @blk.gpsimd
        def _(gpsimd):
            gpsimd.wait_ge(sems["ld"].h, cfg_ld)
            r = gpsimd.alloc_register("role_gpsimd")
            gpsimd.reg_load(r, cfg_sb[0:1, 0:1])
            rr["gpsimd"] = r
            rbit["gpsimd"] = gpsimd.alloc_register("bit_gpsimd")
            gpsimd.reg_load(rbit["gpsimd"], cfg_sb[0:1, 6:7])

        def mk_sec(role_id):
            def sec(blk, eng_name):
                def deco(f):
                    def inner(eng):
                        if role_id is None:
                            f(eng)
                        else:
                            with eng.If_eq(rr[eng_name], role_id):
                                f(eng)
                    getattr(blk, eng_name)(inner)
                return deco
            return sec

        sec_c = mk_sec(None)

        blk = BLK.new("loads")

        @blk.sync
        def _(sync):
            sync.dma_start(out=bufs["bqk_sb"][:], in_=bqk_in).then_inc(sems["ld"].h, 16)
            sync.dma_start(out=bufs["one_i"], in_=one_in).then_inc(sems["one"].h, 16)
        sems["ld"].inc(16)
        sems["one"].inc(16)
        bufs["one_ld"] = sems["one"].v
        bufs["bqk_ld"] = sems["ld"].v

        import os as _os
        if _os.environ.get("MK_NODEQ", "0") != "1":
            emit_dequant(nc, BLK, sec_c, bufs, sems, b2g_in, sg_in, b2d_in, sd_in, wshg, wshd)
        emit_barrier(nc, BLK, sec_c, bufs, sems, 0, "b1", [sems["std"]])
        b1 = sems["bar"].v

        STAGE = 5
        for role_id, tagr in ((0, "L"), (1, "H")):
            sec = mk_sec(role_id)
            S = {k: sems[k].clone() for k in sems}
            To = T_LOW if role_id == 0 else T_HIGH
            qt0, qt1 = (0, KT_LOW) if role_id == 0 else (KT_LOW, KT_HIGH)
            n_kt = KT_LOW if role_id == 0 else KT_HIGH
            tok_pair = (0, T_LOW) if role_id == 0 else (S_SPLIT, S_SPLIT + T_HIGH)
            cone_pair = (0, T_CONE)

            blk = BLK.new(f"{tagr}_start")

            @sec(blk, "tensor")
            def _(tensor, b1=b1):
                tensor.wait_ge(S["bar"].h, b1)

            @sec(blk, "gpsimd")
            def _(gpsimd, b1=b1):
                gpsimd.wait_ge(S["bar"].h, b1)

            if STAGE >= 2:
                emit_ffn(nc, BLK, sec, bufs, S, To, wshg, wshd, f"{tagr}f1",
                         x_src=x_in, x_dst=x1d)

            if STAGE >= 3:
                if role_id == 1:
                    for ci, (ct0, cw) in enumerate(((0, 768), (768, 640))):
                        emit_ffn(nc, BLK, sec, bufs, S, cw, wshg, wshd, f"Hc{ci}",
                                 x_src=xc_in[:, ct0:ct0 + cw])
                        cone_views = [bufs["xt"][c][:, 0:cw] for c in range(16)]
                        emit_qkv(nc, BLK, sec, bufs, S, cw, cone_views,
                                 cos_c, sin_c, ct0,
                                 cone_pair, ct0, wqk_in, wv_in, bv_in, ksh, vsh, qsh,
                                 f"Hc{ci}", with_q=False)

                emit_qkv(nc, BLK, sec, bufs, S, To, None,
                         cos_o, sin_o, 0,
                         tok_pair, 0, wqk_in, wv_in, bv_in, ksh, vsh, qsh, f"{tagr}o",
                         with_q=True, x_src=x1d)

                emit_barrier(nc, BLK, sec, bufs, S, 1, f"{tagr}b2", [S["st"], S["sp0"], S["sp1"]])
                b2 = S["bar"].v

            if STAGE >= 4:
                blk = BLK.new(f"{tagr}_preat")

                @sec(blk, "tensor")
                def _(tensor, b2=b2):
                    tensor.wait_ge(S["bar"].h, b2)

                @sec(blk, "sync")
                def _(sync, b2=b2):
                    sync.wait_ge(S["bar"].h, b2)

                emit_attn(nc, BLK, sec, bufs, S, qt0, qt1, n_kt, f"{tagr}a")

                emit_barrier(nc, BLK, sec, bufs, S, 2, f"{tagr}b3", [S["st"], S["sp0"], S["sp1"]])
                b3 = S["bar"].v

            if STAGE >= 5:
                emit_ffn(nc, BLK, sec, bufs, S, To, wshg, wshd, f"{tagr}f3",
                         x_src=x1d, x_dst=x_out,
                         attn_add={"pair": tok_pair, "b3": b3})

            blk = BLK.new(f"{tagr}_out")

            @sec(blk, "gpsimd")
            def _(gpsimd, st_fin=S["st"].v):
                gpsimd.wait_ge(S["st"].h, st_fin)

        BLK.close()
    return nc



_G128 = np.concatenate([np.arange(0, 128, 2), np.arange(1, 128, 2)])


def _permuted_rows(n):
    a = np.arange(n)
    return (a // 128) * 128 + _G128[a % 128]


def _rope_tables():
    inv = 1.0 / (10000.0 ** (np.arange(0, HD, 2, dtype=np.float64) / HD))
    t = np.arange(SEQ, dtype=np.float64)
    fr = np.outer(t, inv)
    emb = np.concatenate([fr, fr], axis=1)
    cos = np.cos(emb).astype(np.float32).T
    sin = np.sin(emb).astype(np.float32).T
    sinA = sin.copy()
    sinA[0:64] = -sin[0:64]
    return cos, sinA


def kernel(L0_attn_hidden_states, L0_attn_w_qkv, L0_attn_b_qkv,
           L0_ffn_gate_up_packed, L0_ffn_gate_up_scales,
           L0_ffn_down_packed, L0_ffn_down_scales):
    x = np.asarray(L0_attn_hidden_states)
    w_qkv = np.asarray(L0_attn_w_qkv)
    b_qkv = np.asarray(L0_attn_b_qkv)
    gup = np.asarray(L0_ffn_gate_up_packed)
    gus = np.asarray(L0_ffn_gate_up_scales)
    dnp = np.asarray(L0_ffn_down_packed)
    dns = np.asarray(L0_ffn_down_scales)
    cores = list(range(NC))

    pr_d = _permuted_rows(D)
    pr_io = _permuted_rows(2 * NI)
    pr_dm = _permuted_rows(D)
    byrow = lambda n: (np.arange(n) // 128) * 64 + (np.arange(n) % 128) % 64
    bytes_g = gup.view(np.uint8).reshape(2 * NI, 1024).T
    gb = bytes_g[byrow(D)][:, pr_io]
    bytes_d = dnp.view(np.uint8).reshape(D, 2816).T
    db = bytes_d[byrow(NI)][:, pr_dm]
    sg_p = gus[:, pr_io]
    sd_p = dns[:, pr_dm]

    cos, sinA = _rope_tables()
    wr = w_qkv.reshape(NH, 3, HD, D)
    wqkT = wr[:, :2].transpose(3, 0, 1, 2).reshape(D, 2 * D)[pr_d]
    wqk_cols = np.ascontiguousarray(
        wqkT.reshape(16, 128, 32, 128).transpose(1, 2, 0, 3).reshape(128, 32, 2048))
    wvT = wr[:, 2].transpose(2, 0, 1).reshape(D, D)[pr_d][:, _permuted_rows(D)]
    wv_cols = np.ascontiguousarray(
        wvT.reshape(16, 128, 8, 256).transpose(1, 2, 0, 3).reshape(128, 8, 4096))
    br = b_qkv.reshape(NH, 3, HD)
    bqk_t = np.ascontiguousarray(br[:, :2].reshape(NH * 2, HD).T)
    bv_row = np.ascontiguousarray(br[:, 2].reshape(1, D)[:, _permuted_rows(D)])
    maskT = np.triu(np.ones((128, 128), np.float32)).astype(BF)

    xT = np.ascontiguousarray(x.transpose(0, 2, 1)[:, pr_d, :])

    in_maps = []
    for c in cores:
        b = c // 4
        is_high = (c % 4) // 2
        bit = c % 2
        if not is_high:
            tok0, To, ctok0 = bit * T_LOW, T_LOW, 0
        else:
            tok0, To, ctok0 = S_SPLIT + bit * T_HIGH, T_HIGH, bit * T_CONE
        x_own = np.zeros((D, T_LOW), np.float32)
        x_own[:, 0:To] = xT[b][:, tok0:tok0 + To]
        xc = (np.ascontiguousarray(xT[b][:, ctok0:ctok0 + T_CONE]) if is_high
              else np.zeros((D, T_CONE), np.float32))
        cos_own = np.zeros((128, T_LOW), np.float32)
        sin_own = np.zeros((128, T_LOW), np.float32)
        cos_own[:, 0:To] = cos[:, tok0:tok0 + To]
        sin_own[:, 0:To] = sinA[:, tok0:tok0 + To]
        cfg = np.zeros((1, 16), np.int32)
        cfg[0, 0] = is_high
        cfg[0, 1] = tok0 // 128
        cfg[0, 2] = ctok0 // 128
        cfg[0, 3] = bit * 8
        cfg[0, 4] = bit * 44
        cfg[0, 5] = bit * 8
        cfg[0, 6] = bit
        cfg[0, 7] = 1 - bit
        gsl = slice(bit * NI, (bit + 1) * NI)
        dsl = slice(bit * (D // 2), (bit + 1) * (D // 2))
        in_maps.append({
            "x_in": np.ascontiguousarray(x_own),
            "xc_in": xc,
            "cfg_in": cfg,
            "one_in": np.ones((1, 64), np.int32),
            "b2g_in": np.ascontiguousarray(gb[:, gsl]),
            "sg_in": np.ascontiguousarray(sg_p[:, gsl]),
            "b2d_in": np.ascontiguousarray(db[:, dsl]),
            "sd_in": np.ascontiguousarray(sd_p[:, dsl]),
            "wqk_in": wqk_cols, "wv_in": wv_cols,
            "bqk_in": bqk_t, "bv_in": bv_row,
            "cos_o": cos_own.astype(BF), "sin_o": sin_own.astype(BF),
            "cos_c": np.ascontiguousarray(cos[:, ctok0:ctok0 + T_CONE]).astype(BF),
            "sin_c": np.ascontiguousarray(sinA[:, ctok0:ctok0 + T_CONE]).astype(BF),
            "mask_in": maskT,
        })

    nc = build_fused()
    res = run_bass_kernel_spmd(nc, in_maps, core_ids=cores).results

    outT = np.empty((NB, D, SEQ), np.float32)
    for c in cores:
        b = c // 4
        is_high = (c % 4) // 2
        bit = c % 2
        if not is_high:
            tok0, To = bit * T_LOW, T_LOW
        else:
            tok0, To = S_SPLIT + bit * T_HIGH, T_HIGH
        outT[b][:, tok0:tok0 + To] = res[c]["x_out"][:, 0:To]
    un = np.empty((NB, D, SEQ), np.float32)
    un[:, pr_d, :] = outT
    return np.ascontiguousarray(un.transpose(0, 2, 1))
